# version 16
# speedup vs baseline: 1.0468x; 1.0186x over previous
import math
from contextlib import ExitStack
import numpy as np

import concourse.bass as bass
import concourse.mybir as mybir
from concourse.alu_op_type import AluOpType as OP
from concourse.bass_utils import run_bass_kernel_spmd

F32 = mybir.dt.float32
BF16 = mybir.dt.bfloat16
AF = mybir.ActivationFunctionType
AX = mybir.AxisListType

MR = 1.5 * 2.0**23
LN_EPS = 1e-5
INNER_LR = 0.01
QMAX = 127.0

ENGS = ("t", "v", "s", "g", "y")


class DmaTok:
    __slots__ = ("waits",)
    def __init__(self, waits):
        self.waits = waits


class StepTok:
    __slots__ = ("eng", "cnt")
    def __init__(self, eng, cnt):
        self.eng, self.cnt = eng, cnt


class Step:
    __slots__ = ("eng", "deps", "fn", "tag")
    def __init__(self, eng, deps, fn, tag):
        self.eng, self.deps, self.fn, self.tag = eng, deps, fn, tag


class Prog:

    def __init__(self, nc):
        self.nc = nc
        self.eng = {"t": nc.tensor, "v": nc.vector, "s": nc.scalar,
                    "g": nc.gpsimd, "y": nc.sync}
        self.ssem = {e: nc.alloc_semaphore(f"step_{e}") for e in ENGS}
        self.scnt = {e: 0 for e in ENGS}
        self.steps = []
        self.dsems = {}
        self.dlanes = {}

    N_LANES = 4

    def dma_lanes(self, name):
        if name not in self.dlanes:
            self.dlanes[name] = {
                "keys": [], "rr": 0,
            }
            for i in range(self.N_LANES):
                key = f"{name}{i}"
                self.dsems[key] = [self.nc.alloc_semaphore(f"dma_{key}"), 0]
                self.dlanes[name]["keys"].append(key)
        return self.dlanes[name]

    def step(self, eng, deps, fn, tag=""):
        for d in deps:
            assert d is not None, f"None dep in {tag}"
            if isinstance(d, StepTok):
                assert d.cnt <= self.scnt[d.eng], f"forward dep in {tag}"
        self.scnt[eng] += 1
        tok = StepTok(eng, self.scnt[eng])
        self.steps.append(Step(eng, list(deps), fn, tag))
        return tok

    def dma_step(self, qeng, transfers, deps, sem_name, tag=""):
        sem_name = f"{sem_name}_{qeng}"
        lanes = self.dma_lanes(sem_name)
        plan = []
        waits = {}
        for _ in transfers:
            key = lanes["keys"][lanes["rr"] % self.N_LANES]
            lanes["rr"] += 1
            rec = self.dsems[key]
            plan.append((rec[0], rec[1]))
            rec[1] += 16
            waits[key] = rec[1]
        tlist = list(transfers)
        def fn(e):
            for (o, i_), (sem, prev) in zip(tlist, plan):
                if prev > 0:
                    e.wait_ge(sem, prev)
                e.dma_start(out=o, in_=i_).then_inc(sem, 16)
        tok = self.step(qeng, deps, fn, tag)
        return tok, DmaTok(tuple(waits.items()))

    def tok(self, eng):
        return StepTok(eng, self.scnt[eng])

    def barrier(self, tag="bar"):
        toks = {e: self.tok(e) for e in ENGS}
        dwaits = tuple((k, rec[1]) for k, rec in self.dsems.items() if rec[1] > 0)
        dtok = DmaTok(dwaits)
        out = {}
        for e in ENGS:
            deps = [toks[o] for o in ENGS if o != e] + [dtok]
            out[e] = self.step(e, deps, lambda _e: None, tag=tag)
        return out

    def finalize(self, final_waits):
        nc = self.nc
        per_eng = {e: [] for e in ENGS}
        for st in self.steps:
            per_eng[st.eng].append(st)
        hwm = {e: {} for e in ENGS}

        def emit_for(ename):
            e = self.eng[ename]
            marks = hwm[ename]
            own = 0
            for st in per_eng[ename]:
                if own > 0:
                    e.wait_ge(self.ssem[ename], own)
                for d in st.deps:
                    if isinstance(d, StepTok):
                        if d.eng == ename:
                            continue
                        pairs = [(self.ssem[d.eng], ("s", d.eng), d.cnt)]
                    else:
                        pairs = [(self.dsems[k][0], ("d", k), v)
                                 for (k, v) in d.waits]
                    for sem, key, val in pairs:
                        if marks.get(key, -1) < val:
                            e.wait_ge(sem, val)
                            marks[key] = val
                st.fn(e)
                own += 1
                e.drain()
                e.sem_inc(self.ssem[ename], 1)
            for (feng, dtok) in final_waits:
                if feng == ename:
                    for (k, v) in dtok.waits:
                        e.wait_ge(self.dsems[k][0], v)

        with nc.Block() as block:
            block.tensor(lambda e: emit_for("t"))
            block.vector(lambda e: emit_for("v"))
            block.scalar(lambda e: emit_for("s"))
            block.gpsimd(lambda e: emit_for("g"))
            block.sync(lambda e: emit_for("y"))



def build_program(cfg):
    T = cfg["T"]
    B = cfg["B"]
    NPOS = T // 128
    NT = B * T
    C, H, D, L = 1024, 16, 64, 32
    ISW = 2816
    n_cores = cfg["n_cores"]
    RG = [list(range(n_cores))]
    dbg = cfg.get("debug", False)
    phases = cfg.get("phases", "ABCDEF")

    nc = bass.Bass("TRN2", target_bir_lowering=False,
                   dynamic_dma_scratch_size=16384)

    def inp(name, shape, dt=F32):
        return nc.dram_tensor(name, list(shape), dt, kind="ExternalInput").ap()

    x_in = inp("x", (NT, C))
    wqT = inp("wqT", (C, C)); wkT = inp("wkT", (C, C)); wvT = inp("wvT", (C, C))
    woT = inp("woT", (C, C))
    ln1w = inp("ln1w", (1, C)); ln1b = inp("ln1b", (1, C))
    ln2w = inp("ln2w", (1, C)); ln2b = inp("ln2b", (1, C))
    lr_in = inp("lr", (1, H))
    w1T = inp("w1T", (C, 4 * C)); b1_in = inp("b1", (1, 4 * C))
    w2T = inp("w2T", (4 * C, C)); b2_in = inp("b2", (1, C))
    fWin = inp("fWin", (C, D)); fbin = inp("fbin", (1, D))
    fW1 = inp("fW1", (1, 1024)); fb1 = inp("fb1", (1, 1024))
    fW2 = inp("fW2", (1, 1024)); fb2 = inp("fb2", (1, D))
    fWout = inp("fWout", (D, C)); fbout = inp("fbout", (1, C))
    caT = inp("caT", (C, 3 * C)); cab = inp("cab", (1, 3 * C))
    q2lT = inp("q2lT", (D, L)); q2lb = inp("q2lb", (1, L))
    kv2lT = inp("kv2lT", (D, L)); kv2lb = inp("kv2lb", (1, L))
    l2oT = inp("l2oT", (L, D)); l2ob = inp("l2ob", (1, D))
    wqkT = inp("wqkT", (L, L)); wqkb = inp("wqkb", (1, L))
    cpT = inp("cpT", (C, C)); cpb = inp("cpb", (1, C))
    clipx = inp("clipx", (1, 1)); clipd = inp("clipd", (1, 1))
    wg_in = inp("wg", (ISW, C)); wu_in = inp("wu", (ISW, C))
    wd_in = inp("wd", (C, ISW))

    y_out = nc.dram_tensor("y", [NT, C], F32, kind="ExternalOutput").ap()

    x1_s = nc.dram_tensor("x1_s", [NT, C], F32).ap()
    u_s = nc.dram_tensor("u_s", [NT, C], F32).ap()
    pa_s = nc.dram_tensor("pa_s", [NT, C], F32).ap()
    pt_s = nc.dram_tensor("pt_s", [NT, C], F32).ap()
    xnT_s = nc.dram_tensor("xnT_s", [C, NT], F32).ap()
    h_s = nc.dram_tensor("h_s", [ISW, NT], F32).ap()
    wgq_s = nc.dram_tensor("wgq_s", [C, ISW], BF16).ap()
    wuq_s = nc.dram_tensor("wuq_s", [C, ISW], BF16).ap()
    wdq_s = nc.dram_tensor("wdq_s", [ISW, C], BF16).ap()
    ccxi = nc.dram_tensor("ccxi", [1, 1], F32)
    ccxo = nc.dram_tensor("ccxo", [1, 1], F32)
    cchi = nc.dram_tensor("cchi", [1, 1], F32)
    ccho = nc.dram_tensor("ccho", [1, 1], F32)

    dbg_outs = {}
    def dbg_on(name):
        if dbg is True:
            return True
        return isinstance(dbg, str) and name.split("_")[0] in dbg.split(",")
    def dbg_out(name, shape):
        if dbg_on(name):
            dbg_outs[name] = nc.dram_tensor(name, list(shape), F32,
                                            kind="ExternalOutput").ap()
            return dbg_outs[name]
        return None

    p = Prog(nc)
    A = nc.alloc_sbuf_tensor

    def mkarena():
        st = ExitStack()
        def alloc(name, shape, dt):
            return st.enter_context(nc.sbuf_tensor(name, shape, dt, side="right"))
        return st, alloc

    cpool = {}
    def creg(val):
        val = float(val)
        if val not in cpool:
            t = A(f"cst{len(cpool)}", [128, 1], F32)
            cpool[val] = t
            nc.const_aps.register(F32, val, t[:, :])
        return cpool[val]
    creg(LN_EPS)
    creg(1.0 / (2.0 * math.pi))
    creg(math.pi)
    creg(-math.pi)
    creg(math.pi / 2.0)
    creg(float(L) ** -0.5)
    creg(1.0 / QMAX)
    p0_arena = ExitStack()
    def A0(name, shape, dt):
        return p0_arena.enter_context(nc.sbuf_tensor(name, shape, dt, side="right"))

    ident = A("ident", [128, 128], BF16)
    identf = A("identf", [128, 128], F32)
    ones_row = A("ones_row", [1, 128], BF16)
    onesf_row = A("onesf_row", [1, 128], F32)
    one_f = A("one_f", [1, 1], F32)
    maskLTn = A("maskLTn", [128, 128], BF16)
    maskUTn = A("maskUTn", [128, 128], BF16)
    maskUT = A("maskUT", [128, 128], BF16)
    maskCa = A("maskCa", [128, 128], BF16)
    eta_bc = A("eta_bc", [128, H], F32)

    arenaA, AA = mkarena()
    g1bc = AA("g1bc", [128, C], F32); be1bc = AA("be1bc", [128, C], F32)
    fW1bc = AA("fW1bc", [128, 1024], F32); fb1bc = AA("fb1bc", [128, 1024], F32)
    fW2bc = AA("fW2bc", [128, 1024], F32)
    ffb_bf = AA("ffb_bf", [1, C], BF16)
    fbin_bc = AA("fbin_bc", [128, D], F32)
    ffb_bc = AA("ffb_bc", [128, C], F32)

    wq_sb = AA("wq_sb", [128, 8 * C], BF16)
    wk_sb = AA("wk_sb", [128, 8 * C], BF16)
    wv_sb = AA("wv_sb", [128, 8 * C], BF16)
    wo_sb = AA("wo_sb", [64, 16 * C], BF16)
    Win_bf = AA("Win_bf", [128, 8 * D], BF16)
    Wout_bf = AA("Wout_bf", [64, C], BF16)
    xa = AA("xa", [128, 2 * C], F32)
    xn_f = AA("xn_f", [128, 2 * C], F32)
    xn_bf = AA("xn_bf", [128, 2 * C], BF16)
    h1_bf = AA("h1_bf", [128, 2 * C], BF16)
    xnT_bf = AA("xnT_bf", [128, C], BF16)
    h1T_bf = AA("h1T_bf", [128, C], BF16)
    QT_sb = AA("QT_sb", [128, C], BF16)
    KT_sb = AA("KT_sb", [128, C], BF16)
    V_sb = AA("V_sb", [128, C], F32)
    Km_sb = AA("Km_sb", [128, C], BF16)
    WTst = AA("WTst", [128, B * 512], F32)
    WTbf = AA("WTbf", [128, B * 1024], BF16)
    Bc0 = AA("Bc0", [128, 2048], BF16)
    Nc0 = AA("Nc0", [128, 2048], BF16)
    Bc1 = AA("Bc1", [128, 2048], BF16)
    Nc1 = AA("Nc1", [128, 2048], BF16)
    Hbar = AA("Hbar", [128, 2048], BF16)
    c_f = AA("c_f", [128, 1024], F32)
    c_bf = AA("c_bf", [128, 1024], BF16)
    ct_bf = c_bf
    OutT_bf = AA("OutT_bf", [64, 2048], BF16)
    u_f = AA("u_f", [128, C], F32)
    pa_f = AA("pa_f", [128, C], F32)
    s0_f = AA("s0_f", [128, D], F32)
    hf0 = AA("hf0", [128, 1024], F32)
    h1w = hf0
    g0_bf = AA("g0_bf", [128, D], BF16)
    gT_bf = AA("gT_bf", [64, 128], BF16)
    bns = AA("bns", [128, 12], F32)
    ag2 = AA("ag2", [128, 2], F32)
    rstd0 = AA("rstd0", [128, 1], F32)
    nmr0 = AA("nmr0", [128, 1], F32)
    rstd1 = AA("rstd1", [128, 1], F32)
    nmr1 = AA("nmr1", [128, 1], F32)

    clipx_r = A("clipx_r", [1, 1], F32)
    clipd_r = A("clipd_r", [1, 1], F32)
    fbin_bf = A("fbin_bf", [1, D], BF16)
    lr_row = A0("lr_row", [1, H], F32)
    lr_bf = A0("lr_bf", [1, H], BF16)
    fbout_r = A0("fbout_r", [1, C], F32)
    fb2_bf = A0("fb2_bf", [1, D], BF16)


    pf_a = nc.alloc_psum_tensor("pf_a", [128, 2048], F32)
    pf_b = nc.alloc_psum_tensor("pf_b", [128, 1536], F32)
    pbf = nc.alloc_psum_tensor("pbf", [128, 1024], BF16)

    def p0_memset(g):
        for val, t in cpool.items():
            g.memset(t[:], val)
        g.memset(ident[:], 0.0)
        g.memset(identf[:], 0.0)
        g.memset(onesf_row[:], 1.0)
        g.memset(ones_row[:], 1.0)
        g.memset(one_f[:], 1.0)
        g.memset(maskLTn[:], 0.0)
        g.memset(maskUTn[:], 0.0)
        g.memset(maskUT[:], 0.0)
        g.memset(maskCa[:], 0.0)
    tk = p.step("g", [], p0_memset, "p0_memset")

    def p0_masks(g):
        sel = g.affine_select
        sel(out=ident[:], in_=ident[:], compare_op=OP.not_equal, fill=1.0,
            base=0, pattern=[[-1, 128]], channel_multiplier=1)
        sel(out=identf[:], in_=identf[:], compare_op=OP.not_equal, fill=1.0,
            base=0, pattern=[[-1, 128]], channel_multiplier=1)
        sel(out=maskLTn[:], in_=maskLTn[:], compare_op=OP.is_ge, fill=-1.0,
            base=0, pattern=[[1, 128]], channel_multiplier=-1)
        sel(out=maskUTn[:], in_=maskUTn[:], compare_op=OP.is_ge, fill=-1.0,
            base=0, pattern=[[-1, 128]], channel_multiplier=1)
        sel(out=maskUT[:], in_=maskUT[:], compare_op=OP.is_ge, fill=1.0,
            base=0, pattern=[[-1, 128]], channel_multiplier=1)
        sel(out=maskCa[:], in_=maskCa[:], compare_op=OP.is_gt, fill=1.0,
            base=0, pattern=[[-1, 128]], channel_multiplier=1)
    t_mask = p.step("g", [tk], p0_masks, "p0_masks")

    f0row_bf = A0("f0row_bf", [1, 1024], BF16)
    t_rows, d_rows = p.dma_step(
        "y", [(lr_row[:, :], lr_in[:, :]), (fbout_r[:, :], fbout[:, :]),
              (clipx_r[:, :], clipx[:, :]), (clipd_r[:, :], clipd[:, :])],
        [], "cst", "p0_rows")
    t_rows2, d_rows2 = p.dma_step(
        "g", [(fbin_bf[:, :], fbin[:, :]), (fb2_bf[:, :], fb2[:, :])],
        [t_mask], "cst2", "p0_rows2")

    def p0_eta_scale(v):
        v.tensor_scalar_mul(lr_bf[:, :], lr_row[:, :], INNER_LR)
    t_etas = p.step("v", [d_rows], p0_eta_scale, "eta_scale")

    bc_list = [(g1bc, f0row_bf, 0, ln1w), (be1bc, f0row_bf, 0, ln1b),
               (fW1bc, f0row_bf, 0, fW1), (fb1bc, f0row_bf, 0, fb1),
               (fW2bc, f0row_bf, 0, fW2)]
    t_bc_prev = None
    t_bm = None
    for bi, (dst, srcrow, off, hsrc) in enumerate(bc_list):
        deps = [d_rows2, t_mask, t_etas] if bi == 0 else [t_bc_prev]
        if hsrc is not None:
            ld_deps = [t_mask] + ([t_bm] if t_bm else [])
            _, d_f0 = p.dma_step("g", [(f0row_bf[:, :], hsrc[:, :])],
                                 ld_deps, "cst2", f"p0_f0row{bi}")
            deps = deps + [d_f0]
        def mk_mm(srcrow=srcrow, off=off):
            def fn(t):
                for hh in range(2):
                    t.matmul(pf_a[:, hh * 512:(hh + 1) * 512], ones_row[:, :],
                             srcrow[:, off + hh * 512:off + (hh + 1) * 512],
                             start=True, stop=True)
            return fn
        t_bm = p.step("t", deps, mk_mm(), f"bc_mm{bi}")
        def mk_ev(dst=dst):
            def fn(v):
                v.tensor_copy(dst[:, :], pf_a[:, 0:1024])
            return fn
        t_bc_prev = p.step("v", [t_bm], mk_ev(), f"bc_ev{bi}")

    def p0_eta_mm(t):
        t.matmul(pf_b[:, 0:H], ones_row[:, :], lr_bf[:, :],
                 start=True, stop=True)
    t_etam = p.step("t", [t_etas, t_mask], p0_eta_mm, "eta_mm")
    def p0_eta_ev(v):
        v.tensor_copy(eta_bc[:, :], pf_b[:, 0:H])
    t_eta = p.step("v", [t_etam], p0_eta_ev, "eta_ev")


    wload_transfers = []
    for ic in range(8):
        wload_transfers.append((wq_sb[:, ic * C:(ic + 1) * C], wqT[ic * 128:(ic + 1) * 128, :]))
        wload_transfers.append((wk_sb[:, ic * C:(ic + 1) * C], wkT[ic * 128:(ic + 1) * 128, :]))
        wload_transfers.append((wv_sb[:, ic * C:(ic + 1) * C], wvT[ic * 128:(ic + 1) * 128, :]))
        wload_transfers.append((Win_bf[:, ic * D:(ic + 1) * D], fWin[ic * 128:(ic + 1) * 128, :]))
    for cc in range(16):
        wload_transfers.append((wo_sb[:, cc * C:(cc + 1) * C], woT[cc * 64:(cc + 1) * 64, :]))
    wload_transfers.append((Wout_bf[:, :], fWout[:, :]))
    t_wl, d_wl = p.dma_step("g", wload_transfers, [t_mask], "wld", "L0_wload")

    fb2col_bf = A0("fb2col_bf", [64, 1], BF16)
    def p0_fb2col_mm(t):
        t.matmul(pf_b[0:64, 1056:1057], fb2_bf[:, :], ident[0:1, 0:1],
                 start=True, stop=True)
    t_f2m = p.step("t", [d_rows2, t_etam], p0_fb2col_mm, "fb2col_mm")
    def p0_fb2col_ev(v):
        v.tensor_copy(fb2col_bf[:, :], pf_b[0:64, 1056:1057])
    t_f2c = p.step("v", [t_f2m], p0_fb2col_ev, "fb2col_ev")
    def p0_ffb_mm(t):
        for hh in range(2):
            t.matmul(pf_b[0:1, hh * 512:(hh + 1) * 512], fb2col_bf[:, :],
                     Wout_bf[:, hh * 512:(hh + 1) * 512], start=True, stop=True)
    t_ffm = p.step("t", [t_f2c, d_wl, t_eta], p0_ffb_mm, "ffb_mm")
    def p0_ffb_ev(v):
        v.tensor_add(fbout_r[:, :], pf_b[0:1, 0:1024], fbout_r[:, :])
        v.drain()
        v.tensor_copy(ffb_bf[:, :], fbout_r[:, :])
    t_ffb = p.step("v", [t_ffm, d_rows], p0_ffb_ev, "ffb_ev")

    def p0_bc2_mm(t):
        t.matmul(pf_b[:, 64:128], ones_row[:, :], fbin_bf[:, :], start=True, stop=True)
        for hh in range(2):
            t.matmul(pf_a[:, hh * 512:(hh + 1) * 512], ones_row[:, :],
                     ffb_bf[:, hh * 512:(hh + 1) * 512], start=True, stop=True)
    t_bc2 = p.step("t", [t_ffb, t_bc_prev, d_rows2], p0_bc2_mm, "p0_bc2")
    def p0_bc2_ev(v):
        v.tensor_copy(fbin_bc[:, :], pf_b[:, 64:64 + D])
        v.tensor_copy(ffb_bc[:, :], pf_a[:, 0:1024])
    t_bc2e = p.step("v", [t_bc2], p0_bc2_ev, "p0_bc2e")

    p0_done = p.barrier("p0_done")
    p0_arena.close()


    u_dbg = dbg_out("u_dbg", (NT, C))
    pa_dbg = dbg_out("pa_dbg", (NT, C))

    def a_state0(g):
        g.memset(WTst[:], 0.0)
        g.memset(WTbf[:], 0.0)
    t_st0 = p.step("g", [p0_done["g"]], a_state0, "state0")
    t_stbf = t_st0
    t_stf = t_st0

    ln_chain = []

    def ln_steps(x_ap, deps, rstd, nmr, tag):
        def st(v):
            v.bn_stats(bns[:, 0:6], x_ap[:, 0:512])
            v.bn_stats(bns[:, 6:12], x_ap[:, 512:1024])
            v.drain()
            v.bn_aggr(ag2[:, :], bns[:, :])
        t1 = p.step("v", deps, st, f"{tag}_stats")
        def rs(s):
            s.activation(rstd[:, :], ag2[:, 1:2], AF.Ln, bias=LN_EPS, scale=1.0)
            s.drain()
            s.activation(rstd[:, :], rstd[:, :], AF.Exp, bias=0.0, scale=-0.5)
        t2 = p.step("s", [t1], rs, f"{tag}_rstd")
        def nm(v):
            v.tensor_scalar(nmr[:, :], ag2[:, 0:1], rstd[:, 0:1], -1.0,
                            OP.mult, OP.mult)
        t3 = p.step("v", [t2], nm, f"{tag}_nmr")
        return t3

    dx_toks = {}
    t_px, dx_toks[(0, 0)] = p.dma_step(
        "y", [(xa[:, 0:C], x_in[0:128, :])], [p0_done["y"]], "xin", "xin00")
    last_xin = t_px

    t_prev_ttt_tail = None
    t_prev_pbf = None
    t_u_dma = None
    t_pa_dma = None
    d_u_prev = []
    d_pa_prev = []
    wt_war = {b_: t_st0 for b_ in range(B)}
    pa_toks = {}
    ln_toks = {}

    def emit_front(gpos):
        par = gpos % 2
        b_ = gpos % B
        pos_ = gpos // B
        xcur = xa[:, par * C:(par + 1) * C]
        xnv = xn_f[:, par * C:(par + 1) * C]
        xnb = xn_bf[:, par * C:(par + 1) * C]
        h1b = h1_bf[:, par * C:(par + 1) * C]
        dx = dx_toks[(b_, pos_)]
        t_ln0 = ln_steps(xcur, [dx], rstd0, nmr0, f"ln0_{gpos}")
        def ap0(v, xcur=xcur, xnv=xnv, xnb=xnb):
            v.tensor_scalar(xnv, xcur, rstd0[:, 0:1], nmr0[:, 0:1],
                            OP.mult, OP.add)
            v.tensor_scalar(xnb, xcur, rstd0[:, 0:1], nmr0[:, 0:1],
                            OP.mult, OP.add)
        t_xn = p.step("v", [t_ln0], ap0, f"xn_{gpos}")
        t_ln1 = ln_steps(xnv, [t_xn], rstd1, nmr1, f"ln1_{gpos}")
        def ap1(v, xnv=xnv):
            v.tensor_scalar(h1w[:, :], xnv, rstd1[:, 0:1], nmr1[:, 0:1],
                            OP.mult, OP.add)
        t_h1r = p.step("v", [t_ln1], ap1, f"h1raw_{gpos}")
        def gb1(v, h1b=h1b):
            v.tensor_mul(h1w[:, :], h1w[:, :], g1bc[:, :])
            v.drain()
            v.tensor_add(h1w[:, :], h1w[:, :], be1bc[:, :])
            v.drain()
            v.tensor_copy(h1b, h1w[:, :])
        t_h1 = p.step("v", [t_h1r], gb1, f"h1_{gpos}")
        ln_toks[gpos] = (t_xn, t_h1)

    if "A" in phases:
      for pos in range(NPOS):
        for b in range(B):
            gpos = pos * B + b
            buf = gpos % 2
            row0 = b * T + pos * 128
            nb, npos_ = (b + 1, pos) if b + 1 < B else (0, pos + 1)
            if npos_ < NPOS:
                nrow = nb * T + npos_ * 128
                nbuf = (gpos + 1) % 2
                deps = [last_xin]
                if nbuf in pa_toks:
                    deps.append(pa_toks[nbuf])
                last_xin, dx_toks[(nb, npos_)] = p.dma_step(
                    "y", [(xa[:, nbuf * C:(nbuf + 1) * C], x_in[nrow:nrow + 128, :])],
                    deps, "xin", f"xin{nb}_{npos_}")

            if gpos == 0:
                emit_front(0)
            xcur = xa[:, buf * C:(buf + 1) * C]
            xn_v = xn_f[:, buf * C:(buf + 1) * C]
            t_xn, t_h1 = ln_toks.pop(gpos)

            def tx(t, src=xn_bf[:, buf * C:(buf + 1) * C]):
                for ic in range(8):
                    t.transpose(pbf[:, ic * 128:(ic + 1) * 128],
                                src[:, ic * 128:(ic + 1) * 128], ident[:, :])
            deps_t = [t_xn] + ([t_prev_pbf] if t_prev_pbf else [])
            t_txn = p.step("t", deps_t, tx, f"txn_{gpos}")
            def evx(s):
                s.copy(xnT_bf[:, :], pbf[:, :])
            t_xnT = p.step("s", [t_txn], evx, f"xnTev_{gpos}")
            def th1(t, src=h1_bf[:, buf * C:(buf + 1) * C]):
                for ic in range(8):
                    t.transpose(pbf[:, ic * 128:(ic + 1) * 128],
                                src[:, ic * 128:(ic + 1) * 128], ident[:, :])
            t_th1 = p.step("t", [t_h1, t_xnT], th1, f"th1_{gpos}")
            def evh(s):
                s.copy(h1T_bf[:, :], pbf[:, :])
            t_h1T = p.step("s", [t_th1], evh, f"h1Tev_{gpos}")
            t_prev_pbf = t_h1T

            war = [t_prev_ttt_tail] if t_prev_ttt_tail else []
            def qk_mm(t):
                for dc in range(8):
                    for ic in range(8):
                        t.matmul(pf_a[:, dc * 128:(dc + 1) * 128],
                                 wq_sb[:, ic * C + dc * 128:ic * C + (dc + 1) * 128],
                                 h1T_bf[:, ic * 128:(ic + 1) * 128],
                                 start=(ic == 0), stop=(ic == 7))
                for dc in range(8):
                    for ic in range(8):
                        t.matmul(pf_a[:, 1024 + dc * 128:1024 + (dc + 1) * 128],
                                 wk_sb[:, ic * C + dc * 128:ic * C + (dc + 1) * 128],
                                 h1T_bf[:, ic * 128:(ic + 1) * 128],
                                 start=(ic == 0), stop=(ic == 7))
            t_qk = p.step("t", [t_h1T, d_wl] + war, qk_mm, f"qk_{gpos}")
            if gpos + 1 < NPOS * B:
                emit_front(gpos + 1)
            def ev_q(s):
                s.copy(QT_sb[:, :], pf_a[:, 0:1024])
            t_q = p.step("s", [t_qk], ev_q, f"qev_{gpos}")
            def ev_k(v):
                v.tensor_copy(KT_sb[:, :], pf_a[:, 1024:2048])
            t_k = p.step("v", [t_qk], ev_k, f"kev_{gpos}")

            def v_mm(t):
                for hh in range(2):
                    for ic in range(8):
                        t.matmul(pf_b[:, hh * 512:(hh + 1) * 512],
                                 h1T_bf[:, ic * 128:(ic + 1) * 128],
                                 wv_sb[:, ic * C + hh * 512:ic * C + (hh + 1) * 512],
                                 start=(ic == 0), stop=(ic == 7))
            t_vm = p.step("t", [t_h1T, t_q, t_k], v_mm, f"vmm_{gpos}")
            def ev_v(s):
                s.copy(V_sb[:, :], pf_b[:, 0:1024])
            t_v = p.step("s", [t_vm], ev_v, f"vev_{gpos}")
            def km_mm(t):
                for hh in range(2):
                    for ic in range(8):
                        t.matmul(pf_b[:, hh * 512:(hh + 1) * 512],
                                 h1T_bf[:, ic * 128:(ic + 1) * 128],
                                 wk_sb[:, ic * C + hh * 512:ic * C + (hh + 1) * 512],
                                 start=(ic == 0), stop=(ic == 7))
            t_kmm = p.step("t", [t_v], km_mm, f"kmmm_{gpos}")
            def ev_km(v):
                v.tensor_copy(Km_sb[:, :], pf_b[:, 0:1024])
            t_km = p.step("v", [t_kmm], ev_km, f"kmev_{gpos}")

            def KT_h(h):
                return KT_sb[64 * (h % 2):64 * (h % 2) + 64, (h // 2) * 128:(h // 2) * 128 + 128]
            def QT_h(h):
                return QT_sb[64 * (h % 2):64 * (h % 2) + 64, (h // 2) * 128:(h // 2) * 128 + 128]
            def WTbf_h(h, b=b):
                return WTbf[64 * (h % 2):64 * (h % 2) + 64,
                            b * 1024 + 64 * h:b * 1024 + 64 * h + 64]
            def WTbf_hp(h, b=b):
                return WTbf[:, b * 1024 + 64 * h:b * 1024 + 64 * h + 64]

            def g_mm(t, KT_h=KT_h):
                for i, h in enumerate([0, 2, 4, 6, 8, 10, 12, 14, 1, 3, 5, 7, 9, 11, 13, 15]):
                    if i == 8:
                        t.drain()
                    t.matmul(pf_a[:, h * 128:(h + 1) * 128], KT_h(h), KT_h(h),
                             start=True, stop=True)
            t_g = p.step("t", [t_k, t_kmm], g_mm, f"G_{gpos}")

            mLT = maskLTn[:, :].unsqueeze(1).to_broadcast([128, H, 128])
            mUT = maskUTn[:, :].unsqueeze(1).to_broadcast([128, H, 128])
            mUTk = maskUT[:, :].unsqueeze(1).to_broadcast([128, H, 128])
            eta128 = eta_bc[:, :].unsqueeze(2).to_broadcast([128, H, 128])
            eta64 = eta_bc[:, :].unsqueeze(2).to_broadcast([128, H, 64])
            pav = pf_a[:, :].rearrange("p (h w) -> p h w", w=128)
            def b0_opsa(v):
                v.tensor_tensor(Bc0[:, :].rearrange("p (h w) -> p h w", w=128),
                                pav, mLT, OP.mult)
                v.tensor_tensor(Nc0[:, :].rearrange("p (h w) -> p h w", w=128),
                                pav, mUT, OP.mult)
            t_b0a = p.step("v", [t_g, t_eta, t_km], b0_opsa, f"B0a_{gpos}")
            def b0_opsb(v):
                v.tensor_tensor(Bc0[:, :].rearrange("p (h w) -> p h w", w=128),
                                Bc0[:, :].rearrange("p (h w) -> p h w", w=128),
                                eta128, OP.mult)
                v.tensor_tensor(Nc0[:, :].rearrange("p (h w) -> p h w", w=128),
                                Nc0[:, :].rearrange("p (h w) -> p h w", w=128),
                                eta128, OP.mult)
            t_b0 = p.step("v", [t_b0a], b0_opsb, f"B0b_{gpos}")

            def h_mm(t, KT_h=KT_h, QT_h=QT_h):
                for i, h in enumerate([0, 2, 4, 6, 8, 10, 12, 14, 1, 3, 5, 7, 9, 11, 13, 15]):
                    if i == 8:
                        t.drain()
                    t.matmul(pf_a[:, h * 128:(h + 1) * 128], KT_h(h), QT_h(h),
                             start=True, stop=True)
            t_hm = p.step("t", [t_b0a, t_q], h_mm, f"H_{gpos}")
            def hbar_ev(v):
                v.tensor_tensor(Hbar[:, :].rearrange("p (h w) -> p h w", w=128),
                                pav, mUTk, OP.mult)
            t_hb = p.step("v", [t_hm], hbar_ev, f"Hbar_{gpos}")

            def r_mm(t, WTbf_h=WTbf_h, KT_h=KT_h):
                for i, h in enumerate([0, 2, 4, 6, 8, 10, 12, 14, 1, 3, 5, 7, 9, 11, 13, 15]):
                    if i == 8:
                        t.drain()
                    t.matmul(pf_b[:, h * 64:(h + 1) * 64], KT_h(h), WTbf_h(h),
                             start=True, stop=True)
            t_r = p.step("t", [t_km, t_stbf, t_hm], r_mm, f"R_{gpos}")
            def c_init(v):
                v.tensor_tensor(c_f[:, :], pf_b[:, 0:1024], V_sb[:, :], OP.subtract)
            t_cf = p.step("v", [t_r, t_v], c_init, f"cinit_{gpos}")
            def c_cast(s):
                s.copy(c_bf[:, :], c_f[:, :])
            t_cb = p.step("s", [t_cf], c_cast, f"ccast_{gpos}")

            Bcur, Ncur, Bnx, Nnx = Bc0, Nc0, Bc1, Nc1
            t_bn_ready = [t_b0]
            t_sq_prev = None
            for k in range(7):
                def app_mm(t, Ncur=Ncur):
                    for h in range(H):
                        t.matmul(pf_b[:, h * 64:(h + 1) * 64],
                                 Ncur[:, h * 128:(h + 1) * 128],
                                 c_bf[:, h * 64:(h + 1) * 64],
                                 start=True, stop=True)
                t_app = p.step("t", [t_cb, t_cf] + t_bn_ready, app_mm,
                               f"app{k}_{gpos}")
                def c_upd(v):
                    v.tensor_add(c_f[:, :], c_f[:, :], pf_b[:, 0:1024])
                t_cf = p.step("v", [t_app], c_upd, f"cupd{k}_{gpos}")
                if k < 6:
                    def c_cast2(s):
                        s.copy(c_bf[:, :], c_f[:, :])
                    t_cb = p.step("s", [t_cf], c_cast2, f"ccast{k}_{gpos}")
                    do_b = (k < 5)
                    deps_sq = list(t_bn_ready) + [t_app, t_hb]
                    if t_sq_prev:
                        deps_sq.append(t_sq_prev)
                    if do_b:
                        def sq_b(t, Bcur=Bcur, Ncur=Ncur):
                            for h in range(H):
                                t.matmul(pf_a[:, h * 128:(h + 1) * 128],
                                         Ncur[:, h * 128:(h + 1) * 128],
                                         Bcur[:, h * 128:(h + 1) * 128],
                                         start=True, stop=True)
                        t_sqb = p.step("t", deps_sq, sq_b, f"sqB{k}_{gpos}")
                        def evb(s, Bnx=Bnx):
                            s.copy(Bnx[:, :], pf_a[:, :])
                        t_evb = p.step("s", [t_sqb], evb, f"evB{k}_{gpos}")
                    else:
                        t_evb = p.step("t", deps_sq, lambda t: None, f"sqBnop{k}_{gpos}")
                    def sq_n(t, Bcur=Bcur, Ncur=Ncur):
                        for h in range(H):
                            t.matmul(pf_a[:, h * 128:(h + 1) * 128],
                                     Bcur[:, h * 128:(h + 1) * 128],
                                     Ncur[:, h * 128:(h + 1) * 128],
                                     start=True, stop=True)
                    t_sqn = p.step("t", [t_evb], sq_n, f"sqN{k}_{gpos}")
                    def evn(s, Nnx=Nnx):
                        s.copy(Nnx[:, :], pf_a[:, :])
                    t_evn = p.step("s", [t_sqn], evn, f"evN{k}__{gpos}")
                    t_sq_prev = t_sqn
                    Bcur, Bnx = Bnx, Bcur
                    Ncur, Nnx = Nnx, Ncur
                    t_bn_ready = [t_evn] + ([t_evb] if do_b else [])

            def ct_ops(v):
                v.scalar_tensor_tensor(ct_bf[:, :].rearrange("p (h w) -> p h w", w=64),
                                       c_f[:, :].rearrange("p (h w) -> p h w", w=64),
                                       -1.0, eta64, OP.mult, OP.mult)
            t_ct = p.step("v", [t_cf], ct_ops, f"ct_{gpos}")

            def out_mm(t, WTbf_hp=WTbf_hp):
                for h in range(H):
                    t.matmul(pf_a[0:64, h * 128:(h + 1) * 128], WTbf_hp(h),
                             QT_sb[:, (h // 2) * 128:(h // 2) * 128 + 128],
                             start=True, stop=False)
                    t.matmul(pf_a[0:64, h * 128:(h + 1) * 128],
                             ct_bf[:, h * 64:(h + 1) * 64],
                             Hbar[:, h * 128:(h + 1) * 128],
                             start=False, stop=True)
            t_out = p.step("t", [t_ct, t_hb, t_stbf] + t_bn_ready, out_mm,
                           f"out_{gpos}")
            def out_ev(s):
                s.copy(OutT_bf[:, :], pf_a[0:64, :])
            t_oev = p.step("s", [t_out], out_ev, f"outev_{gpos}")

            def st_mm(t):
                for pr in range(8):
                    t.matmul(pf_b[:, pr * 128:(pr + 1) * 128],
                             Km_sb[:, pr * 128:(pr + 1) * 128],
                             ct_bf[:, pr * 128:(pr + 1) * 128],
                             start=True, stop=True)
            t_stm = p.step("t", [t_ct, t_km, t_cf, t_out], st_mm, f"stmm_{gpos}")
            def st_upd(v, b=b):
                top_in = pf_b[0:64, 0:1024].rearrange("p (r w) -> p r w", w=128)[:, :, 0:64]
                top_out = WTst[0:64, b * 512:(b + 1) * 512].rearrange("p (r w) -> p r w", w=64)
                v.tensor_tensor(top_out, top_out, top_in, OP.add)
                bot_in = pf_b[64:128, 0:1024].rearrange("p (r w) -> p r w", w=128)[:, :, 64:128]
                bot_out = WTst[64:128, b * 512:(b + 1) * 512].rearrange("p (r w) -> p r w", w=64)
                v.tensor_tensor(bot_out, bot_out, bot_in, OP.add)
            t_stu = p.step("v", [t_stm, wt_war[b], t_r], st_upd, f"stupd_{gpos}")
            def st_cast(s, b=b):
                wtop = WTbf[0:64, b * 1024:(b + 1) * 1024].rearrange(
                    "p (h w) -> p h w", w=64)[:, 0::2, :]
                s.copy(wtop, WTst[0:64, b * 512:(b + 1) * 512].rearrange(
                    "p (h w) -> p h w", w=64))
                s.drain()
                wbot = WTbf[64:128, b * 1024:(b + 1) * 1024].rearrange(
                    "p (h w) -> p h w", w=64)[:, 1::2, :]
                s.copy(wbot, WTst[64:128, b * 512:(b + 1) * 512].rearrange(
                    "p (h w) -> p h w", w=64))
            t_stbf = p.step("s", [t_stu], st_cast, f"stcast_{gpos}")
            wt_war[b] = t_stbf

            def y1_mm(t):
                for hh in range(2):
                    for cc in range(16):
                        t.matmul(pf_b[:, hh * 512:(hh + 1) * 512],
                                 OutT_bf[:, cc * 128:(cc + 1) * 128],
                                 wo_sb[:, cc * C + hh * 512:cc * C + (hh + 1) * 512],
                                 start=(cc == 0), stop=(cc == 15))
            t_y1 = p.step("t", [t_oev, t_stu, d_wl], y1_mm, f"y1_{gpos}")
            def u_ops(v, xn_v=xn_v):
                v.tensor_add(u_f[:, :], xn_v, pf_b[:, 0:1024])
            t_u = p.step("v", [t_y1, t_xn] + d_u_prev, u_ops, f"u_{gpos}")
            du_deps = [t_u] + ([t_u_dma] if t_u_dma else [])
            t_u_dma, d_u = p.dma_step("g", [(u_s[row0:row0 + 128, :], u_f[:, :])],
                                      du_deps, "out", f"udma_{gpos}")
            d_u_prev = [d_u]

            def s_mm(t):
                for ic in range(8):
                    t.matmul(pf_b[:, 1024 + 0:1024 + 64],
                             xnT_bf[:, ic * 128:(ic + 1) * 128],
                             Win_bf[:, ic * D:(ic + 1) * D],
                             start=(ic == 0), stop=(ic == 7))
            t_s0 = p.step("t", [t_xnT, d_wl, t_y1], s_mm, f"s0_{gpos}")
            def s_ev(v):
                v.tensor_add(s0_f[:, :], pf_b[:, 1024:1088], fbin_bc[:, :])
            t_s0e = p.step("v", [t_s0], s_ev, f"s0ev_{gpos}")
            def f0_el(v):
                s_exp = s0_f[:, :].unsqueeze(2).to_broadcast([128, D, 16])
                v.tensor_tensor(hf0[:, :].rearrange("p (m j) -> p m j", j=16),
                                s_exp, fW1bc[:, :].rearrange("p (m j) -> p m j", j=16),
                                OP.mult)
                v.drain()
                v.tensor_add(hf0[:, :], hf0[:, :], fb1bc[:, :])
                v.drain()
                v.tensor_scalar_max(hf0[:, :], hf0[:, :], 0.0)
                v.drain()
                v.tensor_mul(hf0[:, :], hf0[:, :], fW2bc[:, :])
                v.drain()
                v.tensor_reduce(s0_f[:, 0:D], hf0[:, :].rearrange("p (m j) -> p m j", j=16),
                                axis=AX.X, op=OP.add)
                v.drain()
                v.tensor_copy(g0_bf[:, :], s0_f[:, 0:D])
            t_g0 = p.step("v", [t_s0e, p0_done["v"]], f0_el, f"g0_{gpos}")
            def gt_mm(t):
                t.transpose(pbf[0:64, 0:128], g0_bf[:, :], ident[:, :])
            t_gt = p.step("t", [t_g0, t_s0, t_prev_pbf], gt_mm, f"gt_{gpos}")
            def gt_ev(v):
                v.tensor_copy(gT_bf[:, :], pbf[0:64, 0:128])
            t_gte = p.step("v", [t_gt], gt_ev, f"gtev_{gpos}")
            t_prev_pbf = t_gte
            def f0_mm(t):
                for hh in range(2):
                    t.matmul(pf_a[:, hh * 512:(hh + 1) * 512], gT_bf[:, :],
                             Wout_bf[:, hh * 512:(hh + 1) * 512],
                             start=True, stop=True)
            t_f0 = p.step("t", [t_gte, t_out, t_oev], f0_mm, f"f0_{gpos}")

            def pa_ops(v, xcur=xcur):
                v.tensor_add(pa_f[:, :], xcur, pf_a[:, 0:1024])
                v.drain()
                v.tensor_add(pa_f[:, :], pa_f[:, :], ffb_bc[:, :])
                v.drain()
                v.tensor_add(pa_f[:, :], pa_f[:, :], u_f[:, :])
            t_pa = p.step("v", [t_f0, t_u] + d_pa_prev, pa_ops, f"pa_{gpos}")
            pa_toks[buf] = t_pa
            dpa_deps = [t_pa] + ([t_pa_dma] if t_pa_dma else [])
            t_pa_dma, d_pa = p.dma_step("g", [(pa_s[row0:row0 + 128, :], pa_f[:, :])],
                                        dpa_deps, "out", f"padma_{gpos}")
            d_pa_prev = [d_pa]
            if dbg_on("u"):
                _, d_ud = p.dma_step("g", [(u_dbg[row0:row0 + 128, :], u_f[:, :])],
                                     [t_u, t_u_dma], "dbg", f"udbg_{gpos}")
                _, d_pd = p.dma_step("g", [(pa_dbg[row0:row0 + 128, :], pa_f[:, :])],
                                     [t_pa, t_pa_dma], "dbg", f"padbg_{gpos}")
                d_u_prev.append(d_ud)
                d_pa_prev.append(d_pd)

            t_prev_ttt_tail = t_pa

    st_dbg = dbg_out("st_dbg", (128, B * 512))
    if dbg_on("st") and "A" in phases:
        p.dma_step("g", [(st_dbg[:, :], WTst[:, :])],
                   [p.tok("v"), p.tok("s"), p.tok("t")], "dbg", "stdbg")

    pA_done = p.barrier("pA_done")
    arenaA.close()

    TTOK = 256 if T % 256 == 0 else 128
    NSUB = TTOK // 128
    NTT = T // TTOK
    x1_dbg = dbg_out("x1_dbg", (NT, C))

    if "B" in phases:
        arenaB, AB = mkarena()
        w1_sb = AB("w1_sb", [128, 8 * 4096], BF16)
        w2_sb = AB("w2_sb", [128, 32 * 1024], BF16)
        g2bc = AB("g2bc", [128, C], F32)
        be2bc = AB("be2bc", [128, C], F32)
        b2bc = AB("b2bc", [128, C], F32)
        b1cols = AB("b1cols", [128, 32], F32)
        ln2row_bf = AB("ln2row_bf", [1, 2 * C], BF16)
        b1row_bf = AB("b1row_bf", [1, 1024], BF16)
        b2row_bf = AB("b2row_bf", [1, C], BF16)
        u_t = AB("u_t", [128, 2 * NSUB * C], F32)
        pa_t = AB("pa_t", [128, NSUB * C], F32)
        h2w = AB("h2w", [128, C], F32)
        h2bf = AB("h2bf", [128, NSUB * C], BF16)
        h2T_bf = AB("h2T_bf", [128, 8 * TTOK], BF16)
        gel0 = AB("gel0", [128, TTOK], BF16)
        gel1 = AB("gel1", [128, TTOK], BF16)
        bnsB = AB("bnsB", [128, 12], F32)
        agB = AB("agB", [128, 2], F32)
        rstdB = AB("rstdB", [128, 1], F32)
        nmrB = AB("nmrB", [128, 1], F32)

        tB_w1, dB_w1 = p.dma_step("g", [
            (w1_sb[:, ic * 4096:(ic + 1) * 4096], w1T[ic * 128:(ic + 1) * 128, :])
            for ic in range(8)], [pA_done["g"]], "wld", "B_w1")
        tB_w2, dB_w2 = p.dma_step("g", [
            (w2_sb[:, d4 * 1024:(d4 + 1) * 1024], w2T[d4 * 128:(d4 + 1) * 128, :])
            for d4 in range(32)], [tB_w1], "wld", "B_w2")
        tB_r, dB_r = p.dma_step("g", [
            (ln2row_bf[:, 0:C], ln2w[:, :]), (ln2row_bf[:, C:2 * C], ln2b[:, :]),
            (b2row_bf[:, :], b2_in[:, :])],
            [tB_w2], "wld", "B_rows")

        def b_g2mm(t):
            for hh in range(2):
                t.matmul(pf_a[:, hh * 512:(hh + 1) * 512], ones_row[:, :],
                         ln2row_bf[:, hh * 512:(hh + 1) * 512], start=True, stop=True)
                t.matmul(pf_a[:, 1024 + hh * 512:1024 + (hh + 1) * 512], ones_row[:, :],
                         ln2row_bf[:, C + hh * 512:C + (hh + 1) * 512], start=True, stop=True)
                t.matmul(pf_b[:, hh * 512:(hh + 1) * 512], ones_row[:, :],
                         b2row_bf[:, hh * 512:(hh + 1) * 512], start=True, stop=True)
        tB_gm = p.step("t", [dB_r, pA_done["t"]], b_g2mm, "B_g2mm")
        prev_b1m = None
        for rr in range(4):
            ld_deps = [tB_w2] if prev_b1m is None else [prev_b1m]
            _, d_b1 = p.dma_step("g", [(b1row_bf[:, :],
                                        b1_in[0:1, rr * 1024:(rr + 1) * 1024])],
                                 ld_deps, "wld", f"B_b1r{rr}")
            def b1mm(t, rr=rr):
                for ch in range(8):
                    t.matmul(pf_b[:, 1024 + rr * 8 + ch:1025 + rr * 8 + ch],
                             b1row_bf[:, ch * 128:(ch + 1) * 128],
                             ones_row[0:1, 0:1], start=True, stop=True)
            prev_b1m = p.step("t", [d_b1] + ([tB_gm] if rr == 0 else []), b1mm,
                              f"B_b1m{rr}")
        def b_g2ev(v):
            v.tensor_copy(g2bc[:, :], pf_a[:, 0:1024])
            v.tensor_copy(be2bc[:, :], pf_a[:, 1024:2048])
            v.tensor_copy(b2bc[:, :], pf_b[:, 0:1024])
            v.tensor_copy(b1cols[:, :], pf_b[:, 1024:1056])
        tB_ge = p.step("v", [tB_gm, prev_b1m, pA_done["v"]], b_g2ev, "B_g2ev")

        dB_out_prev = []
        t_outdma = None
        t_prev_tail = None
        t_prev_pbfB = None
        lnB_toks = {}
        uld_war = [None, None]
        NGI = B * NTT

        def emit_frontB(gi, tr_dep):
            par = gi % 2
            b_ = gi // NTT
            row0 = b_ * T + (gi % NTT) * TTOK
            deps_u = [pA_done["y"]]
            if uld_war[par] is not None:
                deps_u.append(uld_war[par])
            t_uin, d_uin = p.dma_step("y", [
                (u_t[:, (par * NSUB + s) * C:(par * NSUB + s + 1) * C],
                 u_s[row0 + s * 128:row0 + (s + 1) * 128, :])
                for s in range(NSUB)], deps_u, "xin", f"B_uin{gi}")
            casts = []
            t4 = None
            for s in range(NSUB):
                usub = u_t[:, (par * NSUB + s) * C:(par * NSUB + s + 1) * C]
                def stB(v, usub=usub):
                    v.bn_stats(bnsB[:, 0:6], usub[:, 0:512])
                    v.bn_stats(bnsB[:, 6:12], usub[:, 512:1024])
                    v.drain()
                    v.bn_aggr(agB[:, :], bnsB[:, :])
                t1 = p.step("v", [d_uin, tB_ge], stB, f"B_ln{gi}_{s}")
                def rsB(sc):
                    sc.activation(rstdB[:, :], agB[:, 1:2], AF.Ln, bias=LN_EPS,
                                  scale=1.0)
                    sc.drain()
                    sc.activation(rstdB[:, :], rstdB[:, :], AF.Exp, bias=0.0,
                                  scale=-0.5)
                t2 = p.step("s", [t1], rsB, f"B_rstd{gi}_{s}")
                def nmB(v):
                    v.tensor_scalar(nmrB[:, :], agB[:, 0:1], rstdB[:, 0:1], -1.0,
                                    OP.mult, OP.mult)
                t3 = p.step("v", [t2], nmB, f"B_nmr{gi}_{s}")
                def apB(v, usub=usub):
                    v.tensor_scalar(h2w[:, :], usub, rstdB[:, 0:1], nmrB[:, 0:1],
                                    OP.mult, OP.add)
                    v.drain()
                    v.tensor_mul(h2w[:, :], h2w[:, :], g2bc[:, :])
                    v.drain()
                    v.tensor_add(h2w[:, :], h2w[:, :], be2bc[:, :])
                t4 = p.step("v", [t3], apB, f"B_h2{gi}_{s}")
                def cstB(sc, s=s):
                    sc.copy(h2bf[:, s * C:(s + 1) * C], h2w[:, :])
                deps_c = [t4]
                if s == 0 and tr_dep is not None:
                    deps_c.append(tr_dep)
                casts.append(p.step("s", deps_c, cstB, f"B_h2c{gi}_{s}"))
            uld_war[par] = t4
            lnB_toks[gi] = casts

        for b in range(B):
            for ti in range(NTT):
                gi = b * NTT + ti
                row0 = b * T + ti * TTOK
                if gi == 0:
                    emit_frontB(0, None)
                t_pain, d_pain = p.dma_step("y", [
                    (pa_t[:, s * C:(s + 1) * C],
                     pa_s[row0 + s * 128:row0 + (s + 1) * 128, :])
                    for s in range(NSUB)], [pA_done["y"]] + dB_out_prev,
                    "xin", f"B_pain{gi}")

                casts = lnB_toks.pop(gi)
                t_h2T = None
                tr2 = None
                for s in range(NSUB):
                    deps_t = [casts[s]] + ([t_prev_pbfB] if t_prev_pbfB else [])
                    def trB(t, s=s):
                        for ic in range(8):
                            t.transpose(pbf[:, ic * 128:(ic + 1) * 128],
                                        h2bf[:, s * C + ic * 128:s * C + (ic + 1) * 128],
                                        ident[:, :])
                    t6 = p.step("t", deps_t, trB, f"B_tr{gi}_{s}")
                    tr2 = t6
                    def evB(v, s=s):
                        v.tensor_copy(
                            h2T_bf[:, :].rearrange("p (i w) -> p i w", w=TTOK)[:, :, s * 128:(s + 1) * 128],
                            pbf[:, :].rearrange("p (i w) -> p i w", w=128))
                    t_h2T = p.step("v", [t6], evB, f"B_trev{gi}_{s}")
                    t_prev_pbfB = t_h2T

                t_w1 = {}
                t_gel = {}
                t_w2 = {}
                gels = [gel0, gel1]
                def emit_w2(d4c):
                    def w2mm(t, d4c=d4c):
                        for s in range(NSUB):
                            for hh in range(2):
                                t.matmul(pf_a[:, (s * 2 + hh) * 512:(s * 2 + hh + 1) * 512],
                                         gels[d4c % 2][:, s * 128:(s + 1) * 128],
                                         w2_sb[:, d4c * 1024 + hh * 512:d4c * 1024 + (hh + 1) * 512],
                                         start=(d4c == 0), stop=(d4c == 31))
                    deps = [t_gel[d4c]]
                    if d4c == 0:
                        deps.append(t_prev_tail or tB_ge)
                    t_w2[d4c] = p.step("t", deps, w2mm, f"B_w2_{gi}_{d4c}")
                for d4c in range(32):
                    def w1mm(t, d4c=d4c):
                        for ic in range(8):
                            t.matmul(pf_b[:, (d4c % 2) * 512:(d4c % 2) * 512 + TTOK],
                                     w1_sb[:, ic * 4096 + d4c * 128:ic * 4096 + (d4c + 1) * 128],
                                     h2T_bf[:, ic * TTOK:(ic + 1) * TTOK],
                                     start=(ic == 0), stop=(ic == 7))
                    deps = [t_h2T, dB_w1]
                    if d4c >= 2:
                        deps.append(t_gel[d4c - 2])
                    t_w1[d4c] = p.step("t", deps, w1mm, f"B_w1_{gi}_{d4c}")
                    def gelB(sc, d4c=d4c):
                        sc.activation(gels[d4c % 2][:, :], pf_b[:, (d4c % 2) * 512:(d4c % 2) * 512 + TTOK],
                                      AF.Gelu, bias=b1cols[:, d4c:d4c + 1], scale=1.0)
                    t_gel[d4c] = p.step("s", [t_w1[d4c], tB_ge] +
                                        ([t_w2[d4c - 2]] if d4c >= 2 else []),
                                        gelB, f"B_gel_{gi}_{d4c}")
                    if d4c >= 1:
                        emit_w2(d4c - 1)
                emit_w2(31)
                tb2 = t_w2[31]

                if gi + 1 < NGI:
                    emit_frontB(gi + 1, tr2)

                def outB(v):
                    for s in range(NSUB):
                        for hh in range(2):
                            sl = pa_t[:, s * C + hh * 512:s * C + (hh + 1) * 512]
                            v.tensor_add(sl, sl, pf_a[:, (s * 2 + hh) * 512:(s * 2 + hh + 1) * 512])
                            v.drain()
                            v.tensor_add(sl, sl, b2bc[:, hh * 512:(hh + 1) * 512])
                t_o = p.step("v", [tb2, d_pain, tB_ge], outB, f"B_out{gi}")
                t_prev_tail = t_o
                od_deps = [t_o] + ([t_outdma] if t_outdma else [])
                trs_o = [(x1_s[row0 + s * 128:row0 + (s + 1) * 128, :],
                          pa_t[:, s * C:(s + 1) * C]) for s in range(NSUB)]
                t_outdma, d_o = p.dma_step("g", trs_o, od_deps, "out", f"B_odma{gi}")
                dB_out_prev = [d_o]
                if dbg_on("x1"):
                    _, d_xd = p.dma_step("g", [(x1_dbg[row0 + s * 128:row0 + (s + 1) * 128, :],
                                                pa_t[:, s * C:(s + 1) * C]) for s in range(NSUB)],
                                         [t_o, t_outdma], "dbg", f"B_xdbg{gi}")
                    dB_out_prev.append(d_xd)

        pB_done = p.barrier("pB_done")
        arenaB.close()

    wsc_g = A("wsc_g", [128, 22], F32)
    wsc_u = A("wsc_u", [128, 22], F32)
    wsc_d = A("wsc_d", [128, 8], F32)

    if "E" in phases:
        arenaW, AW = mkarena()
        wstg = AW("wstg", [128, 2 * 2816], F32)
        wq1 = AW("wq1", [128, 2816], F32)
        wqb = AW("wqb", [128, 2816], BF16)
        wqev = AW("wqev", [128, 1024], BF16)
        wmc = AW("wmc", [128, 1], F32)
        wrc = AW("wrc", [128, 1], F32)
        prevB = p.barrier("preW")

        jobs = []
        for rc in range(22):
            jobs.append((wg_in, wgq_s, wsc_g, rc, 1024, 8))
        for rc in range(22):
            jobs.append((wu_in, wuq_s, wsc_u, rc, 1024, 8))
        for rc in range(8):
            jobs.append((wd_in, wdq_s, wsc_d, rc, 2816, 22))
        jobs_t = {}
        t_q1s = {}
        t_evW_prev = None
        d_outs = []
        for jidx, (w_src, w_dst, wsc, rc, CW, NTR) in enumerate(jobs):
            buf = jidx % 2
            deps = [prevB["y"]] if jidx == 0 else [jobs_t[jidx - 1]]
            if jidx >= 2:
                deps.append(t_q1s[jidx - 2])
            t_l, d_l = p.dma_step("y", [(wstg[:, buf * 2816:buf * 2816 + CW],
                                         w_src[rc * 128:(rc + 1) * 128, :])],
                                  deps, "wld", f"W_l{jidx}")
            jobs_t[jidx] = t_l
            def q0(v, buf=buf, CW=CW, wsc=wsc, rc=rc):
                v.tensor_reduce(wmc[:, :], wstg[:, buf * 2816:buf * 2816 + CW],
                                axis=AX.X, op=OP.max, apply_absolute_value=True)
                v.drain()
                v.reciprocal(wrc[:, :], wmc[:, :])
                v.drain()
                v.tensor_scalar_mul(wrc[:, :], wrc[:, :], QMAX)
                v.tensor_scalar_mul(wsc[:, rc:rc + 1], wmc[:, :], 1.0 / QMAX)
                v.drain()
                v.tensor_scalar(wq1[:, 0:CW], wstg[:, buf * 2816:buf * 2816 + CW],
                                wrc[:, 0:1], MR, OP.mult, OP.add)
                v.drain()
                v.tensor_scalar(wq1[:, 0:CW], wq1[:, 0:CW], -MR, -(QMAX + 1.0),
                                OP.add, OP.max)
                v.drain()
                v.tensor_scalar(wqb[:, 0:CW], wq1[:, 0:CW], QMAX, None, OP.min)
            dep_q = [d_l] + ([t_q1s[jidx - 1]] if jidx >= 1 else [])
            if t_evW_prev is not None:
                dep_q.append(t_trW_last)
            t_q = p.step("v", dep_q, q0, f"W_q{jidx}")
            t_q1s[jidx] = t_q
            for grp in range(0, NTR, 8):
                n = min(8, NTR - grp)
                def trW(t, grp=grp, n=n):
                    for i in range(n):
                        t.transpose(pbf[:, i * 128:(i + 1) * 128],
                                    wqb[:, (grp + i) * 128:(grp + i + 1) * 128],
                                    ident[:, :])
                depsT = [t_q, prevB["t"]] + ([t_evW_prev] if t_evW_prev else [])
                t_tr = p.step("t", depsT, trW, f"W_tr{jidx}_{grp}")
                t_trW_last = t_tr
                def evW(s, n=n):
                    s.copy(wqev[:, 0:n * 128], pbf[:, 0:n * 128])
                t_ev = p.step("s", [t_tr] + d_outs, evW, f"W_ev{jidx}_{grp}")
                t_evW_prev = t_ev
                trs = [(w_dst[(grp + i) * 128:(grp + i + 1) * 128,
                              rc * 128:(rc + 1) * 128],
                        wqev[:, i * 128:(i + 1) * 128]) for i in range(n)]
                t_od, d_od = p.dma_step("g", trs, [t_ev], "out", f"W_o{jidx}_{grp}")
                d_outs = [d_od]
        pW_done = p.barrier("pW_done")
        arenaW.close()

    mx_col = A("mx_col", [128, 1], F32)
    gmax_x = A("gmax_x", [1, 1], F32)
    sx_col = A("sx_col", [128, 1], F32)
    rsx_col = A("rsx_col", [128, 1], F32)
    gmax_h = A("gmax_h", [1, 1], F32)
    sh_col = A("sh_col", [128, 1], F32)
    rsh_col = A("rsh_col", [128, 1], F32)
    scr11 = A("scr11", [1, 4], F32)
    cc_sem = nc.alloc_semaphore("cc_sem")
    p.dsems["cc0"] = [cc_sem, 0]
    p.dlanes["cc"] = {"keys": ["cc0"], "rr": 0}

    pairswap = []
    for i in range(16):
        pairswap += [2 * i + 1, 2 * i]
    Lm05 = float(L) ** -0.5

    if "C" in phases:
        arenaCD, ACD = mkarena()
        klatT = ACD("klatT", [128, B * 4 * T], BF16)
        qfkT = ACD("qfkT", [128, B * 4 * T], BF16)
        vI = ACD("vI", [128, B * NPOS * 528], BF16)
        cos_fm = ACD("cos_fm", [128, T], BF16)
        sin_fm = ACD("sin_fm", [128, T], BF16)
        q2lBD = ACD("q2lBD", [128, 64], BF16)
        kv2lBD = ACD("kv2lBD", [128, 64], BF16)
        kv2l_rep = ACD("kv2l_rep", [128, 32], BF16)
        wqkBD = ACD("wqkBD", [128, 128], BF16)
        qb_col = ACD("qb_col", [128, 1], F32)
        kvb_col = ACD("kvb_col", [128, 1], F32)
        wqkb_col = ACD("wqkb_col", [128, 1], F32)
        kvb_rep = ACD("kvb_rep", [1, 512], BF16)
        kvb_bc = ACD("kvb_bc", [128, 512], F32)
        l2oI = ACD("l2oI", [33, 64], BF16)
        cpb_bf = ACD("cpb_bf", [1, C], BF16)

        arenaC, AC = mkarena()
        ca_sb = AC("ca_sb", [128, 8 * 3 * C], BF16)
        cab_cols = AC("cab_cols", [128, 24], F32)
        cab_bf = klatT[0:1, 0:3 * C]
        kvb_row = AC("kvb_row", [1, L], BF16)
        x1_t = AC("x1_t", [128, 2 * C], F32)
        xn2_f = AC("xn2_f", [128, C], F32)
        xnT_st = AC("xnT_st", [128, 8 * 256], F32)
        xnT_bfc = AC("xnT_bfc", [128, 2048], BF16)
        qkv_bf = AC("qkv_bf", [128, 24 * 256], BF16)
        qr_f = AC("qr_f", [128, 1024], F32)
        qr_bf = AC("qr_bf", [128, 1024], BF16)
        qsw = xn2_f
        bnsC = AC("bnsC", [128, 12], F32)
        agC = AC("agC", [128, 2], F32)
        mxsc = AC("mxsc", [128, 1], F32)
        rstdC = AC("rstdC", [128, 1], F32)
        nmrC = AC("nmrC", [128, 1], F32)
        iotaw = xnT_st[0:1, 0:T]
        invf_row = AC("invf_row", [1, 32], F32)
        sgn_row = AC("sgn_row", [1, 32], F32)
        invf_col = AC("invf_col", [128, 1], F32)
        sgn_col = AC("sgn_col", [128, 1], F32)
        tbc = x1_t[:, 0:T]
        s2t = xnT_st[:, 0:T]
        c2t = qkv_bf[:, :].bitcast(F32)[:, 0:T]

        prevW = p.barrier("preC")
        tC_r, dC_r = p.dma_step("g", [
            (cab_bf[:, :], cab[:, :]), (kvb_row[:, :], kv2lb[:, :]),
            (q2lBD[0:64, 0:32], q2lT[:, :]), (q2lBD[64:128, 32:64], q2lT[:, :]),
            (kv2lBD[0:64, 0:32], kv2lT[:, :]), (kv2lBD[64:128, 32:64], kv2lT[:, :]),
            (kv2l_rep[0:64, :], kv2lT[:, :]), (kv2l_rep[64:128, :], kv2lT[:, :]),
            (wqkBD[0:32, 0:32], wqkT[:, :]), (wqkBD[32:64, 32:64], wqkT[:, :]),
            (wqkBD[64:96, 64:96], wqkT[:, :]), (wqkBD[96:128, 96:128], wqkT[:, :]),
            (l2oI[0:32, :], l2oT[:, :]), (l2oI[32:33, :], l2ob[:, :]),
            (cpb_bf[:, :], cpb[:, :])],
            [prevW["g"]], "cst", "C_rows")

        def c_zero(g):
            g.memset(q2lBD[0:64, 32:64], 0.0)
            g.memset(q2lBD[64:128, 0:32], 0.0)
            g.memset(kv2lBD[0:64, 32:64], 0.0)
            g.memset(kv2lBD[64:128, 0:32], 0.0)
            g.memset(wqkBD[0:32, 32:128], 0.0)
            g.memset(wqkBD[32:64, 0:32], 0.0)
            g.memset(wqkBD[32:64, 64:128], 0.0)
            g.memset(wqkBD[64:96, 0:64], 0.0)
            g.memset(wqkBD[64:96, 96:128], 0.0)
            g.memset(wqkBD[96:128, 0:96], 0.0)
            g.memset(mx_col[:], 0.0)
            g.iota(iotaw, pattern=[[1, T]], base=0, channel_multiplier=0,
                   allow_small_or_imprecise_dtypes=True)
            g.iota(invf_row[:, :], pattern=[[1, 16], [0, 2]], base=0,
                   channel_multiplier=0, allow_small_or_imprecise_dtypes=True)
            g.iota(sgn_row[:, :], pattern=[[0, 16], [1, 2]], base=0,
                   channel_multiplier=0, allow_small_or_imprecise_dtypes=True)
        tC_z = p.step("g", [prevW["g"], dC_r], c_zero, "C_zero")

        p.dsems["colC00"] = [nc.alloc_semaphore("dma_colC0"), 0]
        rec_cc = p.dsems["colC00"]
        def c_cols(g):
            sem = rec_cc[0]
            for gq in range(4):
                g.dma_start(out=qb_col[32 * gq:32 * gq + 32, 0:1], in_=q2lb[0:1, :],
                            allow_slow_non_contiguous=True).then_inc(sem, 16)
                g.dma_start(out=kvb_col[32 * gq:32 * gq + 32, 0:1], in_=kv2lb[0:1, :],
                            allow_slow_non_contiguous=True).then_inc(sem, 16)
                g.dma_start(out=wqkb_col[32 * gq:32 * gq + 32, 0:1], in_=wqkb[0:1, :],
                            allow_slow_non_contiguous=True).then_inc(sem, 16)
        rec_cc[1] += 16 * 12
        tC_cols = p.step("g", [tC_z], c_cols, "C_cols")
        dC_cols = DmaTok((("colC00", rec_cc[1]),))

        def c_invf(s):
            s.activation(invf_row[:, :], invf_row[:, :], AF.Exp,
                         bias=0.0, scale=-math.log(10000.0) / 16.0)
        tC_iv = p.step("s", [tC_z], c_invf, "C_invf")
        def c_sgn(v):
            v.tensor_scalar(sgn_row[:, :], sgn_row[:, :], 2.0, -1.0, OP.mult, OP.add)
        tC_sg = p.step("v", [tC_z], c_sgn, "C_sgn")
        def c_tmm(t):
            wch = min(512, T)
            for ch in range(T // wch):
                t.matmul(pf_a[:, ch * wch:(ch + 1) * wch], onesf_row[:, :],
                         iotaw[0:1, ch * wch:(ch + 1) * wch], start=True, stop=True)
            for gq in range(4):
                t.matmul(pf_b[32 * gq:32 * gq + 32, 1024:1025], invf_row[:, :],
                         one_f[:, :], start=True, stop=True,
                         tile_position=(0, 32 * gq))
                t.matmul(pf_b[32 * gq:32 * gq + 32, 1026:1027], sgn_row[:, :],
                         one_f[:, :], start=True, stop=True,
                         tile_position=(0, 32 * gq))
        tC_tm = p.step("t", [tC_z, tC_iv, tC_sg, prevW["t"]], c_tmm, "C_tmm")
        def c_tb(v):
            v.tensor_copy(tbc[:, :], pf_a[:, 0:T])
            v.tensor_copy(invf_col[:, :], pf_b[:, 1024:1025])
            v.tensor_copy(sgn_col[:, :], pf_b[:, 1026:1027])
        tC_tb = p.step("v", [tC_tm], c_tb, "C_tb")
        def c_rope1(v):
            v.tensor_scalar(tbc, tbc, invf_col[:, 0:1], None, OP.mult)
            v.drain()
            v.tensor_scalar_mul(s2t, tbc, 1.0 / (2.0 * math.pi))
        tC_r1 = p.step("v", [tC_tb], c_rope1, "C_rope1")
        def c_rope2(s):
            s.activation(c2t, s2t, AF.Copy, bias=MR, scale=1.0)
        tC_r2 = p.step("s", [tC_r1], c_rope2, "C_rope2")
        def c_rope3(v):
            v.tensor_scalar_add(c2t, c2t, -MR)
            v.drain()
            v.tensor_sub(s2t, s2t, c2t)
        tC_r3 = p.step("v", [tC_r2], c_rope3, "C_rope3")
        dbg_r_tok = []
        if dbg_on("r"):
            r_dbg = dbg_out("r_dbg", (128, T))
            _, d_rd = p.dma_step("g", [(r_dbg[:, :], s2t)], [tC_r3], "dbg", "C_rdbg")
            dbg_r_tok = [d_rd]
        def c_rope4(s):
            s.activation(c2t, s2t, AF.Sin, bias=math.pi / 2.0,
                         scale=-math.pi)
            s.drain()
            s.activation(s2t, s2t, AF.Sin, bias=0.0, scale=math.pi)
            s.drain()
            s.activation(cos_fm[:, :], s2t, AF.Square, bias=0.0, scale=1.0)
        tC_r4 = p.step("s", [tC_r3] + dbg_r_tok, c_rope4, "C_rope4")
        def c_rope5(v):
            v.tensor_scalar(cos_fm[:, :], cos_fm[:, :], -2.0, 1.0, OP.mult, OP.add)
            v.drain()
            v.tensor_mul(sin_fm[:, :], s2t, c2t)
            v.drain()
            v.tensor_scalar(sin_fm[:, :], sin_fm[:, :], sgn_col[:, 0:1], 2.0,
                            OP.mult, OP.mult)
            v.drain()
            v.tensor_copy(kvb_rep[:, :].rearrange("p (h l) -> p h l", l=32),
                          kvb_row[:, :].unsqueeze(1).to_broadcast([1, 16, 32]))
        tC_r5 = p.step("v", [tC_r4, dC_r], c_rope5, "C_rope5")

        def c_kvbc(t):
            t.matmul(pf_b[:, 512:1024], ones_row[:, :], kvb_rep[:, :],
                     start=True, stop=True)
        tC_kvm = p.step("t", [tC_r5, tC_tm], c_kvbc, "C_kvbm")
        def c_kvbe(v):
            v.tensor_copy(kvb_bc[:, :], pf_b[:, 512:1024])
        tC_kv = p.step("v", [tC_kvm], c_kvbe, "C_kvbe")

        def c_cabc(t):
            for d3c in range(24):
                t.matmul(pf_b[:, 1040 + d3c:1041 + d3c],
                         cab_bf[:, d3c * 128:(d3c + 1) * 128],
                         ones_row[0:1, 0:1], start=True, stop=True)
        tC_cabm = p.step("t", [dC_r, tC_tm], c_cabc, "C_cabm")
        def c_cabe(v):
            v.tensor_copy(cab_cols[:, :], pf_b[:, 1040:1064])
        tC_cc = p.step("v", [tC_cabm], c_cabe, "C_cabe")

        tC_w, dC_w = p.dma_step("g", [
            (ca_sb[:, ic * 3072:(ic + 1) * 3072], caT[ic * 128:(ic + 1) * 128, :])
            for ic in range(8)], [prevW["g"], tC_cols], "wld", "C_ca")

        NTTC = T // 256
        d_inC_prev = []
        t_xnT_dma = None
        t_prev_pfbC = prevW["t"]
        t_prev_tailC = None
        t_mx = tC_z
        t_ptdma = None
        for b in range(B):
            for ti in range(NTTC):
                gi = b * NTTC + ti
                row0 = b * T + ti * 256
                t0 = ti * 256
                t_in, d_in = p.dma_step("y", [
                    (x1_t[:, s * C:(s + 1) * C],
                     x1_s[row0 + s * 128:row0 + (s + 1) * 128, :])
                    for s in range(2)], [prevW["y"], tC_r5] + d_inC_prev,
                    "xin", f"C_in{gi}")

                t_tr_toks = []
                for s in range(2):
                    xsub = x1_t[:, s * C:(s + 1) * C]
                    def stC(v, xsub=xsub):
                        v.bn_stats(bnsC[:, 0:6], xsub[:, 0:512])
                        v.bn_stats(bnsC[:, 6:12], xsub[:, 512:1024])
                        v.drain()
                        v.bn_aggr(agC[:, :], bnsC[:, :])
                    t1 = p.step("v", [d_in, tC_r5], stC, f"C_ln{gi}_{s}")
                    def rsC(sc):
                        sc.activation(rstdC[:, :], agC[:, 1:2], AF.Ln, bias=LN_EPS,
                                      scale=1.0)
                        sc.drain()
                        sc.activation(rstdC[:, :], rstdC[:, :], AF.Exp, bias=0.0,
                                      scale=-0.5)
                    t2 = p.step("s", [t1], rsC, f"C_rstd{gi}_{s}")
                    def nmC(v):
                        v.tensor_scalar(nmrC[:, :], agC[:, 0:1], rstdC[:, 0:1], -1.0,
                                        OP.mult, OP.mult)
                    t3 = p.step("v", [t2], nmC, f"C_nmr{gi}_{s}")
                    def apC(v, xsub=xsub):
                        v.tensor_scalar(xn2_f[:, :], xsub, rstdC[:, 0:1], nmrC[:, 0:1],
                                        OP.mult, OP.add)
                        v.drain()
                        v.tensor_reduce(mxsc[:, :], xn2_f[:, :], axis=AX.X,
                                        op=OP.max, apply_absolute_value=True)
                        v.drain()
                        v.tensor_max(mx_col[:, :], mx_col[:, :], mxsc[:, :])
                    t4 = p.step("v", [t3, t_mx], apC, f"C_xn2{gi}_{s}")
                    t_mx = t4
                    def trC(t):
                        for ic in range(8):
                            t.transpose(pf_b[:, ic * 128:(ic + 1) * 128],
                                        xn2_f[:, ic * 128:(ic + 1) * 128], identf[:, :])
                    depsT = [t4, t_prev_pfbC] + ([t_tr_toks[-1]] if t_tr_toks else [])
                    t5 = p.step("t", depsT, trC, f"C_tr{gi}_{s}")
                    def evC(v, s=s):
                        v.tensor_copy(
                            xnT_st[:, :].rearrange("p (i w) -> p i w", w=256)[:, :, s * 128:(s + 1) * 128],
                            pf_b[:, 0:1024].rearrange("p (i w) -> p i w", w=128))
                    t6 = p.step("v", [t5], evC, f"C_trev{gi}_{s}")
                    t_tr_toks.append(t6)
                    t_prev_pfbC = t6
                def cstC2(sc):
                    sc.copy(xnT_bfc[:, :], xnT_st[:, :])
                t_xc = p.step("s", t_tr_toks, cstC2, f"C_xc{gi}")
                xnTdeps = t_tr_toks + ([t_xnT_dma] if t_xnT_dma else [])
                t_xnT_dma, d_xnT = p.dma_step("g", [
                    (xnT_s[ic * 128:(ic + 1) * 128, row0:row0 + 256],
                     xnT_st[:, ic * 256:(ic + 1) * 256]) for ic in range(8)],
                    xnTdeps, "out", f"C_xnT{gi}")
                d_inC_prev = [d_xnT]

                t_ca = {}
                t_cev = {}
                for d3c in range(24):
                    def camm1(t, d3c=d3c):
                        for ic in range(8):
                            t.matmul(pf_a[:, (d3c % 4) * 512:(d3c % 4) * 512 + 256],
                                     ca_sb[:, ic * 3072 + d3c * 128:ic * 3072 + (d3c + 1) * 128],
                                     xnT_bfc[:, ic * 256:(ic + 1) * 256],
                                     start=(ic == 0), stop=(ic == 7))
                    deps = [t_xc, dC_w, tC_tm]
                    if t_prev_tailC and d3c == 0:
                        deps.append(t_prev_tailC)
                    if d3c >= 4:
                        deps.append(t_cev[d3c - 4])
                    t_ca[d3c] = p.step("t", deps, camm1, f"C_ca{gi}_{d3c}")
                    def cev(v, d3c=d3c):
                        v.tensor_scalar(qkv_bf[:, d3c * 256:(d3c + 1) * 256],
                                        pf_a[:, (d3c % 4) * 512:(d3c % 4) * 512 + 256],
                                        cab_cols[:, d3c:d3c + 1], None, OP.add)
                    t_cev[d3c] = p.step("v", [t_ca[d3c], tC_cc], cev, f"C_cev{gi}_{d3c}")

                def lat_q(t):
                    for g_ in range(8):
                        for s in range(2):
                            t.matmul(pf_b[64 * (g_ % 2):64 * (g_ % 2) + 64,
                                          (g_ // 2) * 256 + s * 128:
                                          (g_ // 2) * 256 + (s + 1) * 128],
                                     q2lBD[:, :],
                                     qkv_bf[:, g_ * 256 + s * 128:g_ * 256 + (s + 1) * 128],
                                     start=True, stop=True,
                                     tile_position=(0, 64 * (g_ % 2)))
                t_lq = p.step("t", [t_cev[7], t_prev_pfbC, dC_r], lat_q, f"C_lq{gi}")
                def lat_k(t):
                    for g_ in range(8):
                        for s in range(2):
                            t.matmul(pf_a[64 * (g_ % 2):64 * (g_ % 2) + 64,
                                          1024 + (g_ // 2) * 256 + s * 128:
                                          1024 + (g_ // 2) * 256 + (s + 1) * 128],
                                     kv2lBD[:, :],
                                     qkv_bf[:, (8 + g_) * 256 + s * 128:
                                            (8 + g_) * 256 + (s + 1) * 128],
                                     start=True, stop=True,
                                     tile_position=(0, 64 * (g_ % 2)))
                t_lk = p.step("t", [t_cev[22], t_cev[23]], lat_k, f"C_lk{gi}")

                cosv = cos_fm[:, t0:t0 + 256].unsqueeze(1).to_broadcast([128, 4, 256])
                sinv = sin_fm[:, t0:t0 + 256].unsqueeze(1).to_broadcast([128, 4, 256])
                qrv = qr_f[:, :].rearrange("p (hg w) -> p hg w", w=256)
                qsv = qsw[:, :].rearrange("p (hg w) -> p hg w", w=256)

                def rope_q(v):
                    v.tensor_scalar(qr_f[:, :], pf_b[:, 0:1024], qb_col[:, 0:1],
                                    None, OP.add)
                    v.drain()
                    v.stream_shuffle(qsw[:, :], qr_f[:, :], pairswap)
                    v.drain()
                    v.tensor_tensor(qrv, qrv, cosv, OP.mult)
                    v.tensor_tensor(qsv, qsv, sinv, OP.mult)
                    v.drain()
                    v.tensor_add(qr_f[:, :], qr_f[:, :], qsw[:, :])
                t_rq = p.step("v", [t_lq, dC_cols, tC_r5], rope_q, f"C_ropeq{gi}")
                def rq_cast(sc):
                    sc.copy(qr_bf[:, :], qr_f[:, :])
                t_rqc = p.step("s", [t_rq], rq_cast, f"C_rqc{gi}")

                klv = klatT[:, :].rearrange("p (bb hg w) -> p (bb hg) w", hg=4, w=T)[
                    :, 4 * b:4 * b + 4, t0:t0 + 256]
                def rope_k(v, klv=klv):
                    v.tensor_scalar(qsw[:, :], pf_a[:, 1024:2048], kvb_col[:, 0:1],
                                    None, OP.add)
                    v.drain()
                    v.stream_shuffle(qr_f[:, :], qsw[:, :], pairswap)
                    v.drain()
                    v.tensor_tensor(qsv, qsv, cosv, OP.mult)
                    v.tensor_tensor(qrv, qrv, sinv, OP.mult)
                    v.drain()
                    v.tensor_add(qsw[:, :], qsw[:, :], qr_f[:, :])
                    v.drain()
                    v.tensor_copy(klv, qsv)
                t_rk = p.step("v", [t_lk, t_rqc, t_rq], rope_k, f"C_ropek{gi}")

                def qfk_mm(t):
                    for hg in range(4):
                        t.matmul(pf_a[:, hg * 256:(hg + 1) * 256], wqkBD[:, :],
                                 qr_bf[:, hg * 256:(hg + 1) * 256],
                                 start=True, stop=True)
                t_qf = p.step("t", [t_rqc, t_cev[23], t_lk], qfk_mm, f"C_qfk{gi}")
                qfv = qfkT[:, :].rearrange("p (bb hg w) -> p (bb hg) w", hg=4, w=T)[
                    :, 4 * b:4 * b + 4, t0:t0 + 256]
                def qfk_ev(v, qfv=qfv):
                    v.tensor_scalar(qfv,
                                    pf_a[:, 0:1024].rearrange("p (hg w) -> p hg w", w=256),
                                    wqkb_col[:, 0:1], cpool[Lm05][:, 0:1],
                                    OP.add, OP.mult)
                t_qfe = p.step("v", [t_qf, dC_cols], qfk_ev, f"C_qfe{gi}")

                t_lv_toks = []
                for s in range(2):
                    kt = ti * 2 + s
                    blk = (b * NPOS + kt) * 528
                    def lat_v(t, s=s):
                        for i, h in enumerate([0, 2, 4, 6, 8, 10, 12, 14, 1, 3, 5, 7, 9, 11, 13, 15]):
                            if i == 8:
                                t.drain()
                            t.matmul(pf_b[:, 1024 + h * 32:1024 + (h + 1) * 32],
                                     qkv_bf[:, (16 + h // 2) * 256 + s * 128:
                                            (16 + h // 2) * 256 + (s + 1) * 128]
                                     [64 * (h % 2):64 * (h % 2) + 64, :],
                                     kv2l_rep[64 * (h % 2):64 * (h % 2) + 64, :],
                                     start=True, stop=True)
                    deps = [t_cev[23], tC_r5] + ([t_lv_toks[-1]] if t_lv_toks else [])
                    t_lvm = p.step("t", deps + [t_qf], lat_v, f"C_lv{gi}_{s}")
                    def lat_vev(v, blk=blk):
                        vIblk = vI[:, blk:blk + 528].rearrange("p (h l) -> p h l", l=33)
                        v.tensor_tensor(vIblk[:, :, 0:32],
                                        pf_b[:, 1024:1536].rearrange("p (h l) -> p h l", l=32),
                                        kvb_bc[:, :].rearrange("p (h l) -> p h l", l=32),
                                        OP.add)
                        v.drain()
                        v.memset(vIblk[:, :, 32:33], 1.0)
                    t_lve = p.step("v", [t_lvm, tC_kv], lat_vev, f"C_lvev{gi}_{s}")
                    t_lv_toks.append(t_lve)

                t_prev_tailC = t_qfe
                t_prev_pfbC = t_lv_toks[-1]

        def mxfin_mm(t):
            t.matmul(pf_b[0:1, 1024:1152], mx_col[:, :], identf[:, :],
                     start=True, stop=True)
        t_mm1 = p.step("t", [t_mx, t_prev_tailC, p.tok("v")], mxfin_mm, "C_mxmm")
        def mxfin_v(v):
            v.tensor_reduce(gmax_x[:, :], pf_b[0:1, 1024:1152], axis=AX.X, op=OP.max)
        t_mv1 = p.step("v", [t_mm1], mxfin_v, "C_mxv")
        tcc1, dcc1_out = p.dma_step("g", [(ccxi.ap()[0:1, 0:1], gmax_x[:, :])],
                                    [t_mv1], "ccio", "C_ccout")
        cc_inc = 16 if cfg.get("skip_cc") else 1
        if cfg.get("skip_cc"):
            def cc1(g):
                g.dma_start(out=ccxo.ap()[:, :], in_=ccxi.ap()[:, :],
                            ).then_inc(cc_sem, 16)
        else:
            def cc1(g):
                g.collective_compute(
                    "AllReduce", OP.max, replica_groups=RG,
                    ins=[ccxi.ap().opt()], outs=[ccxo.ap().opt()],
                ).then_inc(cc_sem, 1)
        t_cc1 = p.step("g", [tcc1, dcc1_out], cc1, "C_cc1")
        p.dsems["cc0"][1] += cc_inc
        dcc1 = DmaTok((("cc0", cc_inc),))
        tcc1b, dcc1b = p.dma_step("g", [(gmax_x[:, :], ccxo.ap()[0:1, 0:1])],
                                  [t_cc1, dcc1], "ccio", "C_ccin")

        if dbg_on("cs"):
            cs_dbg = dbg_out("cs_dbg", (128, 2 * T))
            iv_dbg = dbg_out("iv_dbg", (128, 1))
            sg_dbg = dbg_out("sg_dbg", (128, 1))
            p.dma_step("g", [(cs_dbg[:, 0:T], cos_fm[:, :]),
                             (cs_dbg[:, T:2 * T], sin_fm[:, :]),
                             (iv_dbg[:, :], invf_col[:, :]),
                             (sg_dbg[:, :], sgn_col[:, :])],
                       [tC_r5], "dbg", "C_csdbg")
            kl_dbg = dbg_out("kl_dbg", (128, B * 4 * T))
            qf_dbg = dbg_out("qf_dbg", (128, B * 4 * T))
            vI_dbg = dbg_out("vI_dbg", (128, B * NPOS * 528))
            p.dma_step("g", [(kl_dbg[:, :], klatT[:, :]), (qf_dbg[:, :], qfkT[:, :]),
                             (vI_dbg[:, :], vI[:, :])],
                       [p.tok("v"), p.tok("s"), p.tok("t")], "dbg", "C_dbg")

        pC_done = p.barrier("pC_done")
        arenaC.close()

    pt_dbg = dbg_out("pt_dbg", (NT, C))
    if "D" in phases:
        arenaD, AD = mkarena()
        cp_sb = AD("cp_sb", [128, 8 * C], BF16)
        E_bf = AD("E_bf", [128, 2048], BF16)
        AVZ_bf = AD("AVZ_bf", [33, 2048], BF16)
        zrow_t = AD("zrow_t", [33, 2048], F32)
        onef33 = AD("onef33", [33, 1], F32)
        nu_st = AD("nu_st", [128, 32], F32)
        nu_bf = AD("nu_bf", [128, 32], BF16)
        negc_col = AD("negc_col", [16, 2], F32)
        negc_bf = AD("negc_bf", [16, 2], BF16)
        negrow_bf = AD("negrow_bf", [1, 32], BF16)
        negC_cols = AD("negC_cols", [128, 32], F32)
        invZ_c = AD("invZ_c", [128, 16], F32)
        y_sb = AD("y_sb", [128, 1024], BF16)
        yT_bf = AD("yT_bf", [128, 1024], BF16)
        pt_f = AD("pt_f", [128, 1024], F32)
        cpb_bc = AD("cpb_bc", [128, C], F32)
        x1c = AD("x1c", [128, 2 * C], F32)
        mxD = AD("mxD", [128, 1], F32)

        tD_w, dD_w = p.dma_step("g", [
            (cp_sb[:, ic * C:(ic + 1) * C], cpT[ic * 128:(ic + 1) * 128, :])
            for ic in range(8)], [pC_done["g"]], "wld", "D_cp")
        def d_prep(g):
            g.memset(onef33[:], 1.0)
        tD_pr = p.step("g", [pC_done["g"]], d_prep, "D_prep")
        def d_pbm(t):
            for hh in range(2):
                t.matmul(pf_a[:, hh * 512:(hh + 1) * 512], ones_row[:, :],
                         cpb_bf[:, hh * 512:(hh + 1) * 512], start=True, stop=True)
        tD_pb = p.step("t", [pC_done["t"]], d_pbm, "D_pbm")
        def d_pbe(v):
            v.tensor_copy(cpb_bc[:, :], pf_a[:, 0:1024])
        tD_pe = p.step("v", [tD_pb, pC_done["v"]], d_pbe, "D_pbe")

        def kl_sl(b_, h, kt):
            colb = (b_ * 4 + h // 4) * T
            return klatT[32 * (h % 4):32 * (h % 4) + 32,
                         colb + kt * 128:colb + (kt + 1) * 128]
        def qf_sl(b_, h, qt):
            colb = (b_ * 4 + h // 4) * T
            return qfkT[32 * (h % 4):32 * (h % 4) + 32,
                        colb + qt * 128:colb + (qt + 1) * 128]
        def vI_sl(b_, kt, h):
            return vI[:, (b_ * NPOS + kt) * 528 + 33 * h:
                      (b_ * NPOS + kt) * 528 + 33 * h + 33]

        t_ptdma = None
        d_pt_prev = []
        t_prev_pbfD = pC_done["t"]
        t_prev_cproj = None
        t_prev_zro = [pC_done["s"], pC_done["v"]]
        dx1_toks = {}
        t_x1last = None
        last_slot_tok = [None, None]
        last_Eslot_tok = [[None, None], [None, None]]
        prev_ct = [{}, {}]
        NGQ = B * NPOS

        class DCtx:
            pass

        def emit_round1(cx, i):
            gq, par, b_, qt = cx.gq, cx.par, cx.b, cx.qt
            h, g = divmod(i, cx.ng)
            k0 = g * 4
            k1 = min(qt + 1, k0 + 4)
            def r1(t, h=h, k0=k0, k1=k1, par=par, b_=b_, qt=qt):
                tp = (32 * (h % 4), 0) if h % 4 == 3 else None
                for kt in range(k0, k1):
                    t.matmul(pf_b[:, par * 512 + (kt - k0) * 128:
                                  par * 512 + (kt - k0 + 1) * 128],
                             kl_sl(b_, h, kt), qf_sl(b_, h, qt),
                             start=True, stop=True, tile_position=tp)
            deps = [pC_done["t"]]
            if i > 0:
                deps.append(cx.t_vmax[i - 1])
            elif last_slot_tok[par] is not None:
                deps.append(last_slot_tok[par])
            cx.t_r1[i] = p.step("t", deps, r1, f"D_r1_{gq}_{h}_{g}")
            def vmax(v, h=h, g=g, k0=k0, k1=k1, par=par):
                if g == 0:
                    v.tensor_reduce(nu_st[:, par * 16 + h:par * 16 + h + 1],
                                    pf_b[:, par * 512:par * 512 + (k1 - k0) * 128],
                                    axis=AX.X, op=OP.max)
                else:
                    v.tensor_reduce(mxD[:, :],
                                    pf_b[:, par * 512:par * 512 + (k1 - k0) * 128],
                                    axis=AX.X, op=OP.max)
                    v.drain()
                    v.tensor_max(nu_st[:, par * 16 + h:par * 16 + h + 1],
                                 nu_st[:, par * 16 + h:par * 16 + h + 1],
                                 mxD[:, :])
            cx.t_vmax[i] = p.step("v", [cx.t_r1[i]], vmax, f"D_vm_{gq}_{h}_{g}")
            if i == cx.nidx - 1:
                last_slot_tok[par] = cx.t_vmax[i]

        def emit_ctail(cx):
            gq, par = cx.gq, cx.par
            pc = prev_ct[par]
            ct2 = pf_b[0:16, 1056 + par * 128:1184 + par * 128]
            ct4 = pf_b[0:1, 1312 + par * 16:1328 + par * 16]
            ct6 = pf_b[:, 1344 + par * 16:1360 + par * 16]
            def c_t1(v, par=par):
                v.tensor_copy(nu_bf[:, par * 16:(par + 1) * 16],
                              nu_st[:, par * 16:(par + 1) * 16])
            t_ct1 = p.step("v", [cx.t_vmax[cx.nidx - 1]], c_t1, f"D_ct1_{gq}")
            def c_t2(t, par=par, ct2=ct2):
                t.matmul(ct2, nu_bf[:, par * 16:(par + 1) * 16], ident[:, :],
                         start=True, stop=True)
            d2 = [t_ct1, cx.t_vmax[cx.nidx - 1]]
            if "t3" in pc:
                d2.append(pc["t3"])
            t_ct2 = p.step("t", d2, c_t2, f"D_ct2_{gq}")
            def c_t3(v, par=par, ct2=ct2):
                v.tensor_reduce(negc_col[:, par:par + 1], ct2, axis=AX.X,
                                op=OP.max, negate=True)
                v.drain()
                v.tensor_copy(negc_bf[:, par:par + 1], negc_col[:, par:par + 1])
            t_ct3 = p.step("v", [t_ct2], c_t3, f"D_ct3_{gq}")
            pc["t3"] = t_ct3
            def c_t4(t, par=par, ct4=ct4):
                t.matmul(ct4, negc_bf[:, par:par + 1], ident[0:16, 0:16],
                         start=True, stop=True)
            d4 = [t_ct3] + ([pc["t5"]] if "t5" in pc else [])
            t_ct4 = p.step("t", d4, c_t4, f"D_ct4_{gq}")
            def c_t5(v, par=par, ct4=ct4):
                v.tensor_copy(negrow_bf[:, par * 16:(par + 1) * 16], ct4)
            t_ct5 = p.step("v", [t_ct4], c_t5, f"D_ct5_{gq}")
            pc["t5"] = t_ct5
            def c_t6(t, par=par, ct6=ct6):
                t.matmul(ct6, ones_row[:, :],
                         negrow_bf[:, par * 16:(par + 1) * 16],
                         start=True, stop=True)
            d6 = [t_ct5] + ([pc["t7"]] if "t7" in pc else [])
            t_ct6 = p.step("t", d6, c_t6, f"D_ct6_{gq}")
            def c_t7(v, par=par, ct6=ct6):
                v.tensor_copy(negC_cols[:, par * 16:(par + 1) * 16], ct6)
            t_ct7 = p.step("v", [t_ct6], c_t7, f"D_ct7_{gq}")
            pc["t7"] = t_ct7
            cx.t_ct7 = t_ct7

        def emit_r2exp(cx, i):
            gq, par, b_, qt = cx.gq, cx.par, cx.b, cx.qt
            h, g = divmod(i, cx.ng)
            k0 = g * 4
            k1 = min(qt + 1, k0 + 4)
            es = (par * 2 + (i % 2)) * 512
            def r2(t, h=h, k0=k0, k1=k1, par=par, b_=b_, qt=qt):
                tp = (32 * (h % 4), 0) if h % 4 == 3 else None
                for kt in range(k0, k1):
                    t.matmul(pf_b[:, par * 512 + (kt - k0) * 128:
                                  par * 512 + (kt - k0 + 1) * 128],
                             kl_sl(b_, h, kt), qf_sl(b_, h, qt),
                             start=True, stop=True, tile_position=tp)
            deps = [cx.t_ct7]
            if i > 0:
                deps.append(cx.t_exp[i - 1])
            cx.t_r2[i] = p.step("t", deps, r2, f"D_r2_{gq}_{h}_{g}")
            def expf(sc, h=h, k0=k0, k1=k1, par=par, es=es):
                sc.activation(E_bf[:, es:es + (k1 - k0) * 128],
                              pf_b[:, par * 512:par * 512 + (k1 - k0) * 128],
                              AF.Exp, bias=negC_cols[:, par * 16 + h:par * 16 + h + 1],
                              scale=1.0)
            deps_e = [cx.t_r2[i], cx.t_ct7]
            if i > 1:
                deps_e.append(cx.t_av[i - 2])
            elif last_Eslot_tok[par][i % 2] is not None:
                deps_e.append(last_Eslot_tok[par][i % 2])
            cx.t_exp[i] = p.step("s", deps_e, expf, f"D_ex_{gq}_{h}_{g}")
            cx.tok_e[i] = cx.t_exp[i]
            if k1 - 1 == qt:
                def mskf(v, k0=k0, k1=k1, es=es):
                    sl = E_bf[:, es + (k1 - 1 - k0) * 128:es + (k1 - k0) * 128]
                    v.tensor_mul(sl, sl, maskCa[:, :])
                cx.tok_e[i] = p.step("v", [cx.t_exp[i]], mskf, f"D_msk_{gq}_{h}")

        def emit_av(cx, i):
            gq, par, b_, qt = cx.gq, cx.par, cx.b, cx.qt
            h, g = divmod(i, cx.ng)
            k0 = g * 4
            k1 = min(qt + 1, k0 + 4)
            es = (par * 2 + (i % 2)) * 512
            def avf(t, h=h, k0=k0, k1=k1, es=es, b_=b_, qt=qt):
                for kt in range(k0, k1):
                    t.matmul(pf_a[0:33, h * 128:(h + 1) * 128],
                             vI_sl(b_, kt, h),
                             E_bf[:, es + (kt - k0) * 128:es + (kt - k0 + 1) * 128],
                             start=(kt == 0), stop=(kt == qt))
            deps_a = [cx.tok_e[i]] + (cx.zro + [tD_pe] if i == 0 else [])
            cx.t_av[i] = p.step("t", deps_a, avf, f"D_av_{gq}_{h}_{g}")
            if i == cx.nidx - 1:
                last_slot_tok[par] = cx.t_exp[i]
                last_Eslot_tok[par] = [cx.t_av[i - 1] if i > 0 else None,
                                       cx.t_av[i]] if i % 2 == 1 else \
                                      [cx.t_av[i],
                                       cx.t_av[i - 1] if i > 0 else None]

        def emit_tail(cx):
            nonlocal t_prev_pbfD, t_prev_cproj, t_prev_zro, t_ptdma, d_pt_prev
            gq, par, row0 = cx.gq, cx.par, cx.row0
            zt = pf_b[:, 1376 + par * 16:1392 + par * 16]
            t_last_av = cx.t_av[cx.nidx - 1]
            def d_ave(sc):
                sc.copy(AVZ_bf[:, :], pf_a[0:33, :])
                sc.drain()
                sc.copy(zrow_t[32:33, :], pf_a[32:33, :])
            t_ave = p.step("s", [t_last_av], d_ave, f"D_ave_{gq}")
            def d_zt(t, zt=zt):
                for h in range(16):
                    t.matmul(zt[:, h:h + 1],
                             zrow_t[32:33, h * 128:(h + 1) * 128],
                             onef33[32:33, :], start=True, stop=True)
            dz = [t_ave, t_last_av, tD_pr]
            if "zi" in prev_ct[par]:
                dz.append(prev_ct[par]["zi"])
            t_zt = p.step("t", dz, d_zt, f"D_zt_{gq}")
            def d_iz(v, zt=zt):
                v.reciprocal(invZ_c[:, :], zt)
            t_iz = p.step("v", [t_zt], d_iz, f"D_iz_{gq}")
            prev_ct[par]["zi"] = t_iz
            def d_l2o(t):
                for h in range(16):
                    t.matmul(pf_a[:, 1024 + h * 64:1024 + (h + 1) * 64],
                             AVZ_bf[:, h * 128:(h + 1) * 128], l2oI[:, :],
                             start=True, stop=True)
            t_l2o = p.step("t", [t_ave, t_iz], d_l2o, f"D_l2o_{gq}")
            def d_ye(v):
                for h in range(16):
                    v.tensor_scalar(y_sb[:, h * 64:(h + 1) * 64],
                                    pf_a[:, 1024 + h * 64:1024 + (h + 1) * 64],
                                    invZ_c[:, h:h + 1], None, OP.mult)
            t_ye = p.step("v", [t_l2o, t_iz], d_ye, f"D_ye_{gq}")
            def d_ytr(t):
                for ic in range(8):
                    t.transpose(pbf[:, ic * 128:(ic + 1) * 128],
                                y_sb[:, ic * 128:(ic + 1) * 128], ident[:, :])
            t_ytr = p.step("t", [t_ye, t_prev_pbfD], d_ytr, f"D_ytr_{gq}")
            def d_ytre(sc):
                sc.copy(yT_bf[:, :], pbf[:, :])
            t_ytre = p.step("s", [t_ytr], d_ytre, f"D_ytre_{gq}")
            t_prev_pbfD = t_ytre
            def d_cproj(t):
                for hh in range(2):
                    for ic in range(8):
                        t.matmul(pf_a[:, hh * 512:(hh + 1) * 512],
                                 yT_bf[:, ic * 128:(ic + 1) * 128],
                                 cp_sb[:, ic * C + hh * 512:ic * C + (hh + 1) * 512],
                                 start=(ic == 0), stop=(ic == 7))
            t_cproj = p.step("t", [t_ytre, dD_w, t_ave], d_cproj, f"D_cp_{gq}")
            t_prev_cproj = t_cproj
            def d_pt(v, par=par):
                v.tensor_add(pt_f[:, :], x1c[:, par * C:(par + 1) * C],
                             pf_a[:, 0:1024])
                v.drain()
                v.tensor_add(pt_f[:, :], pt_f[:, :], cpb_bc[:, :])
            t_pt = p.step("v", [t_cproj, dx1_toks[gq], tD_pe] + d_pt_prev, d_pt,
                          f"D_pt_{gq}")
            t_prev_zro = [t_pt]
            ptdeps = [t_pt] + ([t_ptdma] if t_ptdma else [])
            t_ptdma, d_ptd = p.dma_step("g", [(pt_s[row0:row0 + 128, :],
                                               pt_f[:, :])], ptdeps, "out",
                                        f"D_ptd_{gq}")
            d_pt_prev = [d_ptd]
            if dbg_on("pt"):
                _, d_ptdbg = p.dma_step("g", [(pt_dbg[row0:row0 + 128, :],
                                               pt_f[:, :])],
                                        [t_pt, t_ptdma], "dbg", f"D_ptdbg_{gq}")
                d_pt_prev.append(d_ptdbg)

        prev_cx = None
        for gq in range(NGQ + 1):
            cx = None
            if gq < NGQ:
                cx = DCtx()
                cx.gq = gq
                cx.b = gq // NPOS
                cx.qt = gq % NPOS
                cx.par = gq % 2
                cx.row0 = cx.b * T + cx.qt * 128
                cx.ng = (cx.qt + 1 + 3) // 4
                cx.nidx = 16 * cx.ng
                cx.t_r1, cx.t_vmax = {}, {}
                cx.t_r2, cx.t_exp, cx.t_av, cx.tok_e = {}, {}, {}, {}
                if gq == 0:
                    t_x1last, dx1_toks[0] = p.dma_step(
                        "y", [(x1c[:, 0:C], x1_s[cx.row0:cx.row0 + 128, :])],
                        [pC_done["y"]], "xin", "D_x0")
            n1 = cx.nidx if cx else 0
            n2 = prev_cx.nidx if prev_cx else 0
            if prev_cx:
                prev_cx.zro = t_prev_zro
            for i in range(max(n1, n2)):
                if prev_cx and i < n2:
                    emit_r2exp(prev_cx, i)
                if cx and i < n1:
                    emit_round1(cx, i)
                if prev_cx and i < n2:
                    emit_av(prev_cx, i)
            if cx:
                emit_ctail(cx)
            if prev_cx:
                emit_tail(prev_cx)
            if cx and gq + 1 < NGQ:
                nb, nqt = (gq + 1) // NPOS, (gq + 1) % NPOS
                nrow = nb * T + nqt * 128
                deps = [t_x1last] + d_pt_prev
                t_x1last, dx1_toks[gq + 1] = p.dma_step(
                    "y", [(x1c[:, ((gq + 1) % 2) * C:((gq + 1) % 2 + 1) * C],
                           x1_s[nrow:nrow + 128, :])], deps, "xin",
                    f"D_x{gq+1}")
            prev_cx = cx

        pD_done = p.barrier("pD_done")
        arenaD.close()
        arenaCD.close()

    if "E" in phases:
        arenaE, AE = mkarena()
        wgq_sb = AE("wgq_sb", [128, 8 * ISW], BF16)
        wuq_sb = AE("wuq_sb", [128, 8 * ISW], BF16)
        xstgE = AE("xstgE", [128, 4096], F32)
        Xq_bf = AE("Xq_bf", [128, 4096], BF16)
        silu_g = AE("silu_g", [128, 512], F32)
        up_f = AE("up_f", [128, 512], F32)
        hbuf = AE("hbuf", [128, 2 * 512], F32)
        sgc = AE("sgc", [128, 22], F32)
        suc = AE("suc", [128, 22], F32)
        hmx_col = AE("hmx_col", [128, 1], F32)
        mxE = AE("mxE", [128, 1], F32)
        sgE = AE("sgE", [1, 1], F32)

        tE_w, dE_w = p.dma_step("g", [
            (wgq_sb[:, hc * ISW:(hc + 1) * ISW], wgq_s[hc * 128:(hc + 1) * 128, :])
            for hc in range(8)] + [
            (wuq_sb[:, hc * ISW:(hc + 1) * ISW], wuq_s[hc * 128:(hc + 1) * 128, :])
            for hc in range(8)], [pD_done["g"]], "wld", "E_w")

        def e_sg(sc):
            sc.activation(sgE[:, :], clipx_r[:, :], AF.Sigmoid, bias=0.0, scale=1.0)
        tE_sg = p.step("s", [pD_done["s"], dcc1b], e_sg, "E_sg")
        def e_sx(v):
            v.tensor_mul(sgE[:, :], sgE[:, :], gmax_x[:, :])
            v.drain()
            v.tensor_scalar_mul(sgE[:, :], sgE[:, :], 1.0 / QMAX)
        tE_sx = p.step("v", [tE_sg, dcc1b, pD_done["v"]], e_sx, "E_sx")
        def e_sxb(t):
            t.matmul(pf_b[:, 0:1], onesf_row[:, :], sgE[:, :], start=True, stop=True)
        tE_sxb = p.step("t", [tE_sx, pD_done["t"]], e_sxb, "E_sxb")
        def e_sxc(v):
            v.tensor_copy(sx_col[:, :], pf_b[:, 0:1])
            v.drain()
            v.reciprocal(rsx_col[:, :], sx_col[:, :])
            v.drain()
            v.tensor_scalar(sgc[:, :], wsc_g[:, :], sx_col[:, 0:1], None, OP.mult)
            v.tensor_scalar(suc[:, :], wsc_u[:, :], sx_col[:, 0:1], None, OP.mult)
            v.memset(hmx_col[:, :], 0.0)
        tE_sxc = p.step("v", [tE_sxb], e_sxc, "E_sxc")

        WE = min(512, NT)
        NTT5 = NT // WE
        d_inE_prev = []
        t_hdma = None
        d_h_prev_by = {}
        t_mxE = tE_sxc
        t_prev_g = None
        t_prev_u = None
        for tts in range(NTT5):
            c0 = tts * WE
            t_in, d_in = p.dma_step("y", [
                (xstgE[:, hc * WE:(hc + 1) * WE],
                 xnT_s[hc * 128:(hc + 1) * 128, c0:c0 + WE]) for hc in range(8)],
                [pD_done["y"]] + d_inE_prev, "xin", f"E_in{tts}")
            def e_q(v):
                v.tensor_scalar(xstgE[:, 0:8 * WE], xstgE[:, 0:8 * WE], rsx_col[:, 0:1], MR,
                                OP.mult, OP.add)
                v.drain()
                v.tensor_scalar(xstgE[:, 0:8 * WE], xstgE[:, 0:8 * WE], -MR, -(QMAX + 1.0),
                                OP.add, OP.max)
                v.drain()
                v.tensor_scalar(Xq_bf[:, 0:8 * WE], xstgE[:, 0:8 * WE], QMAX, None, OP.min)
            t_q = p.step("v", [d_in, tE_sxc], e_q, f"E_q{tts}")
            d_inE_prev = [DmaTok(())]
            d_inE_prev = []

            t_gmm = {}
            t_umm = {}
            t_sil = {}
            t_upe = {}
            t_hop = {}
            for ic in range(22):
                def gmm(t, ic=ic):
                    for hc in range(8):
                        t.matmul(pf_a[:, (ic % 2) * 512:(ic % 2) * 512 + WE],
                                 wgq_sb[:, hc * ISW + ic * 128:hc * ISW + (ic + 1) * 128],
                                 Xq_bf[:, hc * WE:(hc + 1) * WE],
                                 start=(hc == 0), stop=(hc == 7))
                deps = [t_q, dE_w]
                if ic >= 2:
                    deps.append(t_sil[ic - 2])
                if ic == 0 and t_prev_g:
                    deps.append(t_prev_g)
                    deps.append(t_prev_u)
                if ic == 1 and t_prev_g:
                    deps.append(t_prev_g)
                    deps.append(t_prev_u)
                t_gmm[ic] = p.step("t", deps, gmm, f"E_g{tts}_{ic}")
                def sil(sc, ic=ic):
                    sc.activation(silu_g[:, 0:WE], pf_a[:, (ic % 2) * 512:(ic % 2) * 512 + WE],
                                  AF.Silu, bias=0.0, scale=sgc[:, ic:ic + 1])
                t_sil[ic] = p.step("s", [t_gmm[ic], tE_sxc] +
                                   ([t_hop[ic - 1]] if ic >= 1 else []),
                                   sil, f"E_s{tts}_{ic}")
                def umm(t, ic=ic):
                    for hc in range(8):
                        t.matmul(pf_a[:, 1024 + (ic % 2) * 512:1024 + (ic % 2) * 512 + WE],
                                 wuq_sb[:, hc * ISW + ic * 128:hc * ISW + (ic + 1) * 128],
                                 Xq_bf[:, hc * WE:(hc + 1) * WE],
                                 start=(hc == 0), stop=(hc == 7))
                deps_u = [t_gmm[ic]]
                if ic >= 2:
                    deps_u.append(t_hop[ic - 2])
                t_umm[ic] = p.step("t", deps_u, umm, f"E_u{tts}_{ic}")
                def upe(v, ic=ic):
                    v.tensor_scalar(up_f[:, 0:WE], pf_a[:, 1024 + (ic % 2) * 512:
                                                        1024 + (ic % 2) * 512 + WE],
                                    suc[:, ic:ic + 1], None, OP.mult)
                deps_ue = [t_umm[ic], tE_sxc] + ([t_hop[ic - 1]] if ic >= 1 else [])
                t_upe[ic] = p.step("v", deps_ue, upe, f"E_ue{tts}_{ic}")
                def hop(v, ic=ic):
                    v.tensor_mul(hbuf[:, (ic % 2) * 512:(ic % 2) * 512 + WE],
                                 silu_g[:, 0:WE], up_f[:, 0:WE])
                    v.drain()
                    v.tensor_reduce(mxE[:, :], hbuf[:, (ic % 2) * 512:(ic % 2) * 512 + WE],
                                    axis=AX.X, op=OP.max, apply_absolute_value=True)
                    v.drain()
                    v.tensor_max(hmx_col[:, :], hmx_col[:, :], mxE[:, :])
                deps_h = [t_upe[ic], t_sil[ic], t_mxE]
                dkey = (tts, ic - 2) if ic >= 2 else (tts - 1, 20 + ic)
                if dkey in d_h_prev_by:
                    deps_h.append(d_h_prev_by[dkey])
                t_hop[ic] = p.step("v", deps_h, hop, f"E_h{tts}_{ic}")
                t_mxE = t_hop[ic]
                hdeps = [t_hop[ic]] + ([t_hdma] if t_hdma else [])
                t_hdma, d_hd = p.dma_step("g", [
                    (h_s[ic * 128:(ic + 1) * 128, c0:c0 + WE],
                     hbuf[:, (ic % 2) * 512:(ic % 2) * 512 + WE])], hdeps,
                    "out", f"E_hd{tts}_{ic}")
                d_h_prev_by[(tts, ic)] = d_hd
            t_prev_g = t_sil[21]
            t_prev_u = t_upe[21]
            d_inE_prev = [d_h_prev_by[(tts, 21)]]

        def e_mxmm(t):
            t.matmul(pf_b[0:1, 2:130], hmx_col[:, :], identf[:, :],
                     start=True, stop=True)
        t_em = p.step("t", [t_mxE, t_prev_g, t_prev_u], e_mxmm, "E_mxmm")
        def e_mxv(v):
            v.tensor_reduce(gmax_h[:, :], pf_b[0:1, 2:130], axis=AX.X, op=OP.max)
        t_ev2 = p.step("v", [t_em], e_mxv, "E_mxv")
        tcc2, dcc2_out = p.dma_step("g", [(cchi.ap()[0:1, 0:1], gmax_h[:, :])],
                                    [t_ev2], "ccio", "E_ccout")
        if cfg.get("skip_cc"):
            def cc2(g):
                g.dma_start(out=ccho.ap()[:, :], in_=cchi.ap()[:, :],
                            ).then_inc(cc_sem, 16)
        else:
            def cc2(g):
                g.collective_compute(
                    "AllReduce", OP.max, replica_groups=RG,
                    ins=[cchi.ap().opt()], outs=[ccho.ap().opt()],
                ).then_inc(cc_sem, 1)
        t_cc2 = p.step("g", [tcc2, dcc2_out], cc2, "E_cc2")
        p.dsems["cc0"][1] += cc_inc
        dcc2 = DmaTok((("cc0", 2 * cc_inc),))
        tcc2b, dcc2b = p.dma_step("g", [(gmax_h[:, :], ccho.ap()[0:1, 0:1])],
                                  [t_cc2, dcc2], "ccio", "E_ccin")

        pE_done = p.barrier("pE_done")
        arenaE.close()

    if "F" in phases:
        arenaF, AF_ = mkarena()
        wdq_sb = AF_("wdq_sb", [128, 22 * C], BF16)
        hstgF = AF_("hstgF", [128, 2 * 2816], F32)
        Hq_bf = AF_("Hq_bf", [128, 2 * 2816], BF16)
        ptF = AF_("ptF", [128, 2 * C], F32)
        otF = AF_("otF", [128, 2 * C], F32)
        swd_bc = AF_("swd_bc", [128, C], F32)
        swrow = AF_("swrow", [1, C], F32)
        shE = AF_("shE", [1, 1], F32)

        tF_w, dF_w = p.dma_step("g", [
            (wdq_sb[:, ic * C:(ic + 1) * C], wdq_s[ic * 128:(ic + 1) * 128, :])
            for ic in range(22)], [pE_done["g"]], "wld", "F_w")

        def f_sg(sc):
            sc.activation(shE[:, :], clipd_r[:, :], AF.Sigmoid, bias=0.0, scale=1.0)
        tF_sg = p.step("s", [pE_done["s"], dcc2b], f_sg, "F_sg")
        def f_sx(v):
            v.tensor_mul(shE[:, :], shE[:, :], gmax_h[:, :])
            v.drain()
            v.tensor_scalar_mul(shE[:, :], shE[:, :], 1.0 / QMAX)
        tF_sx = p.step("v", [tF_sg, dcc2b, pE_done["v"]], f_sx, "F_sx")
        def f_sxb(t):
            t.matmul(pf_b[:, 1024:1025], onesf_row[:, :], shE[:, :],
                     start=True, stop=True)
            for hc in range(8):
                t.matmul(pf_b[0:1, hc * 128:(hc + 1) * 128],
                         wsc_d[:, hc:hc + 1], identf[:, :], start=True, stop=True)
        tF_sxb = p.step("t", [tF_sx, pE_done["t"]], f_sxb, "F_sxb")
        def f_sxc(v):
            v.tensor_copy(sh_col[:, :], pf_b[:, 1024:1025])
            v.drain()
            v.reciprocal(rsh_col[:, :], sh_col[:, :])
            v.tensor_scalar(swrow[:, :], pf_b[0:1, 0:1024], sh_col[0:1, 0:1],
                            None, OP.mult)
        tF_sxc = p.step("v", [tF_sxb], f_sxc, "F_sxc")
        def f_swb(t):
            for hh in range(2):
                t.matmul(pf_a[:, hh * 512:(hh + 1) * 512], onesf_row[:, :],
                         swrow[:, hh * 512:(hh + 1) * 512], start=True, stop=True)
        tF_swb = p.step("t", [tF_sxc], f_swb, "F_swb")
        def f_swe(v):
            v.tensor_copy(swd_bc[:, :], pf_a[:, 0:1024])
        tF_swe = p.step("v", [tF_swb], f_swe, "F_swe")

        t_ydma = None
        t_inF = [None, None]
        prev_q = [None, None]
        prev_o = [None, None]
        prev_dy = [[], []]
        for tc in range(NT // 128):
            bb = tc % 2
            hst = hstgF[:, bb * 2816:(bb + 1) * 2816]
            hq = Hq_bf[:, bb * 2816:(bb + 1) * 2816]
            ptv = ptF[:, bb * C:(bb + 1) * C]
            otv = otF[:, bb * C:(bb + 1) * C]
            psl = pf_a[:, bb * 1024:(bb + 1) * 1024]
            in_deps = [pE_done["y"]]
            for tok in (t_inF[bb], prev_q[bb], prev_o[bb]):
                if tok is not None:
                    in_deps.append(tok)
            t_in, d_in = p.dma_step("y", [
                (hst[:, ic * 128:(ic + 1) * 128],
                 h_s[ic * 128:(ic + 1) * 128, tc * 128:(tc + 1) * 128])
                for ic in range(22)] + [
                (ptv, pt_s[tc * 128:(tc + 1) * 128, :])],
                in_deps, "xin", f"F_in{tc}")
            t_inF[bb] = t_in
            def f_q(v, hst=hst, hq=hq):
                v.tensor_scalar(hst, hst, rsh_col[:, 0:1], MR, OP.mult, OP.add)
                v.drain()
                v.tensor_scalar(hst, hst, -MR, -(QMAX + 1.0), OP.add, OP.max)
                v.drain()
                v.tensor_scalar(hq, hst, QMAX, None, OP.min)
            t_q = p.step("v", [d_in, tF_sxc, tF_swe], f_q, f"F_q{tc}")
            prev_q[bb] = t_q
            def f_mm(t, hq=hq, psl=psl):
                for hh in range(2):
                    for ic in range(22):
                        t.matmul(psl[:, hh * 512:(hh + 1) * 512],
                                 hq[:, ic * 128:(ic + 1) * 128],
                                 wdq_sb[:, ic * C + hh * 512:ic * C + (hh + 1) * 512],
                                 start=(ic == 0), stop=(ic == 21))
            deps_m = [t_q, dF_w, tF_swe]
            if prev_o[bb] is not None:
                deps_m.append(prev_o[bb])
            t_mm = p.step("t", deps_m, f_mm, f"F_mm{tc}")
            def f_out(v, otv=otv, ptv=ptv, psl=psl):
                v.tensor_mul(otv, psl, swd_bc[:, :])
                v.drain()
                v.tensor_add(otv, otv, ptv)
            t_o = p.step("v", [t_mm] + prev_dy[bb], f_out, f"F_o{tc}")
            prev_o[bb] = t_o
            ydeps = [t_o] + ([t_ydma] if t_ydma else [])
            t_ydma, d_yd = p.dma_step("g", [(y_out[tc * 128:(tc + 1) * 128, :],
                                             otv)], ydeps, "out", f"F_y{tc}")
            prev_dy[bb] = [d_yd]
    final = []
    for nm_ in ("out_g", "out_y", "dbg_g"):
        if nm_ in p.dlanes:
            waits = tuple((k, p.dsems[k][1]) for k in p.dlanes[nm_]["keys"]
                          if p.dsems[k][1] > 0)
            final.append(("g", DmaTok(waits)))
    p.finalize(final)
    return nc, dbg_outs



def shard_inputs(inputs, n_cores=8):
    x = np.ascontiguousarray(inputs["L0_attn_x"])
    Bfull, T, C = x.shape
    Bper = Bfull // n_cores
    asc = np.ascontiguousarray

    def r1(a):
        return asc(np.asarray(a, dtype=np.float32).reshape(1, -1))

    common = dict(
        wqT=asc(np.asarray(inputs["L0_attn_wq"]).T),
        wkT=asc(np.asarray(inputs["L0_attn_wk"]).T),
        wvT=asc(np.asarray(inputs["L0_attn_wv"]).T),
        woT=asc(np.asarray(inputs["L0_attn_wo"]).T),
        ln1w=r1(inputs["L0_attn_ln1_w"]), ln1b=r1(inputs["L0_attn_ln1_b"]),
        ln2w=r1(inputs["L0_attn_ln2_w"]), ln2b=r1(inputs["L0_attn_ln2_b"]),
        lr=r1(inputs["L0_attn_lr_scale"]),
        w1T=asc(np.asarray(inputs["L0_attn_w1"]).T), b1=r1(inputs["L0_attn_b1"]),
        w2T=asc(np.asarray(inputs["L0_attn_w2"]).T), b2=r1(inputs["L0_attn_b2"]),
        fWin=asc(np.asarray(inputs["L0_ffn_Win"])), fbin=r1(inputs["L0_ffn_bin_"]),
        fW1=r1(inputs["L0_ffn_W1"]), fb1=r1(inputs["L0_ffn_b1"]),
        fW2=r1(inputs["L0_ffn_W2"]), fb2=r1(inputs["L0_ffn_b2"]),
        fWout=asc(np.asarray(inputs["L0_ffn_Wout"])), fbout=r1(inputs["L0_ffn_bout"]),
        caT=asc(np.asarray(inputs["L1_attn_c_attn_w"]).T), cab=r1(inputs["L1_attn_c_attn_b"]),
        q2lT=asc(np.asarray(inputs["L1_attn_q2l_w"]).T), q2lb=r1(inputs["L1_attn_q2l_b"]),
        kv2lT=asc(np.asarray(inputs["L1_attn_kv2l_w"]).T), kv2lb=r1(inputs["L1_attn_kv2l_b"]),
        l2oT=asc(np.asarray(inputs["L1_attn_l2o_w"]).T), l2ob=r1(inputs["L1_attn_l2o_b"]),
        wqkT=asc(np.asarray(inputs["L1_attn_wqk_w"]).T), wqkb=r1(inputs["L1_attn_wqk_b"]),
        cpT=asc(np.asarray(inputs["L1_attn_cproj_w"]).T), cpb=r1(inputs["L1_attn_cproj_b"]),
        clipx=r1(inputs["L1_ffn_clip_x"]), clipd=r1(inputs["L1_ffn_clip_down"]),
        wg=asc(np.asarray(inputs["L1_ffn_w_gate"])),
        wu=asc(np.asarray(inputs["L1_ffn_w_up"])),
        wd=asc(np.asarray(inputs["L1_ffn_w_down"])),
    )
    in_maps = []
    for c in range(n_cores):
        m = dict(common)
        m["x"] = asc(x[c * Bper:(c + 1) * Bper].reshape(Bper * T, C))
        in_maps.append(m)
    return in_maps, (Bper, T, C)


def kernel(**inputs):
    import os
    n_cores = 8
    in_maps, (Bper, T, C) = shard_inputs(inputs, n_cores)
    cfg = dict(T=T, B=Bper, n_cores=n_cores, debug=False,
               phases=os.environ.get("KPHASES", "ABCDEF"),
               skip_cc=bool(os.environ.get("KSKIPCC")))
    nc, _ = build_program(cfg)
    res = run_bass_kernel_spmd(nc, in_maps, core_ids=list(range(n_cores)))
    outs = [r["y"].reshape(Bper, T, C) for r in res.results]
    return np.concatenate(outs, axis=0)


# version 17
# speedup vs baseline: 1.0521x; 1.0050x over previous
import math
from contextlib import ExitStack
import numpy as np

import concourse.bass as bass
import concourse.mybir as mybir
from concourse.alu_op_type import AluOpType as OP
from concourse.bass_utils import run_bass_kernel_spmd

F32 = mybir.dt.float32
BF16 = mybir.dt.bfloat16
AF = mybir.ActivationFunctionType
AX = mybir.AxisListType

MR = 1.5 * 2.0**23
LN_EPS = 1e-5
INNER_LR = 0.01
QMAX = 127.0

ENGS = ("t", "v", "s", "g", "y")


class DmaTok:
    __slots__ = ("waits",)
    def __init__(self, waits):
        self.waits = waits


class StepTok:
    __slots__ = ("eng", "cnt")
    def __init__(self, eng, cnt):
        self.eng, self.cnt = eng, cnt


class Step:
    __slots__ = ("eng", "deps", "fn", "tag")
    def __init__(self, eng, deps, fn, tag):
        self.eng, self.deps, self.fn, self.tag = eng, deps, fn, tag


class Prog:

    def __init__(self, nc):
        self.nc = nc
        self.eng = {"t": nc.tensor, "v": nc.vector, "s": nc.scalar,
                    "g": nc.gpsimd, "y": nc.sync}
        self.ssem = {e: nc.alloc_semaphore(f"step_{e}") for e in ENGS}
        self.scnt = {e: 0 for e in ENGS}
        self.steps = []
        self.dsems = {}
        self.dlanes = {}

    N_LANES = 4

    def dma_lanes(self, name):
        if name not in self.dlanes:
            self.dlanes[name] = {
                "keys": [], "rr": 0,
            }
            for i in range(self.N_LANES):
                key = f"{name}{i}"
                self.dsems[key] = [self.nc.alloc_semaphore(f"dma_{key}"), 0]
                self.dlanes[name]["keys"].append(key)
        return self.dlanes[name]

    def step(self, eng, deps, fn, tag=""):
        for d in deps:
            assert d is not None, f"None dep in {tag}"
            if isinstance(d, StepTok):
                assert d.cnt <= self.scnt[d.eng], f"forward dep in {tag}"
        self.scnt[eng] += 1
        tok = StepTok(eng, self.scnt[eng])
        self.steps.append(Step(eng, list(deps), fn, tag))
        return tok

    def dma_step(self, qeng, transfers, deps, sem_name, tag=""):
        sem_name = f"{sem_name}_{qeng}"
        lanes = self.dma_lanes(sem_name)
        plan = []
        waits = {}
        for _ in transfers:
            key = lanes["keys"][lanes["rr"] % self.N_LANES]
            lanes["rr"] += 1
            rec = self.dsems[key]
            plan.append((rec[0], rec[1]))
            rec[1] += 16
            waits[key] = rec[1]
        tlist = list(transfers)
        def fn(e):
            for (o, i_), (sem, prev) in zip(tlist, plan):
                if prev > 0:
                    e.wait_ge(sem, prev)
                e.dma_start(out=o, in_=i_).then_inc(sem, 16)
        tok = self.step(qeng, deps, fn, tag)
        return tok, DmaTok(tuple(waits.items()))

    def tok(self, eng):
        return StepTok(eng, self.scnt[eng])

    def barrier(self, tag="bar"):
        toks = {e: self.tok(e) for e in ENGS}
        dwaits = tuple((k, rec[1]) for k, rec in self.dsems.items() if rec[1] > 0)
        dtok = DmaTok(dwaits)
        out = {}
        for e in ENGS:
            deps = [toks[o] for o in ENGS if o != e] + [dtok]
            out[e] = self.step(e, deps, lambda _e: None, tag=tag)
        return out

    def finalize(self, final_waits):
        nc = self.nc
        per_eng = {e: [] for e in ENGS}
        for st in self.steps:
            per_eng[st.eng].append(st)
        hwm = {e: {} for e in ENGS}

        def emit_for(ename):
            e = self.eng[ename]
            marks = hwm[ename]
            own = 0
            for st in per_eng[ename]:
                if own > 0:
                    e.wait_ge(self.ssem[ename], own)
                for d in st.deps:
                    if isinstance(d, StepTok):
                        if d.eng == ename:
                            continue
                        pairs = [(self.ssem[d.eng], ("s", d.eng), d.cnt)]
                    else:
                        pairs = [(self.dsems[k][0], ("d", k), v)
                                 for (k, v) in d.waits]
                    for sem, key, val in pairs:
                        if marks.get(key, -1) < val:
                            e.wait_ge(sem, val)
                            marks[key] = val
                st.fn(e)
                own += 1
                e.drain()
                e.sem_inc(self.ssem[ename], 1)
            for (feng, dtok) in final_waits:
                if feng == ename:
                    for (k, v) in dtok.waits:
                        e.wait_ge(self.dsems[k][0], v)

        with nc.Block() as block:
            block.tensor(lambda e: emit_for("t"))
            block.vector(lambda e: emit_for("v"))
            block.scalar(lambda e: emit_for("s"))
            block.gpsimd(lambda e: emit_for("g"))
            block.sync(lambda e: emit_for("y"))



def build_program(cfg):
    T = cfg["T"]
    B = cfg["B"]
    NPOS = T // 128
    NT = B * T
    C, H, D, L = 1024, 16, 64, 32
    ISW = 2816
    n_cores = cfg["n_cores"]
    RG = [list(range(n_cores))]
    dbg = cfg.get("debug", False)
    phases = cfg.get("phases", "ABCDEF")

    nc = bass.Bass("TRN2", target_bir_lowering=False,
                   dynamic_dma_scratch_size=16384)

    def inp(name, shape, dt=F32):
        return nc.dram_tensor(name, list(shape), dt, kind="ExternalInput").ap()

    x_in = inp("x", (NT, C))
    wqT = inp("wqT", (C, C)); wkT = inp("wkT", (C, C)); wvT = inp("wvT", (C, C))
    woT = inp("woT", (C, C))
    ln1w = inp("ln1w", (1, C)); ln1b = inp("ln1b", (1, C))
    ln2w = inp("ln2w", (1, C)); ln2b = inp("ln2b", (1, C))
    lr_in = inp("lr", (1, H))
    w1T = inp("w1T", (C, 4 * C)); b1_in = inp("b1", (1, 4 * C))
    w2T = inp("w2T", (4 * C, C)); b2_in = inp("b2", (1, C))
    fWin = inp("fWin", (C, D)); fbin = inp("fbin", (1, D))
    fW1 = inp("fW1", (1, 1024)); fb1 = inp("fb1", (1, 1024))
    fW2 = inp("fW2", (1, 1024)); fb2 = inp("fb2", (1, D))
    fWout = inp("fWout", (D, C)); fbout = inp("fbout", (1, C))
    caT = inp("caT", (C, 3 * C)); cab = inp("cab", (1, 3 * C))
    q2lT = inp("q2lT", (D, L)); q2lb = inp("q2lb", (1, L))
    kv2lT = inp("kv2lT", (D, L)); kv2lb = inp("kv2lb", (1, L))
    l2oT = inp("l2oT", (L, D)); l2ob = inp("l2ob", (1, D))
    wqkT = inp("wqkT", (L, L)); wqkb = inp("wqkb", (1, L))
    cpT = inp("cpT", (C, C)); cpb = inp("cpb", (1, C))
    clipx = inp("clipx", (1, 1)); clipd = inp("clipd", (1, 1))
    wg_in = inp("wg", (ISW, C)); wu_in = inp("wu", (ISW, C))
    wd_in = inp("wd", (C, ISW))

    y_out = nc.dram_tensor("y", [NT, C], F32, kind="ExternalOutput").ap()

    x1_s = nc.dram_tensor("x1_s", [NT, C], F32).ap()
    u_s = nc.dram_tensor("u_s", [NT, C], F32).ap()
    pa_s = nc.dram_tensor("pa_s", [NT, C], F32).ap()
    pt_s = nc.dram_tensor("pt_s", [NT, C], F32).ap()
    xnT_s = nc.dram_tensor("xnT_s", [C, NT], F32).ap()
    h_s = nc.dram_tensor("h_s", [ISW, NT], F32).ap()
    wgq_s = nc.dram_tensor("wgq_s", [C, ISW], BF16).ap()
    wuq_s = nc.dram_tensor("wuq_s", [C, ISW], BF16).ap()
    wdq_s = nc.dram_tensor("wdq_s", [ISW, C], BF16).ap()
    ccxi = nc.dram_tensor("ccxi", [1, 1], F32)
    ccxo = nc.dram_tensor("ccxo", [1, 1], F32)
    cchi = nc.dram_tensor("cchi", [1, 1], F32)
    ccho = nc.dram_tensor("ccho", [1, 1], F32)

    dbg_outs = {}
    def dbg_on(name):
        if dbg is True:
            return True
        return isinstance(dbg, str) and name.split("_")[0] in dbg.split(",")
    def dbg_out(name, shape):
        if dbg_on(name):
            dbg_outs[name] = nc.dram_tensor(name, list(shape), F32,
                                            kind="ExternalOutput").ap()
            return dbg_outs[name]
        return None

    p = Prog(nc)
    A = nc.alloc_sbuf_tensor

    def mkarena():
        st = ExitStack()
        def alloc(name, shape, dt):
            return st.enter_context(nc.sbuf_tensor(name, shape, dt, side="right"))
        return st, alloc

    cpool = {}
    def creg(val):
        val = float(val)
        if val not in cpool:
            t = A(f"cst{len(cpool)}", [128, 1], F32)
            cpool[val] = t
            nc.const_aps.register(F32, val, t[:, :])
        return cpool[val]
    creg(LN_EPS)
    creg(1.0 / (2.0 * math.pi))
    creg(math.pi)
    creg(-math.pi)
    creg(math.pi / 2.0)
    creg(float(L) ** -0.5)
    creg(1.0 / QMAX)
    p0_arena = ExitStack()
    def A0(name, shape, dt):
        return p0_arena.enter_context(nc.sbuf_tensor(name, shape, dt, side="right"))

    ident = A("ident", [128, 128], BF16)
    identf = A("identf", [128, 128], F32)
    ones_row = A("ones_row", [1, 128], BF16)
    onesf_row = A("onesf_row", [1, 128], F32)
    one_f = A("one_f", [1, 1], F32)
    maskLTn = A("maskLTn", [128, 128], BF16)
    maskUTn = A("maskUTn", [128, 128], BF16)
    maskUT = A("maskUT", [128, 128], BF16)
    maskCa = A("maskCa", [128, 128], BF16)
    eta_bc = A("eta_bc", [128, H], F32)

    arenaA, AA = mkarena()
    g1bc = AA("g1bc", [128, C], F32); be1bc = AA("be1bc", [128, C], F32)
    fW1bc = AA("fW1bc", [128, 1024], F32); fb1bc = AA("fb1bc", [128, 1024], F32)
    fW2bc = AA("fW2bc", [128, 1024], F32)
    ffb_bf = AA("ffb_bf", [1, C], BF16)
    fbin_bc = AA("fbin_bc", [128, D], F32)
    ffb_bc = AA("ffb_bc", [128, C], F32)

    wq_sb = AA("wq_sb", [128, 8 * C], BF16)
    wk_sb = AA("wk_sb", [128, 8 * C], BF16)
    wv_sb = AA("wv_sb", [128, 8 * C], BF16)
    wo_sb = AA("wo_sb", [64, 16 * C], BF16)
    Win_bf = AA("Win_bf", [128, 8 * D], BF16)
    Wout_bf = AA("Wout_bf", [64, C], BF16)
    xa = AA("xa", [128, 2 * C], F32)
    xn_f = AA("xn_f", [128, 2 * C], F32)
    xn_bf = AA("xn_bf", [128, 2 * C], BF16)
    h1_bf = AA("h1_bf", [128, 2 * C], BF16)
    xnT_bf = AA("xnT_bf", [128, C], BF16)
    h1T_bf = AA("h1T_bf", [128, C], BF16)
    QT_sb = AA("QT_sb", [128, C], BF16)
    KT_sb = AA("KT_sb", [128, C], BF16)
    V_sb = AA("V_sb", [128, C], F32)
    Km_sb = AA("Km_sb", [128, C], BF16)
    WTst = AA("WTst", [128, B * 512], F32)
    WTbf = AA("WTbf", [128, B * 1024], BF16)
    Bc0 = AA("Bc0", [128, 2048], BF16)
    Nc0 = AA("Nc0", [128, 2048], BF16)
    Bc1 = AA("Bc1", [128, 2048], BF16)
    Nc1 = AA("Nc1", [128, 2048], BF16)
    Hbar = AA("Hbar", [128, 2048], BF16)
    c_f = AA("c_f", [128, 1024], F32)
    c_bf = AA("c_bf", [128, 1024], BF16)
    ct_bf = c_bf
    OutT_bf = AA("OutT_bf", [64, 2048], BF16)
    u_f = AA("u_f", [128, C], F32)
    pa_f = AA("pa_f", [128, C], F32)
    s0_f = AA("s0_f", [128, D], F32)
    hf0 = AA("hf0", [128, 1024], F32)
    h1w = hf0
    g0_bf = AA("g0_bf", [128, D], BF16)
    gT_bf = AA("gT_bf", [64, 128], BF16)
    bns = AA("bns", [128, 12], F32)
    ag2 = AA("ag2", [128, 2], F32)
    rstd0 = AA("rstd0", [128, 1], F32)
    nmr0 = AA("nmr0", [128, 1], F32)
    rstd1 = AA("rstd1", [128, 1], F32)
    nmr1 = AA("nmr1", [128, 1], F32)

    clipx_r = A("clipx_r", [1, 1], F32)
    clipd_r = A("clipd_r", [1, 1], F32)
    fbin_bf = A("fbin_bf", [1, D], BF16)
    lr_row = A0("lr_row", [1, H], F32)
    lr_bf = A0("lr_bf", [1, H], BF16)
    fbout_r = A0("fbout_r", [1, C], F32)
    fb2_bf = A0("fb2_bf", [1, D], BF16)


    pf_a = nc.alloc_psum_tensor("pf_a", [128, 2048], F32)
    pf_b = nc.alloc_psum_tensor("pf_b", [128, 1536], F32)
    pbf = nc.alloc_psum_tensor("pbf", [128, 1024], BF16)

    def p0_memset(g):
        for val, t in cpool.items():
            g.memset(t[:], val)
        g.memset(ident[:], 0.0)
        g.memset(identf[:], 0.0)
        g.memset(onesf_row[:], 1.0)
        g.memset(ones_row[:], 1.0)
        g.memset(one_f[:], 1.0)
        g.memset(maskLTn[:], 0.0)
        g.memset(maskUTn[:], 0.0)
        g.memset(maskUT[:], 0.0)
        g.memset(maskCa[:], 0.0)
    tk = p.step("g", [], p0_memset, "p0_memset")

    def p0_masks(g):
        sel = g.affine_select
        sel(out=ident[:], in_=ident[:], compare_op=OP.not_equal, fill=1.0,
            base=0, pattern=[[-1, 128]], channel_multiplier=1)
        sel(out=identf[:], in_=identf[:], compare_op=OP.not_equal, fill=1.0,
            base=0, pattern=[[-1, 128]], channel_multiplier=1)
        sel(out=maskLTn[:], in_=maskLTn[:], compare_op=OP.is_ge, fill=-1.0,
            base=0, pattern=[[1, 128]], channel_multiplier=-1)
        sel(out=maskUTn[:], in_=maskUTn[:], compare_op=OP.is_ge, fill=-1.0,
            base=0, pattern=[[-1, 128]], channel_multiplier=1)
        sel(out=maskUT[:], in_=maskUT[:], compare_op=OP.is_ge, fill=1.0,
            base=0, pattern=[[-1, 128]], channel_multiplier=1)
        sel(out=maskCa[:], in_=maskCa[:], compare_op=OP.is_gt, fill=1.0,
            base=0, pattern=[[-1, 128]], channel_multiplier=1)
    t_mask = p.step("g", [tk], p0_masks, "p0_masks")

    f0row_bf = A0("f0row_bf", [1, 1024], BF16)
    t_rows, d_rows = p.dma_step(
        "y", [(lr_row[:, :], lr_in[:, :]), (fbout_r[:, :], fbout[:, :]),
              (clipx_r[:, :], clipx[:, :]), (clipd_r[:, :], clipd[:, :])],
        [], "cst", "p0_rows")
    t_rows2, d_rows2 = p.dma_step(
        "g", [(fbin_bf[:, :], fbin[:, :]), (fb2_bf[:, :], fb2[:, :])],
        [t_mask], "cst2", "p0_rows2")

    def p0_eta_scale(v):
        v.tensor_scalar_mul(lr_bf[:, :], lr_row[:, :], INNER_LR)
    t_etas = p.step("v", [d_rows], p0_eta_scale, "eta_scale")

    bc_list = [(g1bc, f0row_bf, 0, ln1w), (be1bc, f0row_bf, 0, ln1b),
               (fW1bc, f0row_bf, 0, fW1), (fb1bc, f0row_bf, 0, fb1),
               (fW2bc, f0row_bf, 0, fW2)]
    t_bc_prev = None
    t_bm = None
    for bi, (dst, srcrow, off, hsrc) in enumerate(bc_list):
        deps = [d_rows2, t_mask, t_etas] if bi == 0 else [t_bc_prev]
        if hsrc is not None:
            ld_deps = [t_mask] + ([t_bm] if t_bm else [])
            _, d_f0 = p.dma_step("g", [(f0row_bf[:, :], hsrc[:, :])],
                                 ld_deps, "cst2", f"p0_f0row{bi}")
            deps = deps + [d_f0]
        def mk_mm(srcrow=srcrow, off=off):
            def fn(t):
                for hh in range(2):
                    t.matmul(pf_a[:, hh * 512:(hh + 1) * 512], ones_row[:, :],
                             srcrow[:, off + hh * 512:off + (hh + 1) * 512],
                             start=True, stop=True)
            return fn
        t_bm = p.step("t", deps, mk_mm(), f"bc_mm{bi}")
        def mk_ev(dst=dst):
            def fn(v):
                v.tensor_copy(dst[:, :], pf_a[:, 0:1024])
            return fn
        t_bc_prev = p.step("v", [t_bm], mk_ev(), f"bc_ev{bi}")

    def p0_eta_mm(t):
        t.matmul(pf_b[:, 0:H], ones_row[:, :], lr_bf[:, :],
                 start=True, stop=True)
    t_etam = p.step("t", [t_etas, t_mask], p0_eta_mm, "eta_mm")
    def p0_eta_ev(v):
        v.tensor_copy(eta_bc[:, :], pf_b[:, 0:H])
    t_eta = p.step("v", [t_etam], p0_eta_ev, "eta_ev")


    wload_transfers = []
    for ic in range(8):
        wload_transfers.append((wq_sb[:, ic * C:(ic + 1) * C], wqT[ic * 128:(ic + 1) * 128, :]))
        wload_transfers.append((wk_sb[:, ic * C:(ic + 1) * C], wkT[ic * 128:(ic + 1) * 128, :]))
        wload_transfers.append((wv_sb[:, ic * C:(ic + 1) * C], wvT[ic * 128:(ic + 1) * 128, :]))
        wload_transfers.append((Win_bf[:, ic * D:(ic + 1) * D], fWin[ic * 128:(ic + 1) * 128, :]))
    for cc in range(16):
        wload_transfers.append((wo_sb[:, cc * C:(cc + 1) * C], woT[cc * 64:(cc + 1) * 64, :]))
    wload_transfers.append((Wout_bf[:, :], fWout[:, :]))
    t_wl, d_wl = p.dma_step("g", wload_transfers, [t_mask], "wld", "L0_wload")

    fb2col_bf = A0("fb2col_bf", [64, 1], BF16)
    def p0_fb2col_mm(t):
        t.matmul(pf_b[0:64, 1056:1057], fb2_bf[:, :], ident[0:1, 0:1],
                 start=True, stop=True)
    t_f2m = p.step("t", [d_rows2, t_etam], p0_fb2col_mm, "fb2col_mm")
    def p0_fb2col_ev(v):
        v.tensor_copy(fb2col_bf[:, :], pf_b[0:64, 1056:1057])
    t_f2c = p.step("v", [t_f2m], p0_fb2col_ev, "fb2col_ev")
    def p0_ffb_mm(t):
        for hh in range(2):
            t.matmul(pf_b[0:1, hh * 512:(hh + 1) * 512], fb2col_bf[:, :],
                     Wout_bf[:, hh * 512:(hh + 1) * 512], start=True, stop=True)
    t_ffm = p.step("t", [t_f2c, d_wl, t_eta], p0_ffb_mm, "ffb_mm")
    def p0_ffb_ev(v):
        v.tensor_add(fbout_r[:, :], pf_b[0:1, 0:1024], fbout_r[:, :])
        v.drain()
        v.tensor_copy(ffb_bf[:, :], fbout_r[:, :])
    t_ffb = p.step("v", [t_ffm, d_rows], p0_ffb_ev, "ffb_ev")

    def p0_bc2_mm(t):
        t.matmul(pf_b[:, 64:128], ones_row[:, :], fbin_bf[:, :], start=True, stop=True)
        for hh in range(2):
            t.matmul(pf_a[:, hh * 512:(hh + 1) * 512], ones_row[:, :],
                     ffb_bf[:, hh * 512:(hh + 1) * 512], start=True, stop=True)
    t_bc2 = p.step("t", [t_ffb, t_bc_prev, d_rows2], p0_bc2_mm, "p0_bc2")
    def p0_bc2_ev(v):
        v.tensor_copy(fbin_bc[:, :], pf_b[:, 64:64 + D])
        v.tensor_copy(ffb_bc[:, :], pf_a[:, 0:1024])
    t_bc2e = p.step("v", [t_bc2], p0_bc2_ev, "p0_bc2e")

    p0_done = p.barrier("p0_done")
    p0_arena.close()


    u_dbg = dbg_out("u_dbg", (NT, C))
    pa_dbg = dbg_out("pa_dbg", (NT, C))

    def a_state0(g):
        g.memset(WTst[:], 0.0)
        g.memset(WTbf[:], 0.0)
    t_st0 = p.step("g", [p0_done["g"]], a_state0, "state0")
    t_stbf = t_st0
    t_stf = t_st0

    ln_chain = []

    def ln_steps(x_ap, deps, rstd, nmr, tag):
        def st(v):
            v.bn_stats(bns[:, 0:6], x_ap[:, 0:512])
            v.bn_stats(bns[:, 6:12], x_ap[:, 512:1024])
            v.drain()
            v.bn_aggr(ag2[:, :], bns[:, :])
        t1 = p.step("v", deps, st, f"{tag}_stats")
        def rs(s):
            s.activation(rstd[:, :], ag2[:, 1:2], AF.Ln, bias=LN_EPS, scale=1.0)
            s.drain()
            s.activation(rstd[:, :], rstd[:, :], AF.Exp, bias=0.0, scale=-0.5)
        t2 = p.step("s", [t1], rs, f"{tag}_rstd")
        def nm(v):
            v.tensor_scalar(nmr[:, :], ag2[:, 0:1], rstd[:, 0:1], -1.0,
                            OP.mult, OP.mult)
        t3 = p.step("v", [t2], nm, f"{tag}_nmr")
        return t3

    dx_toks = {}
    t_px, dx_toks[(0, 0)] = p.dma_step(
        "y", [(xa[:, 0:C], x_in[0:128, :])], [p0_done["y"]], "xin", "xin00")
    last_xin = t_px

    t_prev_ttt_tail = None
    t_prev_pbf = None
    t_u_dma = None
    t_pa_dma = None
    d_u_prev = []
    d_pa_prev = []
    wt_war = {b_: t_st0 for b_ in range(B)}
    pa_toks = {}
    ln_toks = {}

    def emit_front(gpos):
        par = gpos % 2
        b_ = gpos % B
        pos_ = gpos // B
        xcur = xa[:, par * C:(par + 1) * C]
        xnv = xn_f[:, par * C:(par + 1) * C]
        xnb = xn_bf[:, par * C:(par + 1) * C]
        h1b = h1_bf[:, par * C:(par + 1) * C]
        dx = dx_toks[(b_, pos_)]
        t_ln0 = ln_steps(xcur, [dx], rstd0, nmr0, f"ln0_{gpos}")
        def ap0(v, xcur=xcur, xnv=xnv, xnb=xnb):
            v.tensor_scalar(xnv, xcur, rstd0[:, 0:1], nmr0[:, 0:1],
                            OP.mult, OP.add)
            v.tensor_scalar(xnb, xcur, rstd0[:, 0:1], nmr0[:, 0:1],
                            OP.mult, OP.add)
        t_xn = p.step("v", [t_ln0], ap0, f"xn_{gpos}")
        t_ln1 = ln_steps(xnv, [t_xn], rstd1, nmr1, f"ln1_{gpos}")
        def ap1(v, xnv=xnv):
            v.tensor_scalar(h1w[:, :], xnv, rstd1[:, 0:1], nmr1[:, 0:1],
                            OP.mult, OP.add)
        t_h1r = p.step("v", [t_ln1], ap1, f"h1raw_{gpos}")
        def gb1(v, h1b=h1b):
            v.tensor_mul(h1w[:, :], h1w[:, :], g1bc[:, :])
            v.drain()
            v.tensor_add(h1w[:, :], h1w[:, :], be1bc[:, :])
            v.drain()
            v.tensor_copy(h1b, h1w[:, :])
        t_h1 = p.step("v", [t_h1r], gb1, f"h1_{gpos}")
        ln_toks[gpos] = (t_xn, t_h1)

    if "A" in phases:
      for pos in range(NPOS):
        for b in range(B):
            gpos = pos * B + b
            buf = gpos % 2
            row0 = b * T + pos * 128
            nb, npos_ = (b + 1, pos) if b + 1 < B else (0, pos + 1)
            if npos_ < NPOS:
                nrow = nb * T + npos_ * 128
                nbuf = (gpos + 1) % 2
                deps = [last_xin]
                if nbuf in pa_toks:
                    deps.append(pa_toks[nbuf])
                last_xin, dx_toks[(nb, npos_)] = p.dma_step(
                    "y", [(xa[:, nbuf * C:(nbuf + 1) * C], x_in[nrow:nrow + 128, :])],
                    deps, "xin", f"xin{nb}_{npos_}")

            if gpos == 0:
                emit_front(0)
            xcur = xa[:, buf * C:(buf + 1) * C]
            xn_v = xn_f[:, buf * C:(buf + 1) * C]
            t_xn, t_h1 = ln_toks.pop(gpos)

            def tx(t, src=xn_bf[:, buf * C:(buf + 1) * C]):
                for ic in range(8):
                    t.transpose(pbf[:, ic * 128:(ic + 1) * 128],
                                src[:, ic * 128:(ic + 1) * 128], ident[:, :])
            deps_t = [t_xn] + ([t_prev_pbf] if t_prev_pbf else [])
            t_txn = p.step("t", deps_t, tx, f"txn_{gpos}")
            def evx(s):
                s.copy(xnT_bf[:, :], pbf[:, :])
            t_xnT = p.step("s", [t_txn], evx, f"xnTev_{gpos}")
            def th1(t, src=h1_bf[:, buf * C:(buf + 1) * C]):
                for ic in range(8):
                    t.transpose(pbf[:, ic * 128:(ic + 1) * 128],
                                src[:, ic * 128:(ic + 1) * 128], ident[:, :])
            t_th1 = p.step("t", [t_h1, t_xnT], th1, f"th1_{gpos}")
            def evh(s):
                s.copy(h1T_bf[:, :], pbf[:, :])
            t_h1T = p.step("s", [t_th1], evh, f"h1Tev_{gpos}")
            t_prev_pbf = t_h1T

            war = [t_prev_ttt_tail] if t_prev_ttt_tail else []
            def qk_mm(t):
                for dc in range(8):
                    for ic in range(8):
                        t.matmul(pf_a[:, dc * 128:(dc + 1) * 128],
                                 wq_sb[:, ic * C + dc * 128:ic * C + (dc + 1) * 128],
                                 h1T_bf[:, ic * 128:(ic + 1) * 128],
                                 start=(ic == 0), stop=(ic == 7))
                for dc in range(8):
                    for ic in range(8):
                        t.matmul(pf_a[:, 1024 + dc * 128:1024 + (dc + 1) * 128],
                                 wk_sb[:, ic * C + dc * 128:ic * C + (dc + 1) * 128],
                                 h1T_bf[:, ic * 128:(ic + 1) * 128],
                                 start=(ic == 0), stop=(ic == 7))
            t_qk = p.step("t", [t_h1T, d_wl] + war, qk_mm, f"qk_{gpos}")
            if gpos + 1 < NPOS * B:
                emit_front(gpos + 1)
            def ev_q(s):
                s.copy(QT_sb[:, :], pf_a[:, 0:1024])
            t_q = p.step("s", [t_qk], ev_q, f"qev_{gpos}")
            def ev_k(v):
                v.tensor_copy(KT_sb[:, :], pf_a[:, 1024:2048])
            t_k = p.step("v", [t_qk], ev_k, f"kev_{gpos}")

            def v_mm(t):
                for hh in range(2):
                    for ic in range(8):
                        t.matmul(pf_b[:, hh * 512:(hh + 1) * 512],
                                 h1T_bf[:, ic * 128:(ic + 1) * 128],
                                 wv_sb[:, ic * C + hh * 512:ic * C + (hh + 1) * 512],
                                 start=(ic == 0), stop=(ic == 7))
            t_vm = p.step("t", [t_h1T, t_q, t_k], v_mm, f"vmm_{gpos}")
            def ev_v(s):
                s.copy(V_sb[:, :], pf_b[:, 0:1024])
            t_v = p.step("s", [t_vm], ev_v, f"vev_{gpos}")
            def km_mm(t):
                for hh in range(2):
                    for ic in range(8):
                        t.matmul(pf_b[:, hh * 512:(hh + 1) * 512],
                                 h1T_bf[:, ic * 128:(ic + 1) * 128],
                                 wk_sb[:, ic * C + hh * 512:ic * C + (hh + 1) * 512],
                                 start=(ic == 0), stop=(ic == 7))
            t_kmm = p.step("t", [t_v], km_mm, f"kmmm_{gpos}")
            def ev_km(v):
                v.tensor_copy(Km_sb[:, :], pf_b[:, 0:1024])
            t_km = p.step("v", [t_kmm], ev_km, f"kmev_{gpos}")

            def KT_h(h):
                return KT_sb[64 * (h % 2):64 * (h % 2) + 64, (h // 2) * 128:(h // 2) * 128 + 128]
            def QT_h(h):
                return QT_sb[64 * (h % 2):64 * (h % 2) + 64, (h // 2) * 128:(h // 2) * 128 + 128]
            def WTbf_h(h, b=b):
                return WTbf[64 * (h % 2):64 * (h % 2) + 64,
                            b * 1024 + 64 * h:b * 1024 + 64 * h + 64]
            def WTbf_hp(h, b=b):
                return WTbf[:, b * 1024 + 64 * h:b * 1024 + 64 * h + 64]

            def g_mm(t, KT_h=KT_h):
                for i, h in enumerate([0, 2, 4, 6, 8, 10, 12, 14, 1, 3, 5, 7, 9, 11, 13, 15]):
                    if i == 8:
                        t.drain()
                    t.matmul(pf_a[:, h * 128:(h + 1) * 128], KT_h(h), KT_h(h),
                             start=True, stop=True)
            t_g = p.step("t", [t_k, t_kmm], g_mm, f"G_{gpos}")

            mLT = maskLTn[:, :].unsqueeze(1).to_broadcast([128, H, 128])
            mUT = maskUTn[:, :].unsqueeze(1).to_broadcast([128, H, 128])
            mUTk = maskUT[:, :].unsqueeze(1).to_broadcast([128, H, 128])
            eta128 = eta_bc[:, :].unsqueeze(2).to_broadcast([128, H, 128])
            eta64 = eta_bc[:, :].unsqueeze(2).to_broadcast([128, H, 64])
            pav = pf_a[:, :].rearrange("p (h w) -> p h w", w=128)
            def b0_opsa(v):
                v.tensor_tensor(Bc0[:, :].rearrange("p (h w) -> p h w", w=128),
                                pav, mLT, OP.mult)
                v.tensor_tensor(Nc0[:, :].rearrange("p (h w) -> p h w", w=128),
                                pav, mUT, OP.mult)
            t_b0a = p.step("v", [t_g, t_eta, t_km], b0_opsa, f"B0a_{gpos}")
            def b0_opsb(v):
                v.tensor_tensor(Bc0[:, :].rearrange("p (h w) -> p h w", w=128),
                                Bc0[:, :].rearrange("p (h w) -> p h w", w=128),
                                eta128, OP.mult)
                v.tensor_tensor(Nc0[:, :].rearrange("p (h w) -> p h w", w=128),
                                Nc0[:, :].rearrange("p (h w) -> p h w", w=128),
                                eta128, OP.mult)
            t_b0 = p.step("v", [t_b0a], b0_opsb, f"B0b_{gpos}")

            def h_mm(t, KT_h=KT_h, QT_h=QT_h):
                for i, h in enumerate([0, 2, 4, 6, 8, 10, 12, 14, 1, 3, 5, 7, 9, 11, 13, 15]):
                    if i == 8:
                        t.drain()
                    t.matmul(pf_a[:, h * 128:(h + 1) * 128], KT_h(h), QT_h(h),
                             start=True, stop=True)
            t_hm = p.step("t", [t_b0a, t_q], h_mm, f"H_{gpos}")
            def hbar_ev(v):
                v.tensor_tensor(Hbar[:, :].rearrange("p (h w) -> p h w", w=128),
                                pav, mUTk, OP.mult)
            t_hb = p.step("v", [t_hm], hbar_ev, f"Hbar_{gpos}")

            def r_mm(t, WTbf_h=WTbf_h, KT_h=KT_h):
                for i, h in enumerate([0, 2, 4, 6, 8, 10, 12, 14, 1, 3, 5, 7, 9, 11, 13, 15]):
                    if i == 8:
                        t.drain()
                    t.matmul(pf_b[:, h * 64:(h + 1) * 64], KT_h(h), WTbf_h(h),
                             start=True, stop=True)
            t_r = p.step("t", [t_km, t_stbf, t_hm], r_mm, f"R_{gpos}")
            def c_init(v):
                v.tensor_tensor(c_f[:, :], pf_b[:, 0:1024], V_sb[:, :], OP.subtract)
            t_cf = p.step("v", [t_r, t_v], c_init, f"cinit_{gpos}")
            def c_cast(s):
                s.copy(c_bf[:, :], c_f[:, :])
            t_cb = p.step("s", [t_cf], c_cast, f"ccast_{gpos}")

            Bcur, Ncur, Bnx, Nnx = Bc0, Nc0, Bc1, Nc1
            t_bn_ready = [t_b0]
            t_sq_prev = None
            for k in range(7):
                def app_mm(t, Ncur=Ncur):
                    for h in range(H):
                        t.matmul(pf_b[:, h * 64:(h + 1) * 64],
                                 Ncur[:, h * 128:(h + 1) * 128],
                                 c_bf[:, h * 64:(h + 1) * 64],
                                 start=True, stop=True)
                t_app = p.step("t", [t_cb, t_cf] + t_bn_ready, app_mm,
                               f"app{k}_{gpos}")
                def c_upd(v):
                    v.tensor_add(c_f[:, :], c_f[:, :], pf_b[:, 0:1024])
                t_cf = p.step("v", [t_app], c_upd, f"cupd{k}_{gpos}")
                if k < 6:
                    def c_cast2(s):
                        s.copy(c_bf[:, :], c_f[:, :])
                    t_cb = p.step("s", [t_cf], c_cast2, f"ccast{k}_{gpos}")
                    do_b = (k < 5)
                    deps_sq = list(t_bn_ready) + [t_app, t_hb]
                    if t_sq_prev:
                        deps_sq.append(t_sq_prev)
                    if do_b:
                        def sq_b(t, Bcur=Bcur, Ncur=Ncur):
                            for h in range(H):
                                t.matmul(pf_a[:, h * 128:(h + 1) * 128],
                                         Ncur[:, h * 128:(h + 1) * 128],
                                         Bcur[:, h * 128:(h + 1) * 128],
                                         start=True, stop=True)
                        t_sqb = p.step("t", deps_sq, sq_b, f"sqB{k}_{gpos}")
                        def evb(s, Bnx=Bnx):
                            s.copy(Bnx[:, :], pf_a[:, :])
                        t_evb = p.step("s", [t_sqb], evb, f"evB{k}_{gpos}")
                    else:
                        t_evb = p.step("t", deps_sq, lambda t: None, f"sqBnop{k}_{gpos}")
                    def sq_n(t, Bcur=Bcur, Ncur=Ncur):
                        for h in range(H):
                            t.matmul(pf_a[:, h * 128:(h + 1) * 128],
                                     Bcur[:, h * 128:(h + 1) * 128],
                                     Ncur[:, h * 128:(h + 1) * 128],
                                     start=True, stop=True)
                    t_sqn = p.step("t", [t_evb], sq_n, f"sqN{k}_{gpos}")
                    def evn(s, Nnx=Nnx):
                        s.copy(Nnx[:, :], pf_a[:, :])
                    t_evn = p.step("s", [t_sqn], evn, f"evN{k}__{gpos}")
                    t_sq_prev = t_sqn
                    Bcur, Bnx = Bnx, Bcur
                    Ncur, Nnx = Nnx, Ncur
                    t_bn_ready = [t_evn] + ([t_evb] if do_b else [])

            def ct_ops(v):
                v.scalar_tensor_tensor(ct_bf[:, :].rearrange("p (h w) -> p h w", w=64),
                                       c_f[:, :].rearrange("p (h w) -> p h w", w=64),
                                       -1.0, eta64, OP.mult, OP.mult)
            t_ct = p.step("v", [t_cf], ct_ops, f"ct_{gpos}")

            def out_mm(t, WTbf_hp=WTbf_hp):
                for h in range(H):
                    t.matmul(pf_a[0:64, h * 128:(h + 1) * 128], WTbf_hp(h),
                             QT_sb[:, (h // 2) * 128:(h // 2) * 128 + 128],
                             start=True, stop=False)
                    t.matmul(pf_a[0:64, h * 128:(h + 1) * 128],
                             ct_bf[:, h * 64:(h + 1) * 64],
                             Hbar[:, h * 128:(h + 1) * 128],
                             start=False, stop=True)
            t_out = p.step("t", [t_ct, t_hb, t_stbf] + t_bn_ready, out_mm,
                           f"out_{gpos}")
            def out_ev(s):
                s.copy(OutT_bf[:, :], pf_a[0:64, :])
            t_oev = p.step("s", [t_out], out_ev, f"outev_{gpos}")

            def st_mm(t):
                for pr in range(8):
                    t.matmul(pf_b[:, pr * 128:(pr + 1) * 128],
                             Km_sb[:, pr * 128:(pr + 1) * 128],
                             ct_bf[:, pr * 128:(pr + 1) * 128],
                             start=True, stop=True)
            t_stm = p.step("t", [t_ct, t_km, t_cf, t_out], st_mm, f"stmm_{gpos}")
            def st_upd(v, b=b):
                top_in = pf_b[0:64, 0:1024].rearrange("p (r w) -> p r w", w=128)[:, :, 0:64]
                top_out = WTst[0:64, b * 512:(b + 1) * 512].rearrange("p (r w) -> p r w", w=64)
                v.tensor_tensor(top_out, top_out, top_in, OP.add)
                bot_in = pf_b[64:128, 0:1024].rearrange("p (r w) -> p r w", w=128)[:, :, 64:128]
                bot_out = WTst[64:128, b * 512:(b + 1) * 512].rearrange("p (r w) -> p r w", w=64)
                v.tensor_tensor(bot_out, bot_out, bot_in, OP.add)
            t_stu = p.step("v", [t_stm, wt_war[b], t_r], st_upd, f"stupd_{gpos}")
            def st_cast(s, b=b):
                wtop = WTbf[0:64, b * 1024:(b + 1) * 1024].rearrange(
                    "p (h w) -> p h w", w=64)[:, 0::2, :]
                s.copy(wtop, WTst[0:64, b * 512:(b + 1) * 512].rearrange(
                    "p (h w) -> p h w", w=64))
                s.drain()
                wbot = WTbf[64:128, b * 1024:(b + 1) * 1024].rearrange(
                    "p (h w) -> p h w", w=64)[:, 1::2, :]
                s.copy(wbot, WTst[64:128, b * 512:(b + 1) * 512].rearrange(
                    "p (h w) -> p h w", w=64))
            t_stbf = p.step("s", [t_stu], st_cast, f"stcast_{gpos}")
            wt_war[b] = t_stbf

            def y1_mm(t):
                for hh in range(2):
                    for cc in range(16):
                        t.matmul(pf_b[:, hh * 512:(hh + 1) * 512],
                                 OutT_bf[:, cc * 128:(cc + 1) * 128],
                                 wo_sb[:, cc * C + hh * 512:cc * C + (hh + 1) * 512],
                                 start=(cc == 0), stop=(cc == 15))
            t_y1 = p.step("t", [t_oev, t_stu, d_wl], y1_mm, f"y1_{gpos}")
            def u_ops(v, xn_v=xn_v):
                v.tensor_add(u_f[:, :], xn_v, pf_b[:, 0:1024])
            t_u = p.step("v", [t_y1, t_xn] + d_u_prev, u_ops, f"u_{gpos}")
            du_deps = [t_u] + ([t_u_dma] if t_u_dma else [])
            t_u_dma, d_u = p.dma_step("g", [(u_s[row0:row0 + 128, :], u_f[:, :])],
                                      du_deps, "out", f"udma_{gpos}")
            d_u_prev = [d_u]

            def s_mm(t):
                for ic in range(8):
                    t.matmul(pf_b[:, 1024 + 0:1024 + 64],
                             xnT_bf[:, ic * 128:(ic + 1) * 128],
                             Win_bf[:, ic * D:(ic + 1) * D],
                             start=(ic == 0), stop=(ic == 7))
            t_s0 = p.step("t", [t_xnT, d_wl, t_y1], s_mm, f"s0_{gpos}")
            def s_ev(v):
                v.tensor_add(s0_f[:, :], pf_b[:, 1024:1088], fbin_bc[:, :])
            t_s0e = p.step("v", [t_s0], s_ev, f"s0ev_{gpos}")
            def f0_el(v):
                s_exp = s0_f[:, :].unsqueeze(2).to_broadcast([128, D, 16])
                v.tensor_tensor(hf0[:, :].rearrange("p (m j) -> p m j", j=16),
                                s_exp, fW1bc[:, :].rearrange("p (m j) -> p m j", j=16),
                                OP.mult)
                v.drain()
                v.tensor_add(hf0[:, :], hf0[:, :], fb1bc[:, :])
                v.drain()
                v.tensor_scalar_max(hf0[:, :], hf0[:, :], 0.0)
                v.drain()
                v.tensor_mul(hf0[:, :], hf0[:, :], fW2bc[:, :])
                v.drain()
                v.tensor_reduce(s0_f[:, 0:D], hf0[:, :].rearrange("p (m j) -> p m j", j=16),
                                axis=AX.X, op=OP.add)
                v.drain()
                v.tensor_copy(g0_bf[:, :], s0_f[:, 0:D])
            t_g0 = p.step("v", [t_s0e, p0_done["v"]], f0_el, f"g0_{gpos}")
            def gt_mm(t):
                t.transpose(pbf[0:64, 0:128], g0_bf[:, :], ident[:, :])
            t_gt = p.step("t", [t_g0, t_s0, t_prev_pbf], gt_mm, f"gt_{gpos}")
            def gt_ev(v):
                v.tensor_copy(gT_bf[:, :], pbf[0:64, 0:128])
            t_gte = p.step("v", [t_gt], gt_ev, f"gtev_{gpos}")
            t_prev_pbf = t_gte
            def f0_mm(t):
                for hh in range(2):
                    t.matmul(pf_a[:, hh * 512:(hh + 1) * 512], gT_bf[:, :],
                             Wout_bf[:, hh * 512:(hh + 1) * 512],
                             start=True, stop=True)
            t_f0 = p.step("t", [t_gte, t_out, t_oev], f0_mm, f"f0_{gpos}")

            def pa_ops(v, xcur=xcur):
                v.tensor_add(pa_f[:, :], xcur, pf_a[:, 0:1024])
                v.drain()
                v.tensor_add(pa_f[:, :], pa_f[:, :], ffb_bc[:, :])
                v.drain()
                v.tensor_add(pa_f[:, :], pa_f[:, :], u_f[:, :])
            t_pa = p.step("v", [t_f0, t_u] + d_pa_prev, pa_ops, f"pa_{gpos}")
            pa_toks[buf] = t_pa
            dpa_deps = [t_pa] + ([t_pa_dma] if t_pa_dma else [])
            t_pa_dma, d_pa = p.dma_step("g", [(pa_s[row0:row0 + 128, :], pa_f[:, :])],
                                        dpa_deps, "out", f"padma_{gpos}")
            d_pa_prev = [d_pa]
            if dbg_on("u"):
                _, d_ud = p.dma_step("g", [(u_dbg[row0:row0 + 128, :], u_f[:, :])],
                                     [t_u, t_u_dma], "dbg", f"udbg_{gpos}")
                _, d_pd = p.dma_step("g", [(pa_dbg[row0:row0 + 128, :], pa_f[:, :])],
                                     [t_pa, t_pa_dma], "dbg", f"padbg_{gpos}")
                d_u_prev.append(d_ud)
                d_pa_prev.append(d_pd)

            t_prev_ttt_tail = t_pa

    st_dbg = dbg_out("st_dbg", (128, B * 512))
    if dbg_on("st") and "A" in phases:
        p.dma_step("g", [(st_dbg[:, :], WTst[:, :])],
                   [p.tok("v"), p.tok("s"), p.tok("t")], "dbg", "stdbg")

    pA_done = p.barrier("pA_done")
    arenaA.close()

    TTOK = 256 if T % 256 == 0 else 128
    NSUB = TTOK // 128
    NTT = T // TTOK
    x1_dbg = dbg_out("x1_dbg", (NT, C))

    if "B" in phases:
        arenaB, AB = mkarena()
        w1_sb = AB("w1_sb", [128, 8 * 4096], BF16)
        w2_sb = AB("w2_sb", [128, 32 * 1024], BF16)
        g2bc = AB("g2bc", [128, C], F32)
        be2bc = AB("be2bc", [128, C], F32)
        b2bc = AB("b2bc", [128, C], F32)
        b1cols = AB("b1cols", [128, 32], F32)
        ln2row_bf = AB("ln2row_bf", [1, 2 * C], BF16)
        b1row_bf = AB("b1row_bf", [1, 1024], BF16)
        b2row_bf = AB("b2row_bf", [1, C], BF16)
        u_t = AB("u_t", [128, 2 * NSUB * C], F32)
        pa_t = AB("pa_t", [128, NSUB * C], F32)
        h2w = AB("h2w", [128, C], F32)
        h2bf = AB("h2bf", [128, NSUB * C], BF16)
        h2T_bf = AB("h2T_bf", [128, 8 * TTOK], BF16)
        gel0 = AB("gel0", [128, TTOK], BF16)
        gel1 = AB("gel1", [128, TTOK], BF16)
        bnsB = AB("bnsB", [128, 12], F32)
        agB = AB("agB", [128, 2], F32)
        rstdB = AB("rstdB", [128, 1], F32)
        nmrB = AB("nmrB", [128, 1], F32)

        tB_w1, dB_w1 = p.dma_step("g", [
            (w1_sb[:, ic * 4096:(ic + 1) * 4096], w1T[ic * 128:(ic + 1) * 128, :])
            for ic in range(8)], [pA_done["g"]], "wld", "B_w1")
        tB_w2, dB_w2 = p.dma_step("g", [
            (w2_sb[:, d4 * 1024:(d4 + 1) * 1024], w2T[d4 * 128:(d4 + 1) * 128, :])
            for d4 in range(32)], [tB_w1], "wld", "B_w2")
        tB_r, dB_r = p.dma_step("g", [
            (ln2row_bf[:, 0:C], ln2w[:, :]), (ln2row_bf[:, C:2 * C], ln2b[:, :]),
            (b2row_bf[:, :], b2_in[:, :])],
            [tB_w2], "wld", "B_rows")

        def b_g2mm(t):
            for hh in range(2):
                t.matmul(pf_a[:, hh * 512:(hh + 1) * 512], ones_row[:, :],
                         ln2row_bf[:, hh * 512:(hh + 1) * 512], start=True, stop=True)
                t.matmul(pf_a[:, 1024 + hh * 512:1024 + (hh + 1) * 512], ones_row[:, :],
                         ln2row_bf[:, C + hh * 512:C + (hh + 1) * 512], start=True, stop=True)
                t.matmul(pf_b[:, hh * 512:(hh + 1) * 512], ones_row[:, :],
                         b2row_bf[:, hh * 512:(hh + 1) * 512], start=True, stop=True)
        tB_gm = p.step("t", [dB_r, pA_done["t"]], b_g2mm, "B_g2mm")
        prev_b1m = None
        for rr in range(4):
            ld_deps = [tB_w2] if prev_b1m is None else [prev_b1m]
            _, d_b1 = p.dma_step("g", [(b1row_bf[:, :],
                                        b1_in[0:1, rr * 1024:(rr + 1) * 1024])],
                                 ld_deps, "wld", f"B_b1r{rr}")
            def b1mm(t, rr=rr):
                for ch in range(8):
                    t.matmul(pf_b[:, 1024 + rr * 8 + ch:1025 + rr * 8 + ch],
                             b1row_bf[:, ch * 128:(ch + 1) * 128],
                             ones_row[0:1, 0:1], start=True, stop=True)
            prev_b1m = p.step("t", [d_b1] + ([tB_gm] if rr == 0 else []), b1mm,
                              f"B_b1m{rr}")
        def b_g2ev(v):
            v.tensor_copy(g2bc[:, :], pf_a[:, 0:1024])
            v.tensor_copy(be2bc[:, :], pf_a[:, 1024:2048])
            v.tensor_copy(b2bc[:, :], pf_b[:, 0:1024])
            v.tensor_copy(b1cols[:, :], pf_b[:, 1024:1056])
        tB_ge = p.step("v", [tB_gm, prev_b1m, pA_done["v"]], b_g2ev, "B_g2ev")

        dB_out_prev = []
        t_outdma = None
        t_prev_tail = None
        t_prev_pbfB = None
        lnB_toks = {}
        uld_war = [None, None]
        NGI = B * NTT

        def emit_frontB(gi, tr_dep):
            par = gi % 2
            b_ = gi // NTT
            row0 = b_ * T + (gi % NTT) * TTOK
            deps_u = [pA_done["y"]]
            if uld_war[par] is not None:
                deps_u.append(uld_war[par])
            t_uin, d_uin = p.dma_step("y", [
                (u_t[:, (par * NSUB + s) * C:(par * NSUB + s + 1) * C],
                 u_s[row0 + s * 128:row0 + (s + 1) * 128, :])
                for s in range(NSUB)], deps_u, "xin", f"B_uin{gi}")
            casts = []
            t4 = None
            for s in range(NSUB):
                usub = u_t[:, (par * NSUB + s) * C:(par * NSUB + s + 1) * C]
                def stB(v, usub=usub):
                    v.bn_stats(bnsB[:, 0:6], usub[:, 0:512])
                    v.bn_stats(bnsB[:, 6:12], usub[:, 512:1024])
                    v.drain()
                    v.bn_aggr(agB[:, :], bnsB[:, :])
                t1 = p.step("v", [d_uin, tB_ge], stB, f"B_ln{gi}_{s}")
                def rsB(sc):
                    sc.activation(rstdB[:, :], agB[:, 1:2], AF.Ln, bias=LN_EPS,
                                  scale=1.0)
                    sc.drain()
                    sc.activation(rstdB[:, :], rstdB[:, :], AF.Exp, bias=0.0,
                                  scale=-0.5)
                t2 = p.step("s", [t1], rsB, f"B_rstd{gi}_{s}")
                def nmB(v):
                    v.tensor_scalar(nmrB[:, :], agB[:, 0:1], rstdB[:, 0:1], -1.0,
                                    OP.mult, OP.mult)
                t3 = p.step("v", [t2], nmB, f"B_nmr{gi}_{s}")
                def apB(v, usub=usub):
                    v.tensor_scalar(h2w[:, :], usub, rstdB[:, 0:1], nmrB[:, 0:1],
                                    OP.mult, OP.add)
                    v.drain()
                    v.tensor_mul(h2w[:, :], h2w[:, :], g2bc[:, :])
                    v.drain()
                    v.tensor_add(h2w[:, :], h2w[:, :], be2bc[:, :])
                t4 = p.step("v", [t3], apB, f"B_h2{gi}_{s}")
                def cstB(sc, s=s):
                    sc.copy(h2bf[:, s * C:(s + 1) * C], h2w[:, :])
                deps_c = [t4]
                if s == 0 and tr_dep is not None:
                    deps_c.append(tr_dep)
                casts.append(p.step("s", deps_c, cstB, f"B_h2c{gi}_{s}"))
            uld_war[par] = t4
            lnB_toks[gi] = casts

        for b in range(B):
            for ti in range(NTT):
                gi = b * NTT + ti
                row0 = b * T + ti * TTOK
                if gi == 0:
                    emit_frontB(0, None)
                t_pain, d_pain = p.dma_step("y", [
                    (pa_t[:, s * C:(s + 1) * C],
                     pa_s[row0 + s * 128:row0 + (s + 1) * 128, :])
                    for s in range(NSUB)], [pA_done["y"]] + dB_out_prev,
                    "xin", f"B_pain{gi}")

                casts = lnB_toks.pop(gi)
                t_h2T = None
                tr2 = None
                for s in range(NSUB):
                    deps_t = [casts[s]] + ([t_prev_pbfB] if t_prev_pbfB else [])
                    def trB(t, s=s):
                        for ic in range(8):
                            t.transpose(pbf[:, ic * 128:(ic + 1) * 128],
                                        h2bf[:, s * C + ic * 128:s * C + (ic + 1) * 128],
                                        ident[:, :])
                    t6 = p.step("t", deps_t, trB, f"B_tr{gi}_{s}")
                    tr2 = t6
                    def evB(v, s=s):
                        v.tensor_copy(
                            h2T_bf[:, :].rearrange("p (i w) -> p i w", w=TTOK)[:, :, s * 128:(s + 1) * 128],
                            pbf[:, :].rearrange("p (i w) -> p i w", w=128))
                    t_h2T = p.step("v", [t6], evB, f"B_trev{gi}_{s}")
                    t_prev_pbfB = t_h2T

                t_w1 = {}
                t_gel = {}
                t_w2 = {}
                gels = [gel0, gel1]
                def emit_w2(d4c):
                    def w2mm(t, d4c=d4c):
                        for s in range(NSUB):
                            for hh in range(2):
                                t.matmul(pf_a[:, (s * 2 + hh) * 512:(s * 2 + hh + 1) * 512],
                                         gels[d4c % 2][:, s * 128:(s + 1) * 128],
                                         w2_sb[:, d4c * 1024 + hh * 512:d4c * 1024 + (hh + 1) * 512],
                                         start=(d4c == 0), stop=(d4c == 31))
                    deps = [t_gel[d4c]]
                    if d4c == 0:
                        deps.append(t_prev_tail or tB_ge)
                    t_w2[d4c] = p.step("t", deps, w2mm, f"B_w2_{gi}_{d4c}")
                for d4c in range(32):
                    def w1mm(t, d4c=d4c):
                        for ic in range(8):
                            t.matmul(pf_b[:, (d4c % 2) * 512:(d4c % 2) * 512 + TTOK],
                                     w1_sb[:, ic * 4096 + d4c * 128:ic * 4096 + (d4c + 1) * 128],
                                     h2T_bf[:, ic * TTOK:(ic + 1) * TTOK],
                                     start=(ic == 0), stop=(ic == 7))
                    deps = [t_h2T, dB_w1]
                    if d4c >= 2:
                        deps.append(t_gel[d4c - 2])
                    t_w1[d4c] = p.step("t", deps, w1mm, f"B_w1_{gi}_{d4c}")
                    def gelB(sc, d4c=d4c):
                        sc.activation(gels[d4c % 2][:, :], pf_b[:, (d4c % 2) * 512:(d4c % 2) * 512 + TTOK],
                                      AF.Gelu, bias=b1cols[:, d4c:d4c + 1], scale=1.0)
                    t_gel[d4c] = p.step("s", [t_w1[d4c], tB_ge] +
                                        ([t_w2[d4c - 2]] if d4c >= 2 else []),
                                        gelB, f"B_gel_{gi}_{d4c}")
                    if d4c >= 1:
                        emit_w2(d4c - 1)
                emit_w2(31)
                tb2 = t_w2[31]

                if gi + 1 < NGI:
                    emit_frontB(gi + 1, tr2)

                def outB(v):
                    for s in range(NSUB):
                        for hh in range(2):
                            sl = pa_t[:, s * C + hh * 512:s * C + (hh + 1) * 512]
                            v.tensor_add(sl, sl, pf_a[:, (s * 2 + hh) * 512:(s * 2 + hh + 1) * 512])
                            v.drain()
                            v.tensor_add(sl, sl, b2bc[:, hh * 512:(hh + 1) * 512])
                t_o = p.step("v", [tb2, d_pain, tB_ge], outB, f"B_out{gi}")
                t_prev_tail = t_o
                od_deps = [t_o] + ([t_outdma] if t_outdma else [])
                trs_o = [(x1_s[row0 + s * 128:row0 + (s + 1) * 128, :],
                          pa_t[:, s * C:(s + 1) * C]) for s in range(NSUB)]
                t_outdma, d_o = p.dma_step("g", trs_o, od_deps, "out", f"B_odma{gi}")
                dB_out_prev = [d_o]
                if dbg_on("x1"):
                    _, d_xd = p.dma_step("g", [(x1_dbg[row0 + s * 128:row0 + (s + 1) * 128, :],
                                                pa_t[:, s * C:(s + 1) * C]) for s in range(NSUB)],
                                         [t_o, t_outdma], "dbg", f"B_xdbg{gi}")
                    dB_out_prev.append(d_xd)

        pB_done = p.barrier("pB_done")
        arenaB.close()

    wsc_g = A("wsc_g", [128, 22], F32)
    wsc_u = A("wsc_u", [128, 22], F32)
    wsc_d = A("wsc_d", [128, 8], F32)

    if "E" in phases:
        arenaW, AW = mkarena()
        wstg = AW("wstg", [128, 2 * 2816], F32)
        wq1 = AW("wq1", [128, 2816], F32)
        wqb = AW("wqb", [128, 2816], BF16)
        wqev = AW("wqev", [128, 1024], BF16)
        wmc = AW("wmc", [128, 1], F32)
        wrc = AW("wrc", [128, 1], F32)
        prevB = p.barrier("preW")

        jobs = []
        for rc in range(22):
            jobs.append((wg_in, wgq_s, wsc_g, rc, 1024, 8))
        for rc in range(22):
            jobs.append((wu_in, wuq_s, wsc_u, rc, 1024, 8))
        for rc in range(8):
            jobs.append((wd_in, wdq_s, wsc_d, rc, 2816, 22))
        jobs_t = {}
        t_q1s = {}
        t_evW_prev = None
        d_outs = []
        for jidx, (w_src, w_dst, wsc, rc, CW, NTR) in enumerate(jobs):
            buf = jidx % 2
            deps = [prevB["y"]] if jidx == 0 else [jobs_t[jidx - 1]]
            if jidx >= 2:
                deps.append(t_q1s[jidx - 2])
            t_l, d_l = p.dma_step("y", [(wstg[:, buf * 2816:buf * 2816 + CW],
                                         w_src[rc * 128:(rc + 1) * 128, :])],
                                  deps, "wld", f"W_l{jidx}")
            jobs_t[jidx] = t_l
            def q0(v, buf=buf, CW=CW, wsc=wsc, rc=rc):
                v.tensor_reduce(wmc[:, :], wstg[:, buf * 2816:buf * 2816 + CW],
                                axis=AX.X, op=OP.max, apply_absolute_value=True)
                v.drain()
                v.reciprocal(wrc[:, :], wmc[:, :])
                v.drain()
                v.tensor_scalar_mul(wrc[:, :], wrc[:, :], QMAX)
                v.tensor_scalar_mul(wsc[:, rc:rc + 1], wmc[:, :], 1.0 / QMAX)
                v.drain()
                v.tensor_scalar(wq1[:, 0:CW], wstg[:, buf * 2816:buf * 2816 + CW],
                                wrc[:, 0:1], MR, OP.mult, OP.add)
                v.drain()
                v.tensor_scalar(wq1[:, 0:CW], wq1[:, 0:CW], -MR, -(QMAX + 1.0),
                                OP.add, OP.max)
                v.drain()
                v.tensor_scalar(wqb[:, 0:CW], wq1[:, 0:CW], QMAX, None, OP.min)
            dep_q = [d_l] + ([t_q1s[jidx - 1]] if jidx >= 1 else [])
            if t_evW_prev is not None:
                dep_q.append(t_trW_last)
            t_q = p.step("v", dep_q, q0, f"W_q{jidx}")
            t_q1s[jidx] = t_q
            for grp in range(0, NTR, 8):
                n = min(8, NTR - grp)
                def trW(t, grp=grp, n=n):
                    for i in range(n):
                        t.transpose(pbf[:, i * 128:(i + 1) * 128],
                                    wqb[:, (grp + i) * 128:(grp + i + 1) * 128],
                                    ident[:, :])
                depsT = [t_q, prevB["t"]] + ([t_evW_prev] if t_evW_prev else [])
                t_tr = p.step("t", depsT, trW, f"W_tr{jidx}_{grp}")
                t_trW_last = t_tr
                def evW(s, n=n):
                    s.copy(wqev[:, 0:n * 128], pbf[:, 0:n * 128])
                t_ev = p.step("s", [t_tr] + d_outs, evW, f"W_ev{jidx}_{grp}")
                t_evW_prev = t_ev
                trs = [(w_dst[(grp + i) * 128:(grp + i + 1) * 128,
                              rc * 128:(rc + 1) * 128],
                        wqev[:, i * 128:(i + 1) * 128]) for i in range(n)]
                t_od, d_od = p.dma_step("g", trs, [t_ev], "out", f"W_o{jidx}_{grp}")
                d_outs = [d_od]
        pW_done = p.barrier("pW_done")
        arenaW.close()

    mx_col = A("mx_col", [128, 1], F32)
    gmax_x = A("gmax_x", [1, 1], F32)
    sx_col = A("sx_col", [128, 1], F32)
    rsx_col = A("rsx_col", [128, 1], F32)
    gmax_h = A("gmax_h", [1, 1], F32)
    sh_col = A("sh_col", [128, 1], F32)
    rsh_col = A("rsh_col", [128, 1], F32)
    scr11 = A("scr11", [1, 4], F32)
    cc_sem = nc.alloc_semaphore("cc_sem")
    p.dsems["cc0"] = [cc_sem, 0]
    p.dlanes["cc"] = {"keys": ["cc0"], "rr": 0}

    pairswap = []
    for i in range(16):
        pairswap += [2 * i + 1, 2 * i]
    Lm05 = float(L) ** -0.5

    if "C" in phases:
        arenaCD, ACD = mkarena()
        klatT = ACD("klatT", [128, B * 4 * T], BF16)
        qfkT = ACD("qfkT", [128, B * 4 * T], BF16)
        vI = ACD("vI", [128, B * NPOS * 528], BF16)
        cos_fm = ACD("cos_fm", [128, T], BF16)
        sin_fm = ACD("sin_fm", [128, T], BF16)
        q2lBD = ACD("q2lBD", [128, 64], BF16)
        kv2lBD = ACD("kv2lBD", [128, 64], BF16)
        kv2l_rep = ACD("kv2l_rep", [128, 32], BF16)
        wqkBD = ACD("wqkBD", [128, 128], BF16)
        qb_col = ACD("qb_col", [128, 1], F32)
        kvb_col = ACD("kvb_col", [128, 1], F32)
        wqkb_col = ACD("wqkb_col", [128, 1], F32)
        kvb_rep = ACD("kvb_rep", [1, 512], BF16)
        kvb_bc = ACD("kvb_bc", [128, 512], F32)
        l2oI = ACD("l2oI", [33, 64], BF16)
        cpb_bf = ACD("cpb_bf", [1, C], BF16)

        arenaC, AC = mkarena()
        ca_sb = AC("ca_sb", [128, 8 * 3 * C], BF16)
        cab_cols = AC("cab_cols", [128, 24], F32)
        cab_bf = klatT[0:1, 0:3 * C]
        kvb_row = AC("kvb_row", [1, L], BF16)
        x1_t = AC("x1_t", [128, 2 * C], F32)
        xn2_f = AC("xn2_f", [128, C], F32)
        xnT_st = AC("xnT_st", [128, 8 * 256], F32)
        xnT_bfc = AC("xnT_bfc", [128, 2048], BF16)
        qkv_bf = AC("qkv_bf", [128, 24 * 256], BF16)
        qr_f = AC("qr_f", [128, 1024], F32)
        qr_bf = AC("qr_bf", [128, 1024], BF16)
        qsw = xn2_f
        bnsC = AC("bnsC", [128, 12], F32)
        agC = AC("agC", [128, 2], F32)
        mxsc = AC("mxsc", [128, 1], F32)
        rstdC = AC("rstdC", [128, 1], F32)
        nmrC = AC("nmrC", [128, 1], F32)
        iotaw = xnT_st[0:1, 0:T]
        invf_row = AC("invf_row", [1, 32], F32)
        sgn_row = AC("sgn_row", [1, 32], F32)
        invf_col = AC("invf_col", [128, 1], F32)
        sgn_col = AC("sgn_col", [128, 1], F32)
        tbc = x1_t[:, 0:T]
        s2t = xnT_st[:, 0:T]
        c2t = qkv_bf[:, :].bitcast(F32)[:, 0:T]

        prevW = p.barrier("preC")
        tC_r, dC_r = p.dma_step("g", [
            (cab_bf[:, :], cab[:, :]), (kvb_row[:, :], kv2lb[:, :]),
            (q2lBD[0:64, 0:32], q2lT[:, :]), (q2lBD[64:128, 32:64], q2lT[:, :]),
            (kv2lBD[0:64, 0:32], kv2lT[:, :]), (kv2lBD[64:128, 32:64], kv2lT[:, :]),
            (kv2l_rep[0:64, :], kv2lT[:, :]), (kv2l_rep[64:128, :], kv2lT[:, :]),
            (wqkBD[0:32, 0:32], wqkT[:, :]), (wqkBD[32:64, 32:64], wqkT[:, :]),
            (wqkBD[64:96, 64:96], wqkT[:, :]), (wqkBD[96:128, 96:128], wqkT[:, :]),
            (l2oI[0:32, :], l2oT[:, :]), (l2oI[32:33, :], l2ob[:, :]),
            (cpb_bf[:, :], cpb[:, :])],
            [prevW["g"]], "cst", "C_rows")

        def c_zero(g):
            g.memset(q2lBD[0:64, 32:64], 0.0)
            g.memset(q2lBD[64:128, 0:32], 0.0)
            g.memset(kv2lBD[0:64, 32:64], 0.0)
            g.memset(kv2lBD[64:128, 0:32], 0.0)
            g.memset(wqkBD[0:32, 32:128], 0.0)
            g.memset(wqkBD[32:64, 0:32], 0.0)
            g.memset(wqkBD[32:64, 64:128], 0.0)
            g.memset(wqkBD[64:96, 0:64], 0.0)
            g.memset(wqkBD[64:96, 96:128], 0.0)
            g.memset(wqkBD[96:128, 0:96], 0.0)
            g.memset(mx_col[:], 0.0)
            g.iota(iotaw, pattern=[[1, T]], base=0, channel_multiplier=0,
                   allow_small_or_imprecise_dtypes=True)
            g.iota(invf_row[:, :], pattern=[[1, 16], [0, 2]], base=0,
                   channel_multiplier=0, allow_small_or_imprecise_dtypes=True)
            g.iota(sgn_row[:, :], pattern=[[0, 16], [1, 2]], base=0,
                   channel_multiplier=0, allow_small_or_imprecise_dtypes=True)
        tC_z = p.step("g", [prevW["g"], dC_r], c_zero, "C_zero")

        p.dsems["colC00"] = [nc.alloc_semaphore("dma_colC0"), 0]
        rec_cc = p.dsems["colC00"]
        def c_cols(g):
            sem = rec_cc[0]
            for gq in range(4):
                g.dma_start(out=qb_col[32 * gq:32 * gq + 32, 0:1], in_=q2lb[0:1, :],
                            allow_slow_non_contiguous=True).then_inc(sem, 16)
                g.dma_start(out=kvb_col[32 * gq:32 * gq + 32, 0:1], in_=kv2lb[0:1, :],
                            allow_slow_non_contiguous=True).then_inc(sem, 16)
                g.dma_start(out=wqkb_col[32 * gq:32 * gq + 32, 0:1], in_=wqkb[0:1, :],
                            allow_slow_non_contiguous=True).then_inc(sem, 16)
        rec_cc[1] += 16 * 12
        tC_cols = p.step("g", [tC_z], c_cols, "C_cols")
        dC_cols = DmaTok((("colC00", rec_cc[1]),))

        def c_invf(s):
            s.activation(invf_row[:, :], invf_row[:, :], AF.Exp,
                         bias=0.0, scale=-math.log(10000.0) / 16.0)
        tC_iv = p.step("s", [tC_z], c_invf, "C_invf")
        def c_sgn(v):
            v.tensor_scalar(sgn_row[:, :], sgn_row[:, :], 2.0, -1.0, OP.mult, OP.add)
        tC_sg = p.step("v", [tC_z], c_sgn, "C_sgn")
        def c_tmm(t):
            wch = min(512, T)
            for ch in range(T // wch):
                t.matmul(pf_a[:, ch * wch:(ch + 1) * wch], onesf_row[:, :],
                         iotaw[0:1, ch * wch:(ch + 1) * wch], start=True, stop=True)
            for gq in range(4):
                t.matmul(pf_b[32 * gq:32 * gq + 32, 1024:1025], invf_row[:, :],
                         one_f[:, :], start=True, stop=True,
                         tile_position=(0, 32 * gq))
                t.matmul(pf_b[32 * gq:32 * gq + 32, 1026:1027], sgn_row[:, :],
                         one_f[:, :], start=True, stop=True,
                         tile_position=(0, 32 * gq))
        tC_tm = p.step("t", [tC_z, tC_iv, tC_sg, prevW["t"]], c_tmm, "C_tmm")
        def c_tb(v):
            v.tensor_copy(tbc[:, :], pf_a[:, 0:T])
            v.tensor_copy(invf_col[:, :], pf_b[:, 1024:1025])
            v.tensor_copy(sgn_col[:, :], pf_b[:, 1026:1027])
        tC_tb = p.step("v", [tC_tm], c_tb, "C_tb")
        def c_rope1(v):
            v.tensor_scalar(tbc, tbc, invf_col[:, 0:1], None, OP.mult)
            v.drain()
            v.tensor_scalar_mul(s2t, tbc, 1.0 / (2.0 * math.pi))
        tC_r1 = p.step("v", [tC_tb], c_rope1, "C_rope1")
        def c_rope2(s):
            s.activation(c2t, s2t, AF.Copy, bias=MR, scale=1.0)
        tC_r2 = p.step("s", [tC_r1], c_rope2, "C_rope2")
        def c_rope3(v):
            v.tensor_scalar_add(c2t, c2t, -MR)
            v.drain()
            v.tensor_sub(s2t, s2t, c2t)
        tC_r3 = p.step("v", [tC_r2], c_rope3, "C_rope3")
        dbg_r_tok = []
        if dbg_on("r"):
            r_dbg = dbg_out("r_dbg", (128, T))
            _, d_rd = p.dma_step("g", [(r_dbg[:, :], s2t)], [tC_r3], "dbg", "C_rdbg")
            dbg_r_tok = [d_rd]
        def c_rope4(s):
            s.activation(c2t, s2t, AF.Sin, bias=math.pi / 2.0,
                         scale=-math.pi)
            s.drain()
            s.activation(s2t, s2t, AF.Sin, bias=0.0, scale=math.pi)
            s.drain()
            s.activation(cos_fm[:, :], s2t, AF.Square, bias=0.0, scale=1.0)
        tC_r4 = p.step("s", [tC_r3] + dbg_r_tok, c_rope4, "C_rope4")
        def c_rope5(v):
            v.tensor_scalar(cos_fm[:, :], cos_fm[:, :], -2.0, 1.0, OP.mult, OP.add)
            v.drain()
            v.tensor_mul(sin_fm[:, :], s2t, c2t)
            v.drain()
            v.tensor_scalar(sin_fm[:, :], sin_fm[:, :], sgn_col[:, 0:1], 2.0,
                            OP.mult, OP.mult)
            v.drain()
            v.tensor_copy(kvb_rep[:, :].rearrange("p (h l) -> p h l", l=32),
                          kvb_row[:, :].unsqueeze(1).to_broadcast([1, 16, 32]))
        tC_r5 = p.step("v", [tC_r4, dC_r], c_rope5, "C_rope5")

        def c_kvbc(t):
            t.matmul(pf_b[:, 512:1024], ones_row[:, :], kvb_rep[:, :],
                     start=True, stop=True)
        tC_kvm = p.step("t", [tC_r5, tC_tm], c_kvbc, "C_kvbm")
        def c_kvbe(v):
            v.tensor_copy(kvb_bc[:, :], pf_b[:, 512:1024])
        tC_kv = p.step("v", [tC_kvm], c_kvbe, "C_kvbe")

        def c_cabc(t):
            for d3c in range(24):
                t.matmul(pf_b[:, 1040 + d3c:1041 + d3c],
                         cab_bf[:, d3c * 128:(d3c + 1) * 128],
                         ones_row[0:1, 0:1], start=True, stop=True)
        tC_cabm = p.step("t", [dC_r, tC_tm], c_cabc, "C_cabm")
        def c_cabe(v):
            v.tensor_copy(cab_cols[:, :], pf_b[:, 1040:1064])
        tC_cc = p.step("v", [tC_cabm], c_cabe, "C_cabe")

        tC_w, dC_w = p.dma_step("g", [
            (ca_sb[:, ic * 3072:(ic + 1) * 3072], caT[ic * 128:(ic + 1) * 128, :])
            for ic in range(8)], [prevW["g"], tC_cols], "wld", "C_ca")

        NTTC = T // 256
        d_inC_prev = []
        t_xnT_dma = None
        t_prev_pfbC = prevW["t"]
        t_prev_tailC = None
        t_mx = tC_z
        t_ptdma = None
        for b in range(B):
            for ti in range(NTTC):
                gi = b * NTTC + ti
                row0 = b * T + ti * 256
                t0 = ti * 256
                t_in, d_in = p.dma_step("y", [
                    (x1_t[:, s * C:(s + 1) * C],
                     x1_s[row0 + s * 128:row0 + (s + 1) * 128, :])
                    for s in range(2)], [prevW["y"], tC_r5] + d_inC_prev,
                    "xin", f"C_in{gi}")

                t_tr_toks = []
                for s in range(2):
                    xsub = x1_t[:, s * C:(s + 1) * C]
                    def stC(v, xsub=xsub):
                        v.bn_stats(bnsC[:, 0:6], xsub[:, 0:512])
                        v.bn_stats(bnsC[:, 6:12], xsub[:, 512:1024])
                        v.drain()
                        v.bn_aggr(agC[:, :], bnsC[:, :])
                    t1 = p.step("v", [d_in, tC_r5], stC, f"C_ln{gi}_{s}")
                    def rsC(sc):
                        sc.activation(rstdC[:, :], agC[:, 1:2], AF.Ln, bias=LN_EPS,
                                      scale=1.0)
                        sc.drain()
                        sc.activation(rstdC[:, :], rstdC[:, :], AF.Exp, bias=0.0,
                                      scale=-0.5)
                    t2 = p.step("s", [t1], rsC, f"C_rstd{gi}_{s}")
                    def nmC(v):
                        v.tensor_scalar(nmrC[:, :], agC[:, 0:1], rstdC[:, 0:1], -1.0,
                                        OP.mult, OP.mult)
                    t3 = p.step("v", [t2], nmC, f"C_nmr{gi}_{s}")
                    def apC(v, xsub=xsub):
                        v.tensor_scalar(xn2_f[:, :], xsub, rstdC[:, 0:1], nmrC[:, 0:1],
                                        OP.mult, OP.add)
                        v.drain()
                        v.tensor_reduce(mxsc[:, :], xn2_f[:, :], axis=AX.X,
                                        op=OP.max, apply_absolute_value=True)
                        v.drain()
                        v.tensor_max(mx_col[:, :], mx_col[:, :], mxsc[:, :])
                    t4 = p.step("v", [t3, t_mx], apC, f"C_xn2{gi}_{s}")
                    t_mx = t4
                    def trC(t):
                        for ic in range(8):
                            t.transpose(pf_b[:, ic * 128:(ic + 1) * 128],
                                        xn2_f[:, ic * 128:(ic + 1) * 128], identf[:, :])
                    depsT = [t4, t_prev_pfbC] + ([t_tr_toks[-1]] if t_tr_toks else [])
                    t5 = p.step("t", depsT, trC, f"C_tr{gi}_{s}")
                    def evC(v, s=s):
                        v.tensor_copy(
                            xnT_st[:, :].rearrange("p (i w) -> p i w", w=256)[:, :, s * 128:(s + 1) * 128],
                            pf_b[:, 0:1024].rearrange("p (i w) -> p i w", w=128))
                    t6 = p.step("v", [t5], evC, f"C_trev{gi}_{s}")
                    t_tr_toks.append(t6)
                    t_prev_pfbC = t6
                def cstC2(sc):
                    sc.copy(xnT_bfc[:, :], xnT_st[:, :])
                t_xc = p.step("s", t_tr_toks, cstC2, f"C_xc{gi}")
                xnTdeps = t_tr_toks + ([t_xnT_dma] if t_xnT_dma else [])
                t_xnT_dma, d_xnT = p.dma_step("g", [
                    (xnT_s[ic * 128:(ic + 1) * 128, row0:row0 + 256],
                     xnT_st[:, ic * 256:(ic + 1) * 256]) for ic in range(8)],
                    xnTdeps, "out", f"C_xnT{gi}")
                d_inC_prev = [d_xnT]

                t_ca = {}
                t_cev = {}
                for d3c in range(24):
                    def camm1(t, d3c=d3c):
                        for ic in range(8):
                            t.matmul(pf_a[:, (d3c % 4) * 512:(d3c % 4) * 512 + 256],
                                     ca_sb[:, ic * 3072 + d3c * 128:ic * 3072 + (d3c + 1) * 128],
                                     xnT_bfc[:, ic * 256:(ic + 1) * 256],
                                     start=(ic == 0), stop=(ic == 7))
                    deps = [t_xc, dC_w, tC_tm]
                    if t_prev_tailC and d3c == 0:
                        deps.append(t_prev_tailC)
                    if d3c >= 4:
                        deps.append(t_cev[d3c - 4])
                    t_ca[d3c] = p.step("t", deps, camm1, f"C_ca{gi}_{d3c}")
                    def cev(v, d3c=d3c):
                        v.tensor_scalar(qkv_bf[:, d3c * 256:(d3c + 1) * 256],
                                        pf_a[:, (d3c % 4) * 512:(d3c % 4) * 512 + 256],
                                        cab_cols[:, d3c:d3c + 1], None, OP.add)
                    t_cev[d3c] = p.step("v", [t_ca[d3c], tC_cc], cev, f"C_cev{gi}_{d3c}")

                def lat_q(t):
                    for g_ in range(8):
                        for s in range(2):
                            t.matmul(pf_b[64 * (g_ % 2):64 * (g_ % 2) + 64,
                                          (g_ // 2) * 256 + s * 128:
                                          (g_ // 2) * 256 + (s + 1) * 128],
                                     q2lBD[:, :],
                                     qkv_bf[:, g_ * 256 + s * 128:g_ * 256 + (s + 1) * 128],
                                     start=True, stop=True,
                                     tile_position=(0, 64 * (g_ % 2)))
                t_lq = p.step("t", [t_cev[7], t_prev_pfbC, dC_r], lat_q, f"C_lq{gi}")
                def lat_k(t):
                    for g_ in range(8):
                        for s in range(2):
                            t.matmul(pf_a[64 * (g_ % 2):64 * (g_ % 2) + 64,
                                          1024 + (g_ // 2) * 256 + s * 128:
                                          1024 + (g_ // 2) * 256 + (s + 1) * 128],
                                     kv2lBD[:, :],
                                     qkv_bf[:, (8 + g_) * 256 + s * 128:
                                            (8 + g_) * 256 + (s + 1) * 128],
                                     start=True, stop=True,
                                     tile_position=(0, 64 * (g_ % 2)))
                t_lk = p.step("t", [t_cev[22], t_cev[23]], lat_k, f"C_lk{gi}")

                cosv = cos_fm[:, t0:t0 + 256].unsqueeze(1).to_broadcast([128, 4, 256])
                sinv = sin_fm[:, t0:t0 + 256].unsqueeze(1).to_broadcast([128, 4, 256])
                qrv = qr_f[:, :].rearrange("p (hg w) -> p hg w", w=256)
                qsv = qsw[:, :].rearrange("p (hg w) -> p hg w", w=256)

                def rope_q(v):
                    v.tensor_scalar(qr_f[:, :], pf_b[:, 0:1024], qb_col[:, 0:1],
                                    None, OP.add)
                    v.drain()
                    v.stream_shuffle(qsw[:, :], qr_f[:, :], pairswap)
                    v.drain()
                    v.tensor_tensor(qrv, qrv, cosv, OP.mult)
                    v.tensor_tensor(qsv, qsv, sinv, OP.mult)
                    v.drain()
                    v.tensor_add(qr_f[:, :], qr_f[:, :], qsw[:, :])
                t_rq = p.step("v", [t_lq, dC_cols, tC_r5], rope_q, f"C_ropeq{gi}")
                def rq_cast(sc):
                    sc.copy(qr_bf[:, :], qr_f[:, :])
                t_rqc = p.step("s", [t_rq], rq_cast, f"C_rqc{gi}")

                klv = klatT[:, :].rearrange("p (bb hg w) -> p (bb hg) w", hg=4, w=T)[
                    :, 4 * b:4 * b + 4, t0:t0 + 256]
                def rope_k(v, klv=klv):
                    v.tensor_scalar(qsw[:, :], pf_a[:, 1024:2048], kvb_col[:, 0:1],
                                    None, OP.add)
                    v.drain()
                    v.stream_shuffle(qr_f[:, :], qsw[:, :], pairswap)
                    v.drain()
                    v.tensor_tensor(qsv, qsv, cosv, OP.mult)
                    v.tensor_tensor(qrv, qrv, sinv, OP.mult)
                    v.drain()
                    v.tensor_add(qsw[:, :], qsw[:, :], qr_f[:, :])
                    v.drain()
                    v.tensor_copy(klv, qsv)
                t_rk = p.step("v", [t_lk, t_rqc, t_rq], rope_k, f"C_ropek{gi}")

                def qfk_mm(t):
                    for hg in range(4):
                        t.matmul(pf_a[:, hg * 256:(hg + 1) * 256], wqkBD[:, :],
                                 qr_bf[:, hg * 256:(hg + 1) * 256],
                                 start=True, stop=True)
                t_qf = p.step("t", [t_rqc, t_cev[23], t_lk], qfk_mm, f"C_qfk{gi}")
                qfv = qfkT[:, :].rearrange("p (bb hg w) -> p (bb hg) w", hg=4, w=T)[
                    :, 4 * b:4 * b + 4, t0:t0 + 256]
                def qfk_ev(v, qfv=qfv):
                    v.tensor_scalar(qfv,
                                    pf_a[:, 0:1024].rearrange("p (hg w) -> p hg w", w=256),
                                    wqkb_col[:, 0:1], cpool[Lm05][:, 0:1],
                                    OP.add, OP.mult)
                t_qfe = p.step("v", [t_qf, dC_cols], qfk_ev, f"C_qfe{gi}")

                t_lv_toks = []
                for s in range(2):
                    kt = ti * 2 + s
                    blk = (b * NPOS + kt) * 528
                    def lat_v(t, s=s):
                        for i, h in enumerate([0, 2, 4, 6, 8, 10, 12, 14, 1, 3, 5, 7, 9, 11, 13, 15]):
                            if i == 8:
                                t.drain()
                            t.matmul(pf_b[:, 1024 + h * 32:1024 + (h + 1) * 32],
                                     qkv_bf[:, (16 + h // 2) * 256 + s * 128:
                                            (16 + h // 2) * 256 + (s + 1) * 128]
                                     [64 * (h % 2):64 * (h % 2) + 64, :],
                                     kv2l_rep[64 * (h % 2):64 * (h % 2) + 64, :],
                                     start=True, stop=True)
                    deps = [t_cev[23], tC_r5] + ([t_lv_toks[-1]] if t_lv_toks else [])
                    t_lvm = p.step("t", deps + [t_qf], lat_v, f"C_lv{gi}_{s}")
                    def lat_vev(v, blk=blk):
                        vIblk = vI[:, blk:blk + 528].rearrange("p (h l) -> p h l", l=33)
                        v.tensor_tensor(vIblk[:, :, 0:32],
                                        pf_b[:, 1024:1536].rearrange("p (h l) -> p h l", l=32),
                                        kvb_bc[:, :].rearrange("p (h l) -> p h l", l=32),
                                        OP.add)
                        v.drain()
                        v.memset(vIblk[:, :, 32:33], 1.0)
                    t_lve = p.step("v", [t_lvm, tC_kv], lat_vev, f"C_lvev{gi}_{s}")
                    t_lv_toks.append(t_lve)

                t_prev_tailC = t_qfe
                t_prev_pfbC = t_lv_toks[-1]

        def mxfin_mm(t):
            t.matmul(pf_b[0:1, 1024:1152], mx_col[:, :], identf[:, :],
                     start=True, stop=True)
        t_mm1 = p.step("t", [t_mx, t_prev_tailC, p.tok("v")], mxfin_mm, "C_mxmm")
        def mxfin_v(v):
            v.tensor_reduce(gmax_x[:, :], pf_b[0:1, 1024:1152], axis=AX.X, op=OP.max)
        t_mv1 = p.step("v", [t_mm1], mxfin_v, "C_mxv")
        tcc1, dcc1_out = p.dma_step("g", [(ccxi.ap()[0:1, 0:1], gmax_x[:, :])],
                                    [t_mv1], "ccio", "C_ccout")
        cc_inc = 16 if cfg.get("skip_cc") else 1
        if cfg.get("skip_cc"):
            def cc1(g):
                g.dma_start(out=ccxo.ap()[:, :], in_=ccxi.ap()[:, :],
                            ).then_inc(cc_sem, 16)
        else:
            def cc1(g):
                g.collective_compute(
                    "AllReduce", OP.max, replica_groups=RG,
                    ins=[ccxi.ap().opt()], outs=[ccxo.ap().opt()],
                ).then_inc(cc_sem, 1)
        t_cc1 = p.step("g", [tcc1, dcc1_out], cc1, "C_cc1")
        p.dsems["cc0"][1] += cc_inc
        dcc1 = DmaTok((("cc0", cc_inc),))
        tcc1b, dcc1b = p.dma_step("g", [(gmax_x[:, :], ccxo.ap()[0:1, 0:1])],
                                  [t_cc1, dcc1], "ccio", "C_ccin")

        if dbg_on("cs"):
            cs_dbg = dbg_out("cs_dbg", (128, 2 * T))
            iv_dbg = dbg_out("iv_dbg", (128, 1))
            sg_dbg = dbg_out("sg_dbg", (128, 1))
            p.dma_step("g", [(cs_dbg[:, 0:T], cos_fm[:, :]),
                             (cs_dbg[:, T:2 * T], sin_fm[:, :]),
                             (iv_dbg[:, :], invf_col[:, :]),
                             (sg_dbg[:, :], sgn_col[:, :])],
                       [tC_r5], "dbg", "C_csdbg")
            kl_dbg = dbg_out("kl_dbg", (128, B * 4 * T))
            qf_dbg = dbg_out("qf_dbg", (128, B * 4 * T))
            vI_dbg = dbg_out("vI_dbg", (128, B * NPOS * 528))
            p.dma_step("g", [(kl_dbg[:, :], klatT[:, :]), (qf_dbg[:, :], qfkT[:, :]),
                             (vI_dbg[:, :], vI[:, :])],
                       [p.tok("v"), p.tok("s"), p.tok("t")], "dbg", "C_dbg")

        pC_done = p.barrier("pC_done")
        arenaC.close()

    pt_dbg = dbg_out("pt_dbg", (NT, C))
    if "D" in phases:
        arenaD, AD = mkarena()
        cp_sb = AD("cp_sb", [128, 8 * C], BF16)
        E_bf = AD("E_bf", [128, 2048], BF16)
        AVZ_bf = AD("AVZ_bf", [33, 2048], BF16)
        zrow_t = AD("zrow_t", [33, 2048], F32)
        onef33 = AD("onef33", [33, 1], F32)
        nu_st = AD("nu_st", [128, 32], F32)
        nu_bf = AD("nu_bf", [128, 32], BF16)
        negc_col = AD("negc_col", [16, 2], F32)
        negc_bf = AD("negc_bf", [16, 2], BF16)
        negrow_bf = AD("negrow_bf", [1, 32], BF16)
        negC_cols = AD("negC_cols", [128, 32], F32)
        invZ_c = AD("invZ_c", [128, 16], F32)
        y_sb = AD("y_sb", [128, 1024], BF16)
        yT_bf = AD("yT_bf", [128, 1024], BF16)
        pt_f = AD("pt_f", [128, 1024], F32)
        cpb_bc = AD("cpb_bc", [128, C], F32)
        x1c = AD("x1c", [128, 2 * C], F32)
        mxD = AD("mxD", [128, 1], F32)

        tD_w, dD_w = p.dma_step("g", [
            (cp_sb[:, ic * C:(ic + 1) * C], cpT[ic * 128:(ic + 1) * 128, :])
            for ic in range(8)], [pC_done["g"]], "wld", "D_cp")
        def d_prep(g):
            g.memset(onef33[:], 1.0)
        tD_pr = p.step("g", [pC_done["g"]], d_prep, "D_prep")
        def d_pbm(t):
            for hh in range(2):
                t.matmul(pf_a[:, hh * 512:(hh + 1) * 512], ones_row[:, :],
                         cpb_bf[:, hh * 512:(hh + 1) * 512], start=True, stop=True)
        tD_pb = p.step("t", [pC_done["t"]], d_pbm, "D_pbm")
        def d_pbe(v):
            v.tensor_copy(cpb_bc[:, :], pf_a[:, 0:1024])
        tD_pe = p.step("v", [tD_pb, pC_done["v"]], d_pbe, "D_pbe")

        def kl_sl(b_, h, kt):
            colb = (b_ * 4 + h // 4) * T
            return klatT[32 * (h % 4):32 * (h % 4) + 32,
                         colb + kt * 128:colb + (kt + 1) * 128]
        def qf_sl(b_, h, qt):
            colb = (b_ * 4 + h // 4) * T
            return qfkT[32 * (h % 4):32 * (h % 4) + 32,
                        colb + qt * 128:colb + (qt + 1) * 128]
        def vI_sl(b_, kt, h):
            return vI[:, (b_ * NPOS + kt) * 528 + 33 * h:
                      (b_ * NPOS + kt) * 528 + 33 * h + 33]

        t_ptdma = None
        d_pt_prev = []
        t_prev_pbfD = pC_done["t"]
        t_prev_cproj = None
        t_prev_zro = [pC_done["s"], pC_done["v"]]
        dx1_toks = {}
        t_x1last = None
        last_slot_tok = [None, None]
        last_Eslot_tok = [[None, None], [None, None]]
        prev_ct = [{}, {}]
        NGQ = B * NPOS

        class DCtx:
            pass

        def emit_round1(cx, i):
            gq, par, b_, qt = cx.gq, cx.par, cx.b, cx.qt
            h, g = divmod(i, cx.ng)
            k0 = g * 4
            k1 = min(qt + 1, k0 + 4)
            def r1(t, h=h, k0=k0, k1=k1, par=par, b_=b_, qt=qt):
                tp = (32 * (h % 4), 0) if h % 4 == 3 else None
                for kt in range(k0, k1):
                    t.matmul(pf_b[:, par * 512 + (kt - k0) * 128:
                                  par * 512 + (kt - k0 + 1) * 128],
                             kl_sl(b_, h, kt), qf_sl(b_, h, qt),
                             start=True, stop=True, tile_position=tp)
            deps = [pC_done["t"]]
            if i > 0:
                deps.append(cx.t_vmax[i - 1])
            elif last_slot_tok[par] is not None:
                deps.append(last_slot_tok[par])
            cx.t_r1[i] = p.step("t", deps, r1, f"D_r1_{gq}_{h}_{g}")
            def vmax(v, h=h, g=g, k0=k0, k1=k1, par=par):
                if g == 0:
                    v.tensor_reduce(nu_st[:, par * 16 + h:par * 16 + h + 1],
                                    pf_b[:, par * 512:par * 512 + (k1 - k0) * 128],
                                    axis=AX.X, op=OP.max)
                else:
                    v.tensor_reduce(mxD[:, :],
                                    pf_b[:, par * 512:par * 512 + (k1 - k0) * 128],
                                    axis=AX.X, op=OP.max)
                    v.drain()
                    v.tensor_max(nu_st[:, par * 16 + h:par * 16 + h + 1],
                                 nu_st[:, par * 16 + h:par * 16 + h + 1],
                                 mxD[:, :])
            cx.t_vmax[i] = p.step("v", [cx.t_r1[i]], vmax, f"D_vm_{gq}_{h}_{g}")
            if i == cx.nidx - 1:
                last_slot_tok[par] = cx.t_vmax[i]

        def emit_ctail(cx):
            gq, par = cx.gq, cx.par
            pc = prev_ct[par]
            ct2 = pf_b[0:16, 1056 + par * 128:1184 + par * 128]
            ct4 = pf_b[0:1, 1312 + par * 16:1328 + par * 16]
            ct6 = pf_b[:, 1344 + par * 16:1360 + par * 16]
            def c_t1(v, par=par):
                v.tensor_copy(nu_bf[:, par * 16:(par + 1) * 16],
                              nu_st[:, par * 16:(par + 1) * 16])
            t_ct1 = p.step("v", [cx.t_vmax[cx.nidx - 1]], c_t1, f"D_ct1_{gq}")
            def c_t2(t, par=par, ct2=ct2):
                t.matmul(ct2, nu_bf[:, par * 16:(par + 1) * 16], ident[:, :],
                         start=True, stop=True)
            d2 = [t_ct1, cx.t_vmax[cx.nidx - 1]]
            if "t3" in pc:
                d2.append(pc["t3"])
            t_ct2 = p.step("t", d2, c_t2, f"D_ct2_{gq}")
            def c_t3(v, par=par, ct2=ct2):
                v.tensor_reduce(negc_col[:, par:par + 1], ct2, axis=AX.X,
                                op=OP.max, negate=True)
                v.drain()
                v.tensor_copy(negc_bf[:, par:par + 1], negc_col[:, par:par + 1])
            t_ct3 = p.step("v", [t_ct2], c_t3, f"D_ct3_{gq}")
            pc["t3"] = t_ct3
            def c_t4(t, par=par, ct4=ct4):
                t.matmul(ct4, negc_bf[:, par:par + 1], ident[0:16, 0:16],
                         start=True, stop=True)
            d4 = [t_ct3] + ([pc["t5"]] if "t5" in pc else [])
            t_ct4 = p.step("t", d4, c_t4, f"D_ct4_{gq}")
            def c_t5(v, par=par, ct4=ct4):
                v.tensor_copy(negrow_bf[:, par * 16:(par + 1) * 16], ct4)
            t_ct5 = p.step("v", [t_ct4], c_t5, f"D_ct5_{gq}")
            pc["t5"] = t_ct5
            def c_t6(t, par=par, ct6=ct6):
                t.matmul(ct6, ones_row[:, :],
                         negrow_bf[:, par * 16:(par + 1) * 16],
                         start=True, stop=True)
            d6 = [t_ct5] + ([pc["t7"]] if "t7" in pc else [])
            t_ct6 = p.step("t", d6, c_t6, f"D_ct6_{gq}")
            def c_t7(v, par=par, ct6=ct6):
                v.tensor_copy(negC_cols[:, par * 16:(par + 1) * 16], ct6)
            t_ct7 = p.step("v", [t_ct6], c_t7, f"D_ct7_{gq}")
            pc["t7"] = t_ct7
            cx.t_ct7 = t_ct7

        def emit_r2exp(cx, i):
            gq, par, b_, qt = cx.gq, cx.par, cx.b, cx.qt
            h, g = divmod(i, cx.ng)
            k0 = g * 4
            k1 = min(qt + 1, k0 + 4)
            es = (par * 2 + (i % 2)) * 512
            def r2(t, h=h, k0=k0, k1=k1, par=par, b_=b_, qt=qt):
                tp = (32 * (h % 4), 0) if h % 4 == 3 else None
                for kt in range(k0, k1):
                    t.matmul(pf_b[:, par * 512 + (kt - k0) * 128:
                                  par * 512 + (kt - k0 + 1) * 128],
                             kl_sl(b_, h, kt), qf_sl(b_, h, qt),
                             start=True, stop=True, tile_position=tp)
            deps = [cx.t_ct7]
            if i > 0:
                deps.append(cx.t_exp[i - 1])
            cx.t_r2[i] = p.step("t", deps, r2, f"D_r2_{gq}_{h}_{g}")
            def expf(sc, h=h, k0=k0, k1=k1, par=par, es=es):
                sc.activation(E_bf[:, es:es + (k1 - k0) * 128],
                              pf_b[:, par * 512:par * 512 + (k1 - k0) * 128],
                              AF.Exp, bias=negC_cols[:, par * 16 + h:par * 16 + h + 1],
                              scale=1.0)
            deps_e = [cx.t_r2[i], cx.t_ct7]
            if i > 1:
                deps_e.append(cx.t_av[i - 2])
            elif last_Eslot_tok[par][i % 2] is not None:
                deps_e.append(last_Eslot_tok[par][i % 2])
            cx.t_exp[i] = p.step("s", deps_e, expf, f"D_ex_{gq}_{h}_{g}")
            cx.tok_e[i] = cx.t_exp[i]
            if k1 - 1 == qt:
                def mskf(v, k0=k0, k1=k1, es=es):
                    sl = E_bf[:, es + (k1 - 1 - k0) * 128:es + (k1 - k0) * 128]
                    v.tensor_mul(sl, sl, maskCa[:, :])
                cx.tok_e[i] = p.step("v", [cx.t_exp[i]], mskf, f"D_msk_{gq}_{h}")

        def emit_av(cx, i):
            gq, par, b_, qt = cx.gq, cx.par, cx.b, cx.qt
            h, g = divmod(i, cx.ng)
            k0 = g * 4
            k1 = min(qt + 1, k0 + 4)
            es = (par * 2 + (i % 2)) * 512
            def avf(t, h=h, k0=k0, k1=k1, es=es, b_=b_, qt=qt):
                for kt in range(k0, k1):
                    t.matmul(pf_a[0:33, h * 128:(h + 1) * 128],
                             vI_sl(b_, kt, h),
                             E_bf[:, es + (kt - k0) * 128:es + (kt - k0 + 1) * 128],
                             start=(kt == 0), stop=(kt == qt))
            deps_a = [cx.tok_e[i]] + (cx.zro + [tD_pe] if i == 0 else [])
            cx.t_av[i] = p.step("t", deps_a, avf, f"D_av_{gq}_{h}_{g}")
            if i == cx.nidx - 1:
                last_slot_tok[par] = cx.t_exp[i]
                last_Eslot_tok[par] = [cx.t_av[i - 1] if i > 0 else None,
                                       cx.t_av[i]] if i % 2 == 1 else \
                                      [cx.t_av[i],
                                       cx.t_av[i - 1] if i > 0 else None]

        def emit_tail(cx):
            nonlocal t_prev_pbfD, t_prev_cproj, t_prev_zro, t_ptdma, d_pt_prev
            gq, par, row0 = cx.gq, cx.par, cx.row0
            zt = pf_b[:, 1376 + par * 16:1392 + par * 16]
            t_last_av = cx.t_av[cx.nidx - 1]
            def d_ave(sc):
                sc.copy(AVZ_bf[:, :], pf_a[0:33, :])
                sc.drain()
                sc.copy(zrow_t[32:33, :], pf_a[32:33, :])
            t_ave = p.step("s", [t_last_av], d_ave, f"D_ave_{gq}")
            def d_zt(t, zt=zt):
                for h in range(16):
                    t.matmul(zt[:, h:h + 1],
                             zrow_t[32:33, h * 128:(h + 1) * 128],
                             onef33[32:33, :], start=True, stop=True)
            dz = [t_ave, t_last_av, tD_pr]
            if "zi" in prev_ct[par]:
                dz.append(prev_ct[par]["zi"])
            t_zt = p.step("t", dz, d_zt, f"D_zt_{gq}")
            def d_iz(v, zt=zt):
                v.reciprocal(invZ_c[:, :], zt)
            t_iz = p.step("v", [t_zt], d_iz, f"D_iz_{gq}")
            prev_ct[par]["zi"] = t_iz
            def d_l2o(t):
                for h in range(16):
                    t.matmul(pf_a[:, 1024 + h * 64:1024 + (h + 1) * 64],
                             AVZ_bf[:, h * 128:(h + 1) * 128], l2oI[:, :],
                             start=True, stop=True)
            t_l2o = p.step("t", [t_ave, t_iz], d_l2o, f"D_l2o_{gq}")
            def d_ye(v):
                for h in range(16):
                    v.tensor_scalar(y_sb[:, h * 64:(h + 1) * 64],
                                    pf_a[:, 1024 + h * 64:1024 + (h + 1) * 64],
                                    invZ_c[:, h:h + 1], None, OP.mult)
            t_ye = p.step("v", [t_l2o, t_iz], d_ye, f"D_ye_{gq}")
            def d_ytr(t):
                for ic in range(8):
                    t.transpose(pbf[:, ic * 128:(ic + 1) * 128],
                                y_sb[:, ic * 128:(ic + 1) * 128], ident[:, :])
            t_ytr = p.step("t", [t_ye, t_prev_pbfD], d_ytr, f"D_ytr_{gq}")
            def d_ytre(sc):
                sc.copy(yT_bf[:, :], pbf[:, :])
            t_ytre = p.step("s", [t_ytr], d_ytre, f"D_ytre_{gq}")
            t_prev_pbfD = t_ytre
            def d_cproj(t):
                for hh in range(2):
                    for ic in range(8):
                        t.matmul(pf_a[:, hh * 512:(hh + 1) * 512],
                                 yT_bf[:, ic * 128:(ic + 1) * 128],
                                 cp_sb[:, ic * C + hh * 512:ic * C + (hh + 1) * 512],
                                 start=(ic == 0), stop=(ic == 7))
            t_cproj = p.step("t", [t_ytre, dD_w, t_ave], d_cproj, f"D_cp_{gq}")
            t_prev_cproj = t_cproj
            def d_pt(v, par=par):
                v.tensor_add(pt_f[:, :], x1c[:, par * C:(par + 1) * C],
                             pf_a[:, 0:1024])
                v.drain()
                v.tensor_add(pt_f[:, :], pt_f[:, :], cpb_bc[:, :])
            t_pt = p.step("v", [t_cproj, dx1_toks[gq], tD_pe] + d_pt_prev, d_pt,
                          f"D_pt_{gq}")
            t_prev_zro = [t_pt]
            ptdeps = [t_pt] + ([t_ptdma] if t_ptdma else [])
            t_ptdma, d_ptd = p.dma_step("g", [(pt_s[row0:row0 + 128, :],
                                               pt_f[:, :])], ptdeps, "out",
                                        f"D_ptd_{gq}")
            d_pt_prev = [d_ptd]
            if dbg_on("pt"):
                _, d_ptdbg = p.dma_step("g", [(pt_dbg[row0:row0 + 128, :],
                                               pt_f[:, :])],
                                        [t_pt, t_ptdma], "dbg", f"D_ptdbg_{gq}")
                d_pt_prev.append(d_ptdbg)

        prev_cx = None
        for gq in range(NGQ + 1):
            cx = None
            if gq < NGQ:
                cx = DCtx()
                cx.gq = gq
                cx.b = gq // NPOS
                cx.qt = gq % NPOS
                cx.par = gq % 2
                cx.row0 = cx.b * T + cx.qt * 128
                cx.ng = (cx.qt + 1 + 3) // 4
                cx.nidx = 16 * cx.ng
                cx.t_r1, cx.t_vmax = {}, {}
                cx.t_r2, cx.t_exp, cx.t_av, cx.tok_e = {}, {}, {}, {}
                if gq == 0:
                    t_x1last, dx1_toks[0] = p.dma_step(
                        "y", [(x1c[:, 0:C], x1_s[cx.row0:cx.row0 + 128, :])],
                        [pC_done["y"]], "xin", "D_x0")
            n1 = cx.nidx if cx else 0
            n2 = prev_cx.nidx if prev_cx else 0
            if prev_cx:
                prev_cx.zro = t_prev_zro
            for i in range(max(n1, n2)):
                if prev_cx and i < n2:
                    emit_r2exp(prev_cx, i)
                if cx and i < n1:
                    emit_round1(cx, i)
                if prev_cx and i < n2:
                    emit_av(prev_cx, i)
            if cx:
                emit_ctail(cx)
            if prev_cx:
                emit_tail(prev_cx)
            if cx and gq + 1 < NGQ:
                nb, nqt = (gq + 1) // NPOS, (gq + 1) % NPOS
                nrow = nb * T + nqt * 128
                deps = [t_x1last] + d_pt_prev
                t_x1last, dx1_toks[gq + 1] = p.dma_step(
                    "y", [(x1c[:, ((gq + 1) % 2) * C:((gq + 1) % 2 + 1) * C],
                           x1_s[nrow:nrow + 128, :])], deps, "xin",
                    f"D_x{gq+1}")
            prev_cx = cx

        pD_done = p.barrier("pD_done")
        arenaD.close()
        arenaCD.close()

    if "E" in phases:
        arenaE, AE = mkarena()
        wgq_sb = AE("wgq_sb", [128, 8 * ISW], BF16)
        wuq_sb = AE("wuq_sb", [128, 8 * ISW], BF16)
        xstgE = AE("xstgE", [128, 2 * 4096], F32)
        Xq_bf = AE("Xq_bf", [128, 2 * 4096], BF16)
        silu_g = AE("silu_g", [128, 512], F32)
        up_f = AE("up_f", [128, 512], F32)
        hbuf = AE("hbuf", [128, 2 * 512], F32)
        sgc = AE("sgc", [128, 22], F32)
        suc = AE("suc", [128, 22], F32)
        hmx_col = AE("hmx_col", [128, 1], F32)
        mxE = AE("mxE", [128, 1], F32)
        sgE = AE("sgE", [1, 1], F32)

        tE_w, dE_w = p.dma_step("g", [
            (wgq_sb[:, hc * ISW:(hc + 1) * ISW], wgq_s[hc * 128:(hc + 1) * 128, :])
            for hc in range(8)] + [
            (wuq_sb[:, hc * ISW:(hc + 1) * ISW], wuq_s[hc * 128:(hc + 1) * 128, :])
            for hc in range(8)], [pD_done["g"]], "wld", "E_w")

        def e_sg(sc):
            sc.activation(sgE[:, :], clipx_r[:, :], AF.Sigmoid, bias=0.0, scale=1.0)
        tE_sg = p.step("s", [pD_done["s"], dcc1b], e_sg, "E_sg")
        def e_sx(v):
            v.tensor_mul(sgE[:, :], sgE[:, :], gmax_x[:, :])
            v.drain()
            v.tensor_scalar_mul(sgE[:, :], sgE[:, :], 1.0 / QMAX)
        tE_sx = p.step("v", [tE_sg, dcc1b, pD_done["v"]], e_sx, "E_sx")
        def e_sxb(t):
            t.matmul(pf_b[:, 0:1], onesf_row[:, :], sgE[:, :], start=True, stop=True)
        tE_sxb = p.step("t", [tE_sx, pD_done["t"]], e_sxb, "E_sxb")
        def e_sxc(v):
            v.tensor_copy(sx_col[:, :], pf_b[:, 0:1])
            v.drain()
            v.reciprocal(rsx_col[:, :], sx_col[:, :])
            v.drain()
            v.tensor_scalar(sgc[:, :], wsc_g[:, :], sx_col[:, 0:1], None, OP.mult)
            v.tensor_scalar(suc[:, :], wsc_u[:, :], sx_col[:, 0:1], None, OP.mult)
            v.memset(hmx_col[:, :], 0.0)
        tE_sxc = p.step("v", [tE_sxb], e_sxc, "E_sxc")

        WE = min(512, NT)
        NTT5 = NT // WE
        t_hdma = None
        d_h_prev_by = {}
        t_mxE = tE_sxc
        t_prev_g = None
        t_prev_u = None
        t_inE = [None, None]
        prev_qE = [None, None]
        prev_lastmm = [None, None]
        for tts in range(NTT5):
            c0 = tts * WE
            eb = tts % 2
            xst = xstgE[:, eb * 4096:(eb + 1) * 4096]
            xq = Xq_bf[:, eb * 4096:(eb + 1) * 4096]
            in_deps = [pD_done["y"]]
            for tok in (t_inE[eb], prev_qE[eb]):
                if tok is not None:
                    in_deps.append(tok)
            t_in, d_in = p.dma_step("y", [
                (xst[:, hc * WE:(hc + 1) * WE],
                 xnT_s[hc * 128:(hc + 1) * 128, c0:c0 + WE]) for hc in range(8)],
                in_deps, "xin", f"E_in{tts}")
            t_inE[eb] = t_in
            def e_q(v, xst=xst, xq=xq):
                v.tensor_scalar(xst[:, 0:8 * WE], xst[:, 0:8 * WE], rsx_col[:, 0:1], MR,
                                OP.mult, OP.add)
                v.drain()
                v.tensor_scalar(xst[:, 0:8 * WE], xst[:, 0:8 * WE], -MR, -(QMAX + 1.0),
                                OP.add, OP.max)
                v.drain()
                v.tensor_scalar(xq[:, 0:8 * WE], xst[:, 0:8 * WE], QMAX, None, OP.min)
            qdeps = [d_in, tE_sxc]
            if prev_lastmm[eb] is not None:
                qdeps.append(prev_lastmm[eb])
            t_q = p.step("v", qdeps, e_q, f"E_q{tts}")
            prev_qE[eb] = t_q

            t_gmm = {}
            t_umm = {}
            t_sil = {}
            t_upe = {}
            t_hop = {}
            for ic in range(22):
                def gmm(t, ic=ic, xq=xq):
                    for hc in range(8):
                        t.matmul(pf_a[:, (ic % 2) * 512:(ic % 2) * 512 + WE],
                                 wgq_sb[:, hc * ISW + ic * 128:hc * ISW + (ic + 1) * 128],
                                 xq[:, hc * WE:(hc + 1) * WE],
                                 start=(hc == 0), stop=(hc == 7))
                deps = [t_q, dE_w]
                if ic >= 2:
                    deps.append(t_sil[ic - 2])
                if ic == 0 and t_prev_g:
                    deps.append(t_prev_g)
                    deps.append(t_prev_u)
                if ic == 1 and t_prev_g:
                    deps.append(t_prev_g)
                    deps.append(t_prev_u)
                t_gmm[ic] = p.step("t", deps, gmm, f"E_g{tts}_{ic}")
                def sil(sc, ic=ic):
                    sc.activation(silu_g[:, 0:WE], pf_a[:, (ic % 2) * 512:(ic % 2) * 512 + WE],
                                  AF.Silu, bias=0.0, scale=sgc[:, ic:ic + 1])
                t_sil[ic] = p.step("s", [t_gmm[ic], tE_sxc] +
                                   ([t_hop[ic - 1]] if ic >= 1 else []),
                                   sil, f"E_s{tts}_{ic}")
                def umm(t, ic=ic, xq=xq):
                    for hc in range(8):
                        t.matmul(pf_a[:, 1024 + (ic % 2) * 512:1024 + (ic % 2) * 512 + WE],
                                 wuq_sb[:, hc * ISW + ic * 128:hc * ISW + (ic + 1) * 128],
                                 xq[:, hc * WE:(hc + 1) * WE],
                                 start=(hc == 0), stop=(hc == 7))
                deps_u = [t_gmm[ic]]
                if ic >= 2:
                    deps_u.append(t_hop[ic - 2])
                t_umm[ic] = p.step("t", deps_u, umm, f"E_u{tts}_{ic}")
                def upe(v, ic=ic):
                    v.tensor_scalar(up_f[:, 0:WE], pf_a[:, 1024 + (ic % 2) * 512:
                                                        1024 + (ic % 2) * 512 + WE],
                                    suc[:, ic:ic + 1], None, OP.mult)
                deps_ue = [t_umm[ic], tE_sxc] + ([t_hop[ic - 1]] if ic >= 1 else [])
                t_upe[ic] = p.step("v", deps_ue, upe, f"E_ue{tts}_{ic}")
                def hop(v, ic=ic):
                    v.tensor_mul(hbuf[:, (ic % 2) * 512:(ic % 2) * 512 + WE],
                                 silu_g[:, 0:WE], up_f[:, 0:WE])
                    v.drain()
                    v.tensor_reduce(mxE[:, :], hbuf[:, (ic % 2) * 512:(ic % 2) * 512 + WE],
                                    axis=AX.X, op=OP.max, apply_absolute_value=True)
                    v.drain()
                    v.tensor_max(hmx_col[:, :], hmx_col[:, :], mxE[:, :])
                deps_h = [t_upe[ic], t_sil[ic], t_mxE]
                dkey = (tts, ic - 2) if ic >= 2 else (tts - 1, 20 + ic)
                if dkey in d_h_prev_by:
                    deps_h.append(d_h_prev_by[dkey])
                t_hop[ic] = p.step("v", deps_h, hop, f"E_h{tts}_{ic}")
                t_mxE = t_hop[ic]
                hdeps = [t_hop[ic]] + ([t_hdma] if t_hdma else [])
                t_hdma, d_hd = p.dma_step("g", [
                    (h_s[ic * 128:(ic + 1) * 128, c0:c0 + WE],
                     hbuf[:, (ic % 2) * 512:(ic % 2) * 512 + WE])], hdeps,
                    "out", f"E_hd{tts}_{ic}")
                d_h_prev_by[(tts, ic)] = d_hd
            t_prev_g = t_sil[21]
            t_prev_u = t_upe[21]
            prev_lastmm[eb] = t_umm[21]

        def e_mxmm(t):
            t.matmul(pf_b[0:1, 2:130], hmx_col[:, :], identf[:, :],
                     start=True, stop=True)
        t_em = p.step("t", [t_mxE, t_prev_g, t_prev_u], e_mxmm, "E_mxmm")
        def e_mxv(v):
            v.tensor_reduce(gmax_h[:, :], pf_b[0:1, 2:130], axis=AX.X, op=OP.max)
        t_ev2 = p.step("v", [t_em], e_mxv, "E_mxv")
        tcc2, dcc2_out = p.dma_step("g", [(cchi.ap()[0:1, 0:1], gmax_h[:, :])],
                                    [t_ev2], "ccio", "E_ccout")
        if cfg.get("skip_cc"):
            def cc2(g):
                g.dma_start(out=ccho.ap()[:, :], in_=cchi.ap()[:, :],
                            ).then_inc(cc_sem, 16)
        else:
            def cc2(g):
                g.collective_compute(
                    "AllReduce", OP.max, replica_groups=RG,
                    ins=[cchi.ap().opt()], outs=[ccho.ap().opt()],
                ).then_inc(cc_sem, 1)
        t_cc2 = p.step("g", [tcc2, dcc2_out], cc2, "E_cc2")
        p.dsems["cc0"][1] += cc_inc
        dcc2 = DmaTok((("cc0", 2 * cc_inc),))
        tcc2b, dcc2b = p.dma_step("g", [(gmax_h[:, :], ccho.ap()[0:1, 0:1])],
                                  [t_cc2, dcc2], "ccio", "E_ccin")

        pE_done = p.barrier("pE_done")
        arenaE.close()

    if "F" in phases:
        arenaF, AF_ = mkarena()
        wdq_sb = AF_("wdq_sb", [128, 22 * C], BF16)
        hstgF = AF_("hstgF", [128, 2 * 2816], F32)
        Hq_bf = AF_("Hq_bf", [128, 2 * 2816], BF16)
        ptF = AF_("ptF", [128, 2 * C], F32)
        otF = AF_("otF", [128, 2 * C], F32)
        swd_bc = AF_("swd_bc", [128, C], F32)
        swrow = AF_("swrow", [1, C], F32)
        shE = AF_("shE", [1, 1], F32)

        tF_w, dF_w = p.dma_step("g", [
            (wdq_sb[:, ic * C:(ic + 1) * C], wdq_s[ic * 128:(ic + 1) * 128, :])
            for ic in range(22)], [pE_done["g"]], "wld", "F_w")

        def f_sg(sc):
            sc.activation(shE[:, :], clipd_r[:, :], AF.Sigmoid, bias=0.0, scale=1.0)
        tF_sg = p.step("s", [pE_done["s"], dcc2b], f_sg, "F_sg")
        def f_sx(v):
            v.tensor_mul(shE[:, :], shE[:, :], gmax_h[:, :])
            v.drain()
            v.tensor_scalar_mul(shE[:, :], shE[:, :], 1.0 / QMAX)
        tF_sx = p.step("v", [tF_sg, dcc2b, pE_done["v"]], f_sx, "F_sx")
        def f_sxb(t):
            t.matmul(pf_b[:, 1024:1025], onesf_row[:, :], shE[:, :],
                     start=True, stop=True)
            for hc in range(8):
                t.matmul(pf_b[0:1, hc * 128:(hc + 1) * 128],
                         wsc_d[:, hc:hc + 1], identf[:, :], start=True, stop=True)
        tF_sxb = p.step("t", [tF_sx, pE_done["t"]], f_sxb, "F_sxb")
        def f_sxc(v):
            v.tensor_copy(sh_col[:, :], pf_b[:, 1024:1025])
            v.drain()
            v.reciprocal(rsh_col[:, :], sh_col[:, :])
            v.tensor_scalar(swrow[:, :], pf_b[0:1, 0:1024], sh_col[0:1, 0:1],
                            None, OP.mult)
        tF_sxc = p.step("v", [tF_sxb], f_sxc, "F_sxc")
        def f_swb(t):
            for hh in range(2):
                t.matmul(pf_a[:, hh * 512:(hh + 1) * 512], onesf_row[:, :],
                         swrow[:, hh * 512:(hh + 1) * 512], start=True, stop=True)
        tF_swb = p.step("t", [tF_sxc], f_swb, "F_swb")
        def f_swe(v):
            v.tensor_copy(swd_bc[:, :], pf_a[:, 0:1024])
        tF_swe = p.step("v", [tF_swb], f_swe, "F_swe")

        t_ydma = None
        t_inF = [None, None]
        prev_q = [None, None]
        prev_o = [None, None]
        prev_dy = [[], []]
        for tc in range(NT // 128):
            bb = tc % 2
            hst = hstgF[:, bb * 2816:(bb + 1) * 2816]
            hq = Hq_bf[:, bb * 2816:(bb + 1) * 2816]
            ptv = ptF[:, bb * C:(bb + 1) * C]
            otv = otF[:, bb * C:(bb + 1) * C]
            psl = pf_a[:, bb * 1024:(bb + 1) * 1024]
            in_deps = [pE_done["y"]]
            for tok in (t_inF[bb], prev_q[bb], prev_o[bb]):
                if tok is not None:
                    in_deps.append(tok)
            t_in, d_in = p.dma_step("y", [
                (hst[:, ic * 128:(ic + 1) * 128],
                 h_s[ic * 128:(ic + 1) * 128, tc * 128:(tc + 1) * 128])
                for ic in range(22)] + [
                (ptv, pt_s[tc * 128:(tc + 1) * 128, :])],
                in_deps, "xin", f"F_in{tc}")
            t_inF[bb] = t_in
            def f_q(v, hst=hst, hq=hq):
                v.tensor_scalar(hst, hst, rsh_col[:, 0:1], MR, OP.mult, OP.add)
                v.drain()
                v.tensor_scalar(hst, hst, -MR, -(QMAX + 1.0), OP.add, OP.max)
                v.drain()
                v.tensor_scalar(hq, hst, QMAX, None, OP.min)
            t_q = p.step("v", [d_in, tF_sxc, tF_swe], f_q, f"F_q{tc}")
            prev_q[bb] = t_q
            def f_mm(t, hq=hq, psl=psl):
                for hh in range(2):
                    for ic in range(22):
                        t.matmul(psl[:, hh * 512:(hh + 1) * 512],
                                 hq[:, ic * 128:(ic + 1) * 128],
                                 wdq_sb[:, ic * C + hh * 512:ic * C + (hh + 1) * 512],
                                 start=(ic == 0), stop=(ic == 21))
            deps_m = [t_q, dF_w, tF_swe]
            if prev_o[bb] is not None:
                deps_m.append(prev_o[bb])
            t_mm = p.step("t", deps_m, f_mm, f"F_mm{tc}")
            def f_out(v, otv=otv, ptv=ptv, psl=psl):
                v.tensor_mul(otv, psl, swd_bc[:, :])
                v.drain()
                v.tensor_add(otv, otv, ptv)
            t_o = p.step("v", [t_mm] + prev_dy[bb], f_out, f"F_o{tc}")
            prev_o[bb] = t_o
            ydeps = [t_o] + ([t_ydma] if t_ydma else [])
            t_ydma, d_yd = p.dma_step("g", [(y_out[tc * 128:(tc + 1) * 128, :],
                                             otv)], ydeps, "out", f"F_y{tc}")
            prev_dy[bb] = [d_yd]
    final = []
    for nm_ in ("out_g", "out_y", "dbg_g"):
        if nm_ in p.dlanes:
            waits = tuple((k, p.dsems[k][1]) for k in p.dlanes[nm_]["keys"]
                          if p.dsems[k][1] > 0)
            final.append(("g", DmaTok(waits)))
    p.finalize(final)
    return nc, dbg_outs



def shard_inputs(inputs, n_cores=8):
    x = np.ascontiguousarray(inputs["L0_attn_x"])
    Bfull, T, C = x.shape
    Bper = Bfull // n_cores
    asc = np.ascontiguousarray

    def r1(a):
        return asc(np.asarray(a, dtype=np.float32).reshape(1, -1))

    common = dict(
        wqT=asc(np.asarray(inputs["L0_attn_wq"]).T),
        wkT=asc(np.asarray(inputs["L0_attn_wk"]).T),
        wvT=asc(np.asarray(inputs["L0_attn_wv"]).T),
        woT=asc(np.asarray(inputs["L0_attn_wo"]).T),
        ln1w=r1(inputs["L0_attn_ln1_w"]), ln1b=r1(inputs["L0_attn_ln1_b"]),
        ln2w=r1(inputs["L0_attn_ln2_w"]), ln2b=r1(inputs["L0_attn_ln2_b"]),
        lr=r1(inputs["L0_attn_lr_scale"]),
        w1T=asc(np.asarray(inputs["L0_attn_w1"]).T), b1=r1(inputs["L0_attn_b1"]),
        w2T=asc(np.asarray(inputs["L0_attn_w2"]).T), b2=r1(inputs["L0_attn_b2"]),
        fWin=asc(np.asarray(inputs["L0_ffn_Win"])), fbin=r1(inputs["L0_ffn_bin_"]),
        fW1=r1(inputs["L0_ffn_W1"]), fb1=r1(inputs["L0_ffn_b1"]),
        fW2=r1(inputs["L0_ffn_W2"]), fb2=r1(inputs["L0_ffn_b2"]),
        fWout=asc(np.asarray(inputs["L0_ffn_Wout"])), fbout=r1(inputs["L0_ffn_bout"]),
        caT=asc(np.asarray(inputs["L1_attn_c_attn_w"]).T), cab=r1(inputs["L1_attn_c_attn_b"]),
        q2lT=asc(np.asarray(inputs["L1_attn_q2l_w"]).T), q2lb=r1(inputs["L1_attn_q2l_b"]),
        kv2lT=asc(np.asarray(inputs["L1_attn_kv2l_w"]).T), kv2lb=r1(inputs["L1_attn_kv2l_b"]),
        l2oT=asc(np.asarray(inputs["L1_attn_l2o_w"]).T), l2ob=r1(inputs["L1_attn_l2o_b"]),
        wqkT=asc(np.asarray(inputs["L1_attn_wqk_w"]).T), wqkb=r1(inputs["L1_attn_wqk_b"]),
        cpT=asc(np.asarray(inputs["L1_attn_cproj_w"]).T), cpb=r1(inputs["L1_attn_cproj_b"]),
        clipx=r1(inputs["L1_ffn_clip_x"]), clipd=r1(inputs["L1_ffn_clip_down"]),
        wg=asc(np.asarray(inputs["L1_ffn_w_gate"])),
        wu=asc(np.asarray(inputs["L1_ffn_w_up"])),
        wd=asc(np.asarray(inputs["L1_ffn_w_down"])),
    )
    in_maps = []
    for c in range(n_cores):
        m = dict(common)
        m["x"] = asc(x[c * Bper:(c + 1) * Bper].reshape(Bper * T, C))
        in_maps.append(m)
    return in_maps, (Bper, T, C)


def kernel(**inputs):
    import os
    n_cores = 8
    in_maps, (Bper, T, C) = shard_inputs(inputs, n_cores)
    cfg = dict(T=T, B=Bper, n_cores=n_cores, debug=False,
               phases=os.environ.get("KPHASES", "ABCDEF"),
               skip_cc=bool(os.environ.get("KSKIPCC")))
    nc, _ = build_program(cfg)
    res = run_bass_kernel_spmd(nc, in_maps, core_ids=list(range(n_cores)))
    outs = [r["y"].reshape(Bper, T, C) for r in res.results]
    return np.concatenate(outs, axis=0)


# version 18 (final)
# speedup vs baseline: 1.0530x; 1.0009x over previous
import math
from contextlib import ExitStack
import numpy as np

import concourse.bass as bass
import concourse.mybir as mybir
from concourse.alu_op_type import AluOpType as OP
from concourse.bass_utils import run_bass_kernel_spmd

F32 = mybir.dt.float32
BF16 = mybir.dt.bfloat16
AF = mybir.ActivationFunctionType
AX = mybir.AxisListType

MR = 1.5 * 2.0**23
LN_EPS = 1e-5
INNER_LR = 0.01
QMAX = 127.0

ENGS = ("t", "v", "s", "g", "y")


class DmaTok:
    __slots__ = ("waits",)
    def __init__(self, waits):
        self.waits = waits


class StepTok:
    __slots__ = ("eng", "cnt")
    def __init__(self, eng, cnt):
        self.eng, self.cnt = eng, cnt


class Step:
    __slots__ = ("eng", "deps", "fn", "tag")
    def __init__(self, eng, deps, fn, tag):
        self.eng, self.deps, self.fn, self.tag = eng, deps, fn, tag


class Prog:

    def __init__(self, nc):
        self.nc = nc
        self.eng = {"t": nc.tensor, "v": nc.vector, "s": nc.scalar,
                    "g": nc.gpsimd, "y": nc.sync}
        self.ssem = {e: nc.alloc_semaphore(f"step_{e}") for e in ENGS}
        self.scnt = {e: 0 for e in ENGS}
        self.steps = []
        self.dsems = {}
        self.dlanes = {}

    N_LANES = 4

    def dma_lanes(self, name):
        if name not in self.dlanes:
            self.dlanes[name] = {
                "keys": [], "rr": 0,
            }
            for i in range(self.N_LANES):
                key = f"{name}{i}"
                self.dsems[key] = [self.nc.alloc_semaphore(f"dma_{key}"), 0]
                self.dlanes[name]["keys"].append(key)
        return self.dlanes[name]

    def step(self, eng, deps, fn, tag=""):
        for d in deps:
            assert d is not None, f"None dep in {tag}"
            if isinstance(d, StepTok):
                assert d.cnt <= self.scnt[d.eng], f"forward dep in {tag}"
        self.scnt[eng] += 1
        tok = StepTok(eng, self.scnt[eng])
        self.steps.append(Step(eng, list(deps), fn, tag))
        return tok

    def dma_step(self, qeng, transfers, deps, sem_name, tag=""):
        sem_name = f"{sem_name}_{qeng}"
        lanes = self.dma_lanes(sem_name)
        plan = []
        waits = {}
        for _ in transfers:
            key = lanes["keys"][lanes["rr"] % self.N_LANES]
            lanes["rr"] += 1
            rec = self.dsems[key]
            plan.append((rec[0], rec[1]))
            rec[1] += 16
            waits[key] = rec[1]
        tlist = list(transfers)
        def fn(e):
            for (o, i_), (sem, prev) in zip(tlist, plan):
                if prev > 0:
                    e.wait_ge(sem, prev)
                e.dma_start(out=o, in_=i_).then_inc(sem, 16)
        tok = self.step(qeng, deps, fn, tag)
        return tok, DmaTok(tuple(waits.items()))

    def tok(self, eng):
        return StepTok(eng, self.scnt[eng])

    def barrier(self, tag="bar"):
        toks = {e: self.tok(e) for e in ENGS}
        dwaits = tuple((k, rec[1]) for k, rec in self.dsems.items() if rec[1] > 0)
        dtok = DmaTok(dwaits)
        out = {}
        for e in ENGS:
            deps = [toks[o] for o in ENGS if o != e] + [dtok]
            out[e] = self.step(e, deps, lambda _e: None, tag=tag)
        return out

    def finalize(self, final_waits):
        nc = self.nc
        per_eng = {e: [] for e in ENGS}
        for st in self.steps:
            per_eng[st.eng].append(st)
        hwm = {e: {} for e in ENGS}

        def emit_for(ename):
            e = self.eng[ename]
            marks = hwm[ename]
            own = 0
            for st in per_eng[ename]:
                if own > 0:
                    e.wait_ge(self.ssem[ename], own)
                for d in st.deps:
                    if isinstance(d, StepTok):
                        if d.eng == ename:
                            continue
                        pairs = [(self.ssem[d.eng], ("s", d.eng), d.cnt)]
                    else:
                        pairs = [(self.dsems[k][0], ("d", k), v)
                                 for (k, v) in d.waits]
                    for sem, key, val in pairs:
                        if marks.get(key, -1) < val:
                            e.wait_ge(sem, val)
                            marks[key] = val
                st.fn(e)
                own += 1
                e.drain()
                e.sem_inc(self.ssem[ename], 1)
            for (feng, dtok) in final_waits:
                if feng == ename:
                    for (k, v) in dtok.waits:
                        e.wait_ge(self.dsems[k][0], v)

        with nc.Block() as block:
            block.tensor(lambda e: emit_for("t"))
            block.vector(lambda e: emit_for("v"))
            block.scalar(lambda e: emit_for("s"))
            block.gpsimd(lambda e: emit_for("g"))
            block.sync(lambda e: emit_for("y"))



def build_program(cfg):
    T = cfg["T"]
    B = cfg["B"]
    NPOS = T // 128
    NT = B * T
    C, H, D, L = 1024, 16, 64, 32
    ISW = 2816
    n_cores = cfg["n_cores"]
    RG = [list(range(n_cores))]
    dbg = cfg.get("debug", False)
    phases = cfg.get("phases", "ABCDEF")

    nc = bass.Bass("TRN2", target_bir_lowering=False,
                   dynamic_dma_scratch_size=16384)

    def inp(name, shape, dt=F32):
        return nc.dram_tensor(name, list(shape), dt, kind="ExternalInput").ap()

    x_in = inp("x", (NT, C))
    wqT = inp("wqT", (C, C)); wkT = inp("wkT", (C, C)); wvT = inp("wvT", (C, C))
    woT = inp("woT", (C, C))
    ln1w = inp("ln1w", (1, C)); ln1b = inp("ln1b", (1, C))
    ln2w = inp("ln2w", (1, C)); ln2b = inp("ln2b", (1, C))
    lr_in = inp("lr", (1, H))
    w1T = inp("w1T", (C, 4 * C)); b1_in = inp("b1", (1, 4 * C))
    w2T = inp("w2T", (4 * C, C)); b2_in = inp("b2", (1, C))
    fWin = inp("fWin", (C, D)); fbin = inp("fbin", (1, D))
    fW1 = inp("fW1", (1, 1024)); fb1 = inp("fb1", (1, 1024))
    fW2 = inp("fW2", (1, 1024)); fb2 = inp("fb2", (1, D))
    fWout = inp("fWout", (D, C)); fbout = inp("fbout", (1, C))
    caT = inp("caT", (C, 3 * C)); cab = inp("cab", (1, 3 * C))
    q2lT = inp("q2lT", (D, L)); q2lb = inp("q2lb", (1, L))
    kv2lT = inp("kv2lT", (D, L)); kv2lb = inp("kv2lb", (1, L))
    l2oT = inp("l2oT", (L, D)); l2ob = inp("l2ob", (1, D))
    wqkT = inp("wqkT", (L, L)); wqkb = inp("wqkb", (1, L))
    cpT = inp("cpT", (C, C)); cpb = inp("cpb", (1, C))
    clipx = inp("clipx", (1, 1)); clipd = inp("clipd", (1, 1))
    wg_in = inp("wg", (ISW, C)); wu_in = inp("wu", (ISW, C))
    wd_in = inp("wd", (C, ISW))

    y_out = nc.dram_tensor("y", [NT, C], F32, kind="ExternalOutput").ap()

    x1_s = nc.dram_tensor("x1_s", [NT, C], F32).ap()
    u_s = nc.dram_tensor("u_s", [NT, C], F32).ap()
    pa_s = nc.dram_tensor("pa_s", [NT, C], F32).ap()
    pt_s = nc.dram_tensor("pt_s", [NT, C], F32).ap()
    xnT_s = nc.dram_tensor("xnT_s", [C, NT], F32).ap()
    h_s = nc.dram_tensor("h_s", [ISW, NT], F32).ap()
    wgq_s = nc.dram_tensor("wgq_s", [C, ISW], BF16).ap()
    wuq_s = nc.dram_tensor("wuq_s", [C, ISW], BF16).ap()
    wdq_s = nc.dram_tensor("wdq_s", [ISW, C], BF16).ap()
    ccxi = nc.dram_tensor("ccxi", [1, 1], F32)
    ccxo = nc.dram_tensor("ccxo", [1, 1], F32)
    cchi = nc.dram_tensor("cchi", [1, 1], F32)
    ccho = nc.dram_tensor("ccho", [1, 1], F32)

    dbg_outs = {}
    def dbg_on(name):
        if dbg is True:
            return True
        return isinstance(dbg, str) and name.split("_")[0] in dbg.split(",")
    def dbg_out(name, shape):
        if dbg_on(name):
            dbg_outs[name] = nc.dram_tensor(name, list(shape), F32,
                                            kind="ExternalOutput").ap()
            return dbg_outs[name]
        return None

    p = Prog(nc)
    A = nc.alloc_sbuf_tensor

    def mkarena():
        st = ExitStack()
        def alloc(name, shape, dt):
            return st.enter_context(nc.sbuf_tensor(name, shape, dt, side="right"))
        return st, alloc

    cpool = {}
    def creg(val):
        val = float(val)
        if val not in cpool:
            t = A(f"cst{len(cpool)}", [128, 1], F32)
            cpool[val] = t
            nc.const_aps.register(F32, val, t[:, :])
        return cpool[val]
    creg(LN_EPS)
    creg(1.0 / (2.0 * math.pi))
    creg(math.pi)
    creg(-math.pi)
    creg(math.pi / 2.0)
    creg(float(L) ** -0.5)
    creg(1.0 / QMAX)
    p0_arena = ExitStack()
    def A0(name, shape, dt):
        return p0_arena.enter_context(nc.sbuf_tensor(name, shape, dt, side="right"))

    ident = A("ident", [128, 128], BF16)
    identf = A("identf", [128, 128], F32)
    ones_row = A("ones_row", [1, 128], BF16)
    onesf_row = A("onesf_row", [1, 128], F32)
    one_f = A("one_f", [1, 1], F32)
    maskLTn = A("maskLTn", [128, 128], BF16)
    maskUTn = A("maskUTn", [128, 128], BF16)
    maskUT = A("maskUT", [128, 128], BF16)
    maskCa = A("maskCa", [128, 128], BF16)
    eta_bc = A("eta_bc", [128, H], F32)

    arenaA, AA = mkarena()
    g1bc = AA("g1bc", [128, C], F32); be1bc = AA("be1bc", [128, C], F32)
    fW1bc = AA("fW1bc", [128, 1024], F32); fb1bc = AA("fb1bc", [128, 1024], F32)
    fW2bc = AA("fW2bc", [128, 1024], F32)
    ffb_bf = AA("ffb_bf", [1, C], BF16)
    fbin_bc = AA("fbin_bc", [128, D], F32)
    ffb_bc = AA("ffb_bc", [128, C], F32)

    wq_sb = AA("wq_sb", [128, 8 * C], BF16)
    wk_sb = AA("wk_sb", [128, 8 * C], BF16)
    wv_sb = AA("wv_sb", [128, 8 * C], BF16)
    wo_sb = AA("wo_sb", [64, 16 * C], BF16)
    Win_bf = AA("Win_bf", [128, 8 * D], BF16)
    Wout_bf = AA("Wout_bf", [64, C], BF16)
    xa = AA("xa", [128, 2 * C], F32)
    xn_f = AA("xn_f", [128, 2 * C], F32)
    xn_bf = AA("xn_bf", [128, 2 * C], BF16)
    h1_bf = AA("h1_bf", [128, 2 * C], BF16)
    xnT_bf = AA("xnT_bf", [128, C], BF16)
    h1T_bf = AA("h1T_bf", [128, C], BF16)
    QT_sb = AA("QT_sb", [128, C], BF16)
    KT_sb = AA("KT_sb", [128, C], BF16)
    V_sb = AA("V_sb", [128, C], F32)
    Km_sb = AA("Km_sb", [128, C], BF16)
    WTst = AA("WTst", [128, B * 512], F32)
    WTbf = AA("WTbf", [128, B * 1024], BF16)
    Bc0 = AA("Bc0", [128, 2048], BF16)
    Nc0 = AA("Nc0", [128, 2048], BF16)
    Bc1 = AA("Bc1", [128, 2048], BF16)
    Nc1 = AA("Nc1", [128, 2048], BF16)
    Hbar = AA("Hbar", [128, 2048], BF16)
    c_f = AA("c_f", [128, 1024], F32)
    c_bf = AA("c_bf", [128, 1024], BF16)
    ct_bf = c_bf
    OutT_bf = AA("OutT_bf", [64, 2048], BF16)
    u_f = AA("u_f", [128, C], F32)
    pa_f = AA("pa_f", [128, C], F32)
    s0_f = AA("s0_f", [128, D], F32)
    hf0 = AA("hf0", [128, 1024], F32)
    h1w = hf0
    g0_bf = AA("g0_bf", [128, D], BF16)
    gT_bf = AA("gT_bf", [64, 128], BF16)
    bns = AA("bns", [128, 12], F32)
    ag2 = AA("ag2", [128, 2], F32)
    rstd0 = AA("rstd0", [128, 1], F32)
    nmr0 = AA("nmr0", [128, 1], F32)
    rstd1 = AA("rstd1", [128, 1], F32)
    nmr1 = AA("nmr1", [128, 1], F32)

    clipx_r = A("clipx_r", [1, 1], F32)
    clipd_r = A("clipd_r", [1, 1], F32)
    fbin_bf = A("fbin_bf", [1, D], BF16)
    lr_row = A0("lr_row", [1, H], F32)
    lr_bf = A0("lr_bf", [1, H], BF16)
    fbout_r = A0("fbout_r", [1, C], F32)
    fb2_bf = A0("fb2_bf", [1, D], BF16)


    pf_a = nc.alloc_psum_tensor("pf_a", [128, 2048], F32)
    pf_b = nc.alloc_psum_tensor("pf_b", [128, 1536], F32)
    pbf = nc.alloc_psum_tensor("pbf", [128, 1024], BF16)

    def p0_memset(g):
        for val, t in cpool.items():
            g.memset(t[:], val)
        g.memset(ident[:], 0.0)
        g.memset(identf[:], 0.0)
        g.memset(onesf_row[:], 1.0)
        g.memset(ones_row[:], 1.0)
        g.memset(one_f[:], 1.0)
        g.memset(maskLTn[:], 0.0)
        g.memset(maskUTn[:], 0.0)
        g.memset(maskUT[:], 0.0)
        g.memset(maskCa[:], 0.0)
    tk = p.step("g", [], p0_memset, "p0_memset")

    def p0_masks(g):
        sel = g.affine_select
        sel(out=ident[:], in_=ident[:], compare_op=OP.not_equal, fill=1.0,
            base=0, pattern=[[-1, 128]], channel_multiplier=1)
        sel(out=identf[:], in_=identf[:], compare_op=OP.not_equal, fill=1.0,
            base=0, pattern=[[-1, 128]], channel_multiplier=1)
        sel(out=maskLTn[:], in_=maskLTn[:], compare_op=OP.is_ge, fill=-1.0,
            base=0, pattern=[[1, 128]], channel_multiplier=-1)
        sel(out=maskUTn[:], in_=maskUTn[:], compare_op=OP.is_ge, fill=-1.0,
            base=0, pattern=[[-1, 128]], channel_multiplier=1)
        sel(out=maskUT[:], in_=maskUT[:], compare_op=OP.is_ge, fill=1.0,
            base=0, pattern=[[-1, 128]], channel_multiplier=1)
        sel(out=maskCa[:], in_=maskCa[:], compare_op=OP.is_gt, fill=1.0,
            base=0, pattern=[[-1, 128]], channel_multiplier=1)
    t_mask = p.step("g", [tk], p0_masks, "p0_masks")

    f0row_bf = A0("f0row_bf", [1, 1024], BF16)
    t_rows, d_rows = p.dma_step(
        "y", [(lr_row[:, :], lr_in[:, :]), (fbout_r[:, :], fbout[:, :]),
              (clipx_r[:, :], clipx[:, :]), (clipd_r[:, :], clipd[:, :])],
        [], "cst", "p0_rows")
    t_rows2, d_rows2 = p.dma_step(
        "g", [(fbin_bf[:, :], fbin[:, :]), (fb2_bf[:, :], fb2[:, :])],
        [t_mask], "cst2", "p0_rows2")

    def p0_eta_scale(v):
        v.tensor_scalar_mul(lr_bf[:, :], lr_row[:, :], INNER_LR)
    t_etas = p.step("v", [d_rows], p0_eta_scale, "eta_scale")

    bc_list = [(g1bc, f0row_bf, 0, ln1w), (be1bc, f0row_bf, 0, ln1b),
               (fW1bc, f0row_bf, 0, fW1), (fb1bc, f0row_bf, 0, fb1),
               (fW2bc, f0row_bf, 0, fW2)]
    t_bc_prev = None
    t_bm = None
    for bi, (dst, srcrow, off, hsrc) in enumerate(bc_list):
        deps = [d_rows2, t_mask, t_etas] if bi == 0 else [t_bc_prev]
        if hsrc is not None:
            ld_deps = [t_mask] + ([t_bm] if t_bm else [])
            _, d_f0 = p.dma_step("g", [(f0row_bf[:, :], hsrc[:, :])],
                                 ld_deps, "cst2", f"p0_f0row{bi}")
            deps = deps + [d_f0]
        def mk_mm(srcrow=srcrow, off=off):
            def fn(t):
                for hh in range(2):
                    t.matmul(pf_a[:, hh * 512:(hh + 1) * 512], ones_row[:, :],
                             srcrow[:, off + hh * 512:off + (hh + 1) * 512],
                             start=True, stop=True)
            return fn
        t_bm = p.step("t", deps, mk_mm(), f"bc_mm{bi}")
        def mk_ev(dst=dst):
            def fn(v):
                v.tensor_copy(dst[:, :], pf_a[:, 0:1024])
            return fn
        t_bc_prev = p.step("v", [t_bm], mk_ev(), f"bc_ev{bi}")

    def p0_eta_mm(t):
        t.matmul(pf_b[:, 0:H], ones_row[:, :], lr_bf[:, :],
                 start=True, stop=True)
    t_etam = p.step("t", [t_etas, t_mask], p0_eta_mm, "eta_mm")
    def p0_eta_ev(v):
        v.tensor_copy(eta_bc[:, :], pf_b[:, 0:H])
    t_eta = p.step("v", [t_etam], p0_eta_ev, "eta_ev")


    wload_transfers = []
    for ic in range(8):
        wload_transfers.append((wq_sb[:, ic * C:(ic + 1) * C], wqT[ic * 128:(ic + 1) * 128, :]))
        wload_transfers.append((wk_sb[:, ic * C:(ic + 1) * C], wkT[ic * 128:(ic + 1) * 128, :]))
        wload_transfers.append((wv_sb[:, ic * C:(ic + 1) * C], wvT[ic * 128:(ic + 1) * 128, :]))
        wload_transfers.append((Win_bf[:, ic * D:(ic + 1) * D], fWin[ic * 128:(ic + 1) * 128, :]))
    for cc in range(16):
        wload_transfers.append((wo_sb[:, cc * C:(cc + 1) * C], woT[cc * 64:(cc + 1) * 64, :]))
    wload_transfers.append((Wout_bf[:, :], fWout[:, :]))
    t_wl, d_wl = p.dma_step("g", wload_transfers, [t_mask], "wld", "L0_wload")

    fb2col_bf = A0("fb2col_bf", [64, 1], BF16)
    def p0_fb2col_mm(t):
        t.matmul(pf_b[0:64, 1056:1057], fb2_bf[:, :], ident[0:1, 0:1],
                 start=True, stop=True)
    t_f2m = p.step("t", [d_rows2, t_etam], p0_fb2col_mm, "fb2col_mm")
    def p0_fb2col_ev(v):
        v.tensor_copy(fb2col_bf[:, :], pf_b[0:64, 1056:1057])
    t_f2c = p.step("v", [t_f2m], p0_fb2col_ev, "fb2col_ev")
    def p0_ffb_mm(t):
        for hh in range(2):
            t.matmul(pf_b[0:1, hh * 512:(hh + 1) * 512], fb2col_bf[:, :],
                     Wout_bf[:, hh * 512:(hh + 1) * 512], start=True, stop=True)
    t_ffm = p.step("t", [t_f2c, d_wl, t_eta], p0_ffb_mm, "ffb_mm")
    def p0_ffb_ev(v):
        v.tensor_add(fbout_r[:, :], pf_b[0:1, 0:1024], fbout_r[:, :])
        v.drain()
        v.tensor_copy(ffb_bf[:, :], fbout_r[:, :])
    t_ffb = p.step("v", [t_ffm, d_rows], p0_ffb_ev, "ffb_ev")

    def p0_bc2_mm(t):
        t.matmul(pf_b[:, 64:128], ones_row[:, :], fbin_bf[:, :], start=True, stop=True)
        for hh in range(2):
            t.matmul(pf_a[:, hh * 512:(hh + 1) * 512], ones_row[:, :],
                     ffb_bf[:, hh * 512:(hh + 1) * 512], start=True, stop=True)
    t_bc2 = p.step("t", [t_ffb, t_bc_prev, d_rows2], p0_bc2_mm, "p0_bc2")
    def p0_bc2_ev(v):
        v.tensor_copy(fbin_bc[:, :], pf_b[:, 64:64 + D])
        v.tensor_copy(ffb_bc[:, :], pf_a[:, 0:1024])
    t_bc2e = p.step("v", [t_bc2], p0_bc2_ev, "p0_bc2e")

    p0_done = p.barrier("p0_done")
    p0_arena.close()


    u_dbg = dbg_out("u_dbg", (NT, C))
    pa_dbg = dbg_out("pa_dbg", (NT, C))

    def a_state0(g):
        g.memset(WTst[:], 0.0)
        g.memset(WTbf[:], 0.0)
    t_st0 = p.step("g", [p0_done["g"]], a_state0, "state0")
    t_stbf = t_st0
    t_stf = t_st0

    ln_chain = []

    def ln_steps(x_ap, deps, rstd, nmr, tag):
        def st(v):
            v.bn_stats(bns[:, 0:6], x_ap[:, 0:512])
            v.bn_stats(bns[:, 6:12], x_ap[:, 512:1024])
            v.drain()
            v.bn_aggr(ag2[:, :], bns[:, :])
        t1 = p.step("v", deps, st, f"{tag}_stats")
        def rs(s):
            s.activation(rstd[:, :], ag2[:, 1:2], AF.Ln, bias=LN_EPS, scale=1.0)
            s.drain()
            s.activation(rstd[:, :], rstd[:, :], AF.Exp, bias=0.0, scale=-0.5)
        t2 = p.step("s", [t1], rs, f"{tag}_rstd")
        def nm(v):
            v.tensor_scalar(nmr[:, :], ag2[:, 0:1], rstd[:, 0:1], -1.0,
                            OP.mult, OP.mult)
        t3 = p.step("v", [t2], nm, f"{tag}_nmr")
        return t3

    dx_toks = {}
    t_px, dx_toks[(0, 0)] = p.dma_step(
        "y", [(xa[:, 0:C], x_in[0:128, :])], [p0_done["y"]], "xin", "xin00")
    last_xin = t_px

    t_prev_ttt_tail = None
    t_prev_pbf = None
    t_u_dma = None
    t_pa_dma = None
    d_u_prev = []
    d_pa_prev = []
    wt_war = {b_: t_st0 for b_ in range(B)}
    pa_toks = {}
    ln_toks = {}

    def emit_front(gpos):
        par = gpos % 2
        b_ = gpos % B
        pos_ = gpos // B
        xcur = xa[:, par * C:(par + 1) * C]
        xnv = xn_f[:, par * C:(par + 1) * C]
        xnb = xn_bf[:, par * C:(par + 1) * C]
        h1b = h1_bf[:, par * C:(par + 1) * C]
        dx = dx_toks[(b_, pos_)]
        t_ln0 = ln_steps(xcur, [dx], rstd0, nmr0, f"ln0_{gpos}")
        def ap0(v, xcur=xcur, xnv=xnv, xnb=xnb):
            v.tensor_scalar(xnv, xcur, rstd0[:, 0:1], nmr0[:, 0:1],
                            OP.mult, OP.add)
            v.tensor_scalar(xnb, xcur, rstd0[:, 0:1], nmr0[:, 0:1],
                            OP.mult, OP.add)
        t_xn = p.step("v", [t_ln0], ap0, f"xn_{gpos}")
        t_ln1 = ln_steps(xnv, [t_xn], rstd1, nmr1, f"ln1_{gpos}")
        def ap1(v, xnv=xnv):
            v.tensor_scalar(h1w[:, :], xnv, rstd1[:, 0:1], nmr1[:, 0:1],
                            OP.mult, OP.add)
        t_h1r = p.step("v", [t_ln1], ap1, f"h1raw_{gpos}")
        def gb1(v, h1b=h1b):
            v.tensor_mul(h1w[:, :], h1w[:, :], g1bc[:, :])
            v.drain()
            v.tensor_add(h1w[:, :], h1w[:, :], be1bc[:, :])
            v.drain()
            v.tensor_copy(h1b, h1w[:, :])
        t_h1 = p.step("v", [t_h1r], gb1, f"h1_{gpos}")
        ln_toks[gpos] = (t_xn, t_h1)

    if "A" in phases:
      for pos in range(NPOS):
        for b in range(B):
            gpos = pos * B + b
            buf = gpos % 2
            row0 = b * T + pos * 128
            nb, npos_ = (b + 1, pos) if b + 1 < B else (0, pos + 1)
            if npos_ < NPOS:
                nrow = nb * T + npos_ * 128
                nbuf = (gpos + 1) % 2
                deps = [last_xin]
                if nbuf in pa_toks:
                    deps.append(pa_toks[nbuf])
                last_xin, dx_toks[(nb, npos_)] = p.dma_step(
                    "y", [(xa[:, nbuf * C:(nbuf + 1) * C], x_in[nrow:nrow + 128, :])],
                    deps, "xin", f"xin{nb}_{npos_}")

            if gpos == 0:
                emit_front(0)
            xcur = xa[:, buf * C:(buf + 1) * C]
            xn_v = xn_f[:, buf * C:(buf + 1) * C]
            t_xn, t_h1 = ln_toks.pop(gpos)

            def tx(t, src=xn_bf[:, buf * C:(buf + 1) * C]):
                for ic in range(8):
                    t.transpose(pbf[:, ic * 128:(ic + 1) * 128],
                                src[:, ic * 128:(ic + 1) * 128], ident[:, :])
            deps_t = [t_xn] + ([t_prev_pbf] if t_prev_pbf else [])
            t_txn = p.step("t", deps_t, tx, f"txn_{gpos}")
            def evx(s):
                s.copy(xnT_bf[:, :], pbf[:, :])
            t_xnT = p.step("s", [t_txn], evx, f"xnTev_{gpos}")
            def th1(t, src=h1_bf[:, buf * C:(buf + 1) * C]):
                for ic in range(8):
                    t.transpose(pbf[:, ic * 128:(ic + 1) * 128],
                                src[:, ic * 128:(ic + 1) * 128], ident[:, :])
            t_th1 = p.step("t", [t_h1, t_xnT], th1, f"th1_{gpos}")
            def evh(s):
                s.copy(h1T_bf[:, :], pbf[:, :])
            t_h1T = p.step("s", [t_th1], evh, f"h1Tev_{gpos}")
            t_prev_pbf = t_h1T

            war = [t_prev_ttt_tail] if t_prev_ttt_tail else []
            def qk_mm(t):
                for dc in range(8):
                    for ic in range(8):
                        t.matmul(pf_a[:, dc * 128:(dc + 1) * 128],
                                 wq_sb[:, ic * C + dc * 128:ic * C + (dc + 1) * 128],
                                 h1T_bf[:, ic * 128:(ic + 1) * 128],
                                 start=(ic == 0), stop=(ic == 7))
                for dc in range(8):
                    for ic in range(8):
                        t.matmul(pf_a[:, 1024 + dc * 128:1024 + (dc + 1) * 128],
                                 wk_sb[:, ic * C + dc * 128:ic * C + (dc + 1) * 128],
                                 h1T_bf[:, ic * 128:(ic + 1) * 128],
                                 start=(ic == 0), stop=(ic == 7))
            t_qk = p.step("t", [t_h1T, d_wl] + war, qk_mm, f"qk_{gpos}")
            if gpos + 1 < NPOS * B:
                emit_front(gpos + 1)
            def ev_q(s):
                s.copy(QT_sb[:, :], pf_a[:, 0:1024])
            t_q = p.step("s", [t_qk], ev_q, f"qev_{gpos}")
            def ev_k(v):
                v.tensor_copy(KT_sb[:, :], pf_a[:, 1024:2048])
            t_k = p.step("v", [t_qk], ev_k, f"kev_{gpos}")

            def v_mm(t):
                for hh in range(2):
                    for ic in range(8):
                        t.matmul(pf_b[:, hh * 512:(hh + 1) * 512],
                                 h1T_bf[:, ic * 128:(ic + 1) * 128],
                                 wv_sb[:, ic * C + hh * 512:ic * C + (hh + 1) * 512],
                                 start=(ic == 0), stop=(ic == 7))
            t_vm = p.step("t", [t_h1T, t_q, t_k], v_mm, f"vmm_{gpos}")
            def ev_v(s):
                s.copy(V_sb[:, :], pf_b[:, 0:1024])
            t_v = p.step("s", [t_vm], ev_v, f"vev_{gpos}")
            def km_mm(t):
                for hh in range(2):
                    for ic in range(8):
                        t.matmul(pf_b[:, hh * 512:(hh + 1) * 512],
                                 h1T_bf[:, ic * 128:(ic + 1) * 128],
                                 wk_sb[:, ic * C + hh * 512:ic * C + (hh + 1) * 512],
                                 start=(ic == 0), stop=(ic == 7))
            t_kmm = p.step("t", [t_v], km_mm, f"kmmm_{gpos}")
            def ev_km(v):
                v.tensor_copy(Km_sb[:, :], pf_b[:, 0:1024])
            t_km = p.step("v", [t_kmm], ev_km, f"kmev_{gpos}")

            def KT_h(h):
                return KT_sb[64 * (h % 2):64 * (h % 2) + 64, (h // 2) * 128:(h // 2) * 128 + 128]
            def QT_h(h):
                return QT_sb[64 * (h % 2):64 * (h % 2) + 64, (h // 2) * 128:(h // 2) * 128 + 128]
            def WTbf_h(h, b=b):
                return WTbf[64 * (h % 2):64 * (h % 2) + 64,
                            b * 1024 + 64 * h:b * 1024 + 64 * h + 64]
            def WTbf_hp(h, b=b):
                return WTbf[:, b * 1024 + 64 * h:b * 1024 + 64 * h + 64]

            def g_mm(t, KT_h=KT_h):
                for i, h in enumerate([0, 2, 4, 6, 8, 10, 12, 14, 1, 3, 5, 7, 9, 11, 13, 15]):
                    if i == 8:
                        t.drain()
                    t.matmul(pf_a[:, h * 128:(h + 1) * 128], KT_h(h), KT_h(h),
                             start=True, stop=True)
            t_g = p.step("t", [t_k, t_kmm], g_mm, f"G_{gpos}")

            mLT = maskLTn[:, :].unsqueeze(1).to_broadcast([128, H, 128])
            mUT = maskUTn[:, :].unsqueeze(1).to_broadcast([128, H, 128])
            mUTk = maskUT[:, :].unsqueeze(1).to_broadcast([128, H, 128])
            eta128 = eta_bc[:, :].unsqueeze(2).to_broadcast([128, H, 128])
            eta64 = eta_bc[:, :].unsqueeze(2).to_broadcast([128, H, 64])
            pav = pf_a[:, :].rearrange("p (h w) -> p h w", w=128)
            def b0_opsa(v):
                v.tensor_tensor(Bc0[:, :].rearrange("p (h w) -> p h w", w=128),
                                pav, mLT, OP.mult)
                v.tensor_tensor(Nc0[:, :].rearrange("p (h w) -> p h w", w=128),
                                pav, mUT, OP.mult)
            t_b0a = p.step("v", [t_g, t_eta, t_km], b0_opsa, f"B0a_{gpos}")
            def b0_opsb(v):
                v.tensor_tensor(Bc0[:, :].rearrange("p (h w) -> p h w", w=128),
                                Bc0[:, :].rearrange("p (h w) -> p h w", w=128),
                                eta128, OP.mult)
                v.tensor_tensor(Nc0[:, :].rearrange("p (h w) -> p h w", w=128),
                                Nc0[:, :].rearrange("p (h w) -> p h w", w=128),
                                eta128, OP.mult)
            t_b0 = p.step("v", [t_b0a], b0_opsb, f"B0b_{gpos}")

            def h_mm(t, KT_h=KT_h, QT_h=QT_h):
                for i, h in enumerate([0, 2, 4, 6, 8, 10, 12, 14, 1, 3, 5, 7, 9, 11, 13, 15]):
                    if i == 8:
                        t.drain()
                    t.matmul(pf_a[:, h * 128:(h + 1) * 128], KT_h(h), QT_h(h),
                             start=True, stop=True)
            t_hm = p.step("t", [t_b0a, t_q], h_mm, f"H_{gpos}")
            def hbar_ev(v):
                v.tensor_tensor(Hbar[:, :].rearrange("p (h w) -> p h w", w=128),
                                pav, mUTk, OP.mult)
            t_hb = p.step("v", [t_hm], hbar_ev, f"Hbar_{gpos}")

            def r_mm(t, WTbf_h=WTbf_h, KT_h=KT_h):
                for i, h in enumerate([0, 2, 4, 6, 8, 10, 12, 14, 1, 3, 5, 7, 9, 11, 13, 15]):
                    if i == 8:
                        t.drain()
                    t.matmul(pf_b[:, h * 64:(h + 1) * 64], KT_h(h), WTbf_h(h),
                             start=True, stop=True)
            t_r = p.step("t", [t_km, t_stbf, t_hm], r_mm, f"R_{gpos}")
            def c_init(v):
                v.tensor_tensor(c_f[:, :], pf_b[:, 0:1024], V_sb[:, :], OP.subtract)
            t_cf = p.step("v", [t_r, t_v], c_init, f"cinit_{gpos}")
            def c_cast(s):
                s.copy(c_bf[:, :], c_f[:, :])
            t_cb = p.step("s", [t_cf], c_cast, f"ccast_{gpos}")

            Bcur, Ncur, Bnx, Nnx = Bc0, Nc0, Bc1, Nc1
            t_bn_ready = [t_b0]
            t_sq_prev = None
            for k in range(7):
                def app_mm(t, Ncur=Ncur):
                    for h in range(H):
                        t.matmul(pf_b[:, h * 64:(h + 1) * 64],
                                 Ncur[:, h * 128:(h + 1) * 128],
                                 c_bf[:, h * 64:(h + 1) * 64],
                                 start=True, stop=True)
                t_app = p.step("t", [t_cb, t_cf] + t_bn_ready, app_mm,
                               f"app{k}_{gpos}")
                def c_upd(v):
                    v.tensor_add(c_f[:, :], c_f[:, :], pf_b[:, 0:1024])
                t_cf = p.step("v", [t_app], c_upd, f"cupd{k}_{gpos}")
                if k < 6:
                    def c_cast2(s):
                        s.copy(c_bf[:, :], c_f[:, :])
                    t_cb = p.step("s", [t_cf], c_cast2, f"ccast{k}_{gpos}")
                    do_b = (k < 5)
                    deps_sq = list(t_bn_ready) + [t_app, t_hb]
                    if t_sq_prev:
                        deps_sq.append(t_sq_prev)
                    if do_b:
                        def sq_b(t, Bcur=Bcur, Ncur=Ncur):
                            for h in range(H):
                                t.matmul(pf_a[:, h * 128:(h + 1) * 128],
                                         Ncur[:, h * 128:(h + 1) * 128],
                                         Bcur[:, h * 128:(h + 1) * 128],
                                         start=True, stop=True)
                        t_sqb = p.step("t", deps_sq, sq_b, f"sqB{k}_{gpos}")
                        def evb(s, Bnx=Bnx):
                            s.copy(Bnx[:, :], pf_a[:, :])
                        t_evb = p.step("s", [t_sqb], evb, f"evB{k}_{gpos}")
                    else:
                        t_evb = p.step("t", deps_sq, lambda t: None, f"sqBnop{k}_{gpos}")
                    def sq_n(t, Bcur=Bcur, Ncur=Ncur):
                        for h in range(H):
                            t.matmul(pf_a[:, h * 128:(h + 1) * 128],
                                     Bcur[:, h * 128:(h + 1) * 128],
                                     Ncur[:, h * 128:(h + 1) * 128],
                                     start=True, stop=True)
                    t_sqn = p.step("t", [t_evb], sq_n, f"sqN{k}_{gpos}")
                    def evn(s, Nnx=Nnx):
                        s.copy(Nnx[:, :], pf_a[:, :])
                    t_evn = p.step("s", [t_sqn], evn, f"evN{k}__{gpos}")
                    t_sq_prev = t_sqn
                    Bcur, Bnx = Bnx, Bcur
                    Ncur, Nnx = Nnx, Ncur
                    t_bn_ready = [t_evn] + ([t_evb] if do_b else [])

            def ct_ops(v):
                v.scalar_tensor_tensor(ct_bf[:, :].rearrange("p (h w) -> p h w", w=64),
                                       c_f[:, :].rearrange("p (h w) -> p h w", w=64),
                                       -1.0, eta64, OP.mult, OP.mult)
            t_ct = p.step("v", [t_cf], ct_ops, f"ct_{gpos}")

            def out_mm(t, WTbf_hp=WTbf_hp):
                for h in range(H):
                    t.matmul(pf_a[0:64, h * 128:(h + 1) * 128], WTbf_hp(h),
                             QT_sb[:, (h // 2) * 128:(h // 2) * 128 + 128],
                             start=True, stop=False)
                    t.matmul(pf_a[0:64, h * 128:(h + 1) * 128],
                             ct_bf[:, h * 64:(h + 1) * 64],
                             Hbar[:, h * 128:(h + 1) * 128],
                             start=False, stop=True)
            t_out = p.step("t", [t_ct, t_hb, t_stbf] + t_bn_ready, out_mm,
                           f"out_{gpos}")
            def out_ev(s):
                s.copy(OutT_bf[:, :], pf_a[0:64, :])
            t_oev = p.step("s", [t_out], out_ev, f"outev_{gpos}")

            def st_mm(t):
                for pr in range(8):
                    t.matmul(pf_b[:, pr * 128:(pr + 1) * 128],
                             Km_sb[:, pr * 128:(pr + 1) * 128],
                             ct_bf[:, pr * 128:(pr + 1) * 128],
                             start=True, stop=True)
            t_stm = p.step("t", [t_ct, t_km, t_cf, t_out], st_mm, f"stmm_{gpos}")
            def st_upd(v, b=b):
                top_in = pf_b[0:64, 0:1024].rearrange("p (r w) -> p r w", w=128)[:, :, 0:64]
                top_out = WTst[0:64, b * 512:(b + 1) * 512].rearrange("p (r w) -> p r w", w=64)
                v.tensor_tensor(top_out, top_out, top_in, OP.add)
                bot_in = pf_b[64:128, 0:1024].rearrange("p (r w) -> p r w", w=128)[:, :, 64:128]
                bot_out = WTst[64:128, b * 512:(b + 1) * 512].rearrange("p (r w) -> p r w", w=64)
                v.tensor_tensor(bot_out, bot_out, bot_in, OP.add)
            t_stu = p.step("v", [t_stm, wt_war[b], t_r], st_upd, f"stupd_{gpos}")
            def st_cast(s, b=b):
                wtop = WTbf[0:64, b * 1024:(b + 1) * 1024].rearrange(
                    "p (h w) -> p h w", w=64)[:, 0::2, :]
                s.copy(wtop, WTst[0:64, b * 512:(b + 1) * 512].rearrange(
                    "p (h w) -> p h w", w=64))
                s.drain()
                wbot = WTbf[64:128, b * 1024:(b + 1) * 1024].rearrange(
                    "p (h w) -> p h w", w=64)[:, 1::2, :]
                s.copy(wbot, WTst[64:128, b * 512:(b + 1) * 512].rearrange(
                    "p (h w) -> p h w", w=64))
            t_stbf = p.step("s", [t_stu], st_cast, f"stcast_{gpos}")
            wt_war[b] = t_stbf

            def y1_mm(t):
                for hh in range(2):
                    for cc in range(16):
                        t.matmul(pf_b[:, hh * 512:(hh + 1) * 512],
                                 OutT_bf[:, cc * 128:(cc + 1) * 128],
                                 wo_sb[:, cc * C + hh * 512:cc * C + (hh + 1) * 512],
                                 start=(cc == 0), stop=(cc == 15))
            t_y1 = p.step("t", [t_oev, t_stu, d_wl], y1_mm, f"y1_{gpos}")
            def u_ops(v, xn_v=xn_v):
                v.tensor_add(u_f[:, :], xn_v, pf_b[:, 0:1024])
            t_u = p.step("v", [t_y1, t_xn] + d_u_prev, u_ops, f"u_{gpos}")
            du_deps = [t_u] + ([t_u_dma] if t_u_dma else [])
            t_u_dma, d_u = p.dma_step("g", [(u_s[row0:row0 + 128, :], u_f[:, :])],
                                      du_deps, "out", f"udma_{gpos}")
            d_u_prev = [d_u]

            def s_mm(t):
                for ic in range(8):
                    t.matmul(pf_b[:, 1024 + 0:1024 + 64],
                             xnT_bf[:, ic * 128:(ic + 1) * 128],
                             Win_bf[:, ic * D:(ic + 1) * D],
                             start=(ic == 0), stop=(ic == 7))
            t_s0 = p.step("t", [t_xnT, d_wl, t_y1], s_mm, f"s0_{gpos}")
            def s_ev(v):
                v.tensor_add(s0_f[:, :], pf_b[:, 1024:1088], fbin_bc[:, :])
            t_s0e = p.step("v", [t_s0], s_ev, f"s0ev_{gpos}")
            def f0_el(v):
                s_exp = s0_f[:, :].unsqueeze(2).to_broadcast([128, D, 16])
                v.tensor_tensor(hf0[:, :].rearrange("p (m j) -> p m j", j=16),
                                s_exp, fW1bc[:, :].rearrange("p (m j) -> p m j", j=16),
                                OP.mult)
                v.drain()
                v.tensor_add(hf0[:, :], hf0[:, :], fb1bc[:, :])
                v.drain()
                v.tensor_scalar_max(hf0[:, :], hf0[:, :], 0.0)
                v.drain()
                v.tensor_mul(hf0[:, :], hf0[:, :], fW2bc[:, :])
                v.drain()
                v.tensor_reduce(s0_f[:, 0:D], hf0[:, :].rearrange("p (m j) -> p m j", j=16),
                                axis=AX.X, op=OP.add)
                v.drain()
                v.tensor_copy(g0_bf[:, :], s0_f[:, 0:D])
            t_g0 = p.step("v", [t_s0e, p0_done["v"]], f0_el, f"g0_{gpos}")
            def gt_mm(t):
                t.transpose(pbf[0:64, 0:128], g0_bf[:, :], ident[:, :])
            t_gt = p.step("t", [t_g0, t_s0, t_prev_pbf], gt_mm, f"gt_{gpos}")
            def gt_ev(v):
                v.tensor_copy(gT_bf[:, :], pbf[0:64, 0:128])
            t_gte = p.step("v", [t_gt], gt_ev, f"gtev_{gpos}")
            t_prev_pbf = t_gte
            def f0_mm(t):
                for hh in range(2):
                    t.matmul(pf_a[:, hh * 512:(hh + 1) * 512], gT_bf[:, :],
                             Wout_bf[:, hh * 512:(hh + 1) * 512],
                             start=True, stop=True)
            t_f0 = p.step("t", [t_gte, t_out, t_oev], f0_mm, f"f0_{gpos}")

            def pa_ops(v, xcur=xcur):
                v.tensor_add(pa_f[:, :], xcur, pf_a[:, 0:1024])
                v.drain()
                v.tensor_add(pa_f[:, :], pa_f[:, :], ffb_bc[:, :])
                v.drain()
                v.tensor_add(pa_f[:, :], pa_f[:, :], u_f[:, :])
            t_pa = p.step("v", [t_f0, t_u] + d_pa_prev, pa_ops, f"pa_{gpos}")
            pa_toks[buf] = t_pa
            dpa_deps = [t_pa] + ([t_pa_dma] if t_pa_dma else [])
            t_pa_dma, d_pa = p.dma_step("g", [(pa_s[row0:row0 + 128, :], pa_f[:, :])],
                                        dpa_deps, "out", f"padma_{gpos}")
            d_pa_prev = [d_pa]
            if dbg_on("u"):
                _, d_ud = p.dma_step("g", [(u_dbg[row0:row0 + 128, :], u_f[:, :])],
                                     [t_u, t_u_dma], "dbg", f"udbg_{gpos}")
                _, d_pd = p.dma_step("g", [(pa_dbg[row0:row0 + 128, :], pa_f[:, :])],
                                     [t_pa, t_pa_dma], "dbg", f"padbg_{gpos}")
                d_u_prev.append(d_ud)
                d_pa_prev.append(d_pd)

            t_prev_ttt_tail = t_pa

    st_dbg = dbg_out("st_dbg", (128, B * 512))
    if dbg_on("st") and "A" in phases:
        p.dma_step("g", [(st_dbg[:, :], WTst[:, :])],
                   [p.tok("v"), p.tok("s"), p.tok("t")], "dbg", "stdbg")

    pA_done = p.barrier("pA_done")
    arenaA.close()

    TTOK = 256 if T % 256 == 0 else 128
    NSUB = TTOK // 128
    NTT = T // TTOK
    x1_dbg = dbg_out("x1_dbg", (NT, C))

    if "B" in phases:
        arenaB, AB = mkarena()
        w1_sb = AB("w1_sb", [128, 8 * 4096], BF16)
        w2_sb = AB("w2_sb", [128, 32 * 1024], BF16)
        g2bc = AB("g2bc", [128, C], F32)
        be2bc = AB("be2bc", [128, C], F32)
        b2bc = AB("b2bc", [128, C], F32)
        b1cols = AB("b1cols", [128, 32], F32)
        ln2row_bf = AB("ln2row_bf", [1, 2 * C], BF16)
        b1row_bf = AB("b1row_bf", [1, 1024], BF16)
        b2row_bf = AB("b2row_bf", [1, C], BF16)
        u_t = AB("u_t", [128, 2 * NSUB * C], F32)
        pa_t = AB("pa_t", [128, NSUB * C], F32)
        h2w = AB("h2w", [128, C], F32)
        h2bf = AB("h2bf", [128, NSUB * C], BF16)
        h2T_bf = AB("h2T_bf", [128, 8 * TTOK], BF16)
        gel0 = AB("gel0", [128, TTOK], BF16)
        gel1 = AB("gel1", [128, TTOK], BF16)
        bnsB = AB("bnsB", [128, 12], F32)
        agB = AB("agB", [128, 2], F32)
        rstdB = AB("rstdB", [128, 1], F32)
        nmrB = AB("nmrB", [128, 1], F32)

        tB_w1, dB_w1 = p.dma_step("g", [
            (w1_sb[:, ic * 4096:(ic + 1) * 4096], w1T[ic * 128:(ic + 1) * 128, :])
            for ic in range(8)], [pA_done["g"]], "wld", "B_w1")
        tB_w2, dB_w2 = p.dma_step("g", [
            (w2_sb[:, d4 * 1024:(d4 + 1) * 1024], w2T[d4 * 128:(d4 + 1) * 128, :])
            for d4 in range(32)], [tB_w1], "wld", "B_w2")
        tB_r, dB_r = p.dma_step("g", [
            (ln2row_bf[:, 0:C], ln2w[:, :]), (ln2row_bf[:, C:2 * C], ln2b[:, :]),
            (b2row_bf[:, :], b2_in[:, :])],
            [tB_w2], "wld", "B_rows")

        def b_g2mm(t):
            for hh in range(2):
                t.matmul(pf_a[:, hh * 512:(hh + 1) * 512], ones_row[:, :],
                         ln2row_bf[:, hh * 512:(hh + 1) * 512], start=True, stop=True)
                t.matmul(pf_a[:, 1024 + hh * 512:1024 + (hh + 1) * 512], ones_row[:, :],
                         ln2row_bf[:, C + hh * 512:C + (hh + 1) * 512], start=True, stop=True)
                t.matmul(pf_b[:, hh * 512:(hh + 1) * 512], ones_row[:, :],
                         b2row_bf[:, hh * 512:(hh + 1) * 512], start=True, stop=True)
        tB_gm = p.step("t", [dB_r, pA_done["t"]], b_g2mm, "B_g2mm")
        prev_b1m = None
        for rr in range(4):
            ld_deps = [tB_w2] if prev_b1m is None else [prev_b1m]
            _, d_b1 = p.dma_step("g", [(b1row_bf[:, :],
                                        b1_in[0:1, rr * 1024:(rr + 1) * 1024])],
                                 ld_deps, "wld", f"B_b1r{rr}")
            def b1mm(t, rr=rr):
                for ch in range(8):
                    t.matmul(pf_b[:, 1024 + rr * 8 + ch:1025 + rr * 8 + ch],
                             b1row_bf[:, ch * 128:(ch + 1) * 128],
                             ones_row[0:1, 0:1], start=True, stop=True)
            prev_b1m = p.step("t", [d_b1] + ([tB_gm] if rr == 0 else []), b1mm,
                              f"B_b1m{rr}")
        def b_g2ev(v):
            v.tensor_copy(g2bc[:, :], pf_a[:, 0:1024])
            v.tensor_copy(be2bc[:, :], pf_a[:, 1024:2048])
            v.tensor_copy(b2bc[:, :], pf_b[:, 0:1024])
            v.tensor_copy(b1cols[:, :], pf_b[:, 1024:1056])
        tB_ge = p.step("v", [tB_gm, prev_b1m, pA_done["v"]], b_g2ev, "B_g2ev")

        dB_out_prev = []
        t_outdma = None
        t_prev_tail = None
        t_prev_pbfB = None
        lnB_toks = {}
        uld_war = [None, None]
        NGI = B * NTT

        def emit_frontB(gi, tr_dep):
            par = gi % 2
            b_ = gi // NTT
            row0 = b_ * T + (gi % NTT) * TTOK
            deps_u = [pA_done["y"]]
            if uld_war[par] is not None:
                deps_u.append(uld_war[par])
            t_uin, d_uin = p.dma_step("y", [
                (u_t[:, (par * NSUB + s) * C:(par * NSUB + s + 1) * C],
                 u_s[row0 + s * 128:row0 + (s + 1) * 128, :])
                for s in range(NSUB)], deps_u, "xin", f"B_uin{gi}")
            casts = []
            t4 = None
            for s in range(NSUB):
                usub = u_t[:, (par * NSUB + s) * C:(par * NSUB + s + 1) * C]
                def stB(v, usub=usub):
                    v.bn_stats(bnsB[:, 0:6], usub[:, 0:512])
                    v.bn_stats(bnsB[:, 6:12], usub[:, 512:1024])
                    v.drain()
                    v.bn_aggr(agB[:, :], bnsB[:, :])
                t1 = p.step("v", [d_uin, tB_ge], stB, f"B_ln{gi}_{s}")
                def rsB(sc):
                    sc.activation(rstdB[:, :], agB[:, 1:2], AF.Ln, bias=LN_EPS,
                                  scale=1.0)
                    sc.drain()
                    sc.activation(rstdB[:, :], rstdB[:, :], AF.Exp, bias=0.0,
                                  scale=-0.5)
                t2 = p.step("s", [t1], rsB, f"B_rstd{gi}_{s}")
                def nmB(v):
                    v.tensor_scalar(nmrB[:, :], agB[:, 0:1], rstdB[:, 0:1], -1.0,
                                    OP.mult, OP.mult)
                t3 = p.step("v", [t2], nmB, f"B_nmr{gi}_{s}")
                def apB(v, usub=usub):
                    v.tensor_scalar(h2w[:, :], usub, rstdB[:, 0:1], nmrB[:, 0:1],
                                    OP.mult, OP.add)
                    v.drain()
                    v.tensor_mul(h2w[:, :], h2w[:, :], g2bc[:, :])
                    v.drain()
                    v.tensor_add(h2w[:, :], h2w[:, :], be2bc[:, :])
                t4 = p.step("v", [t3], apB, f"B_h2{gi}_{s}")
                def cstB(sc, s=s):
                    sc.copy(h2bf[:, s * C:(s + 1) * C], h2w[:, :])
                deps_c = [t4]
                if s == 0 and tr_dep is not None:
                    deps_c.append(tr_dep)
                casts.append(p.step("s", deps_c, cstB, f"B_h2c{gi}_{s}"))
            uld_war[par] = t4
            lnB_toks[gi] = casts

        for b in range(B):
            for ti in range(NTT):
                gi = b * NTT + ti
                row0 = b * T + ti * TTOK
                if gi == 0:
                    emit_frontB(0, None)
                t_pain, d_pain = p.dma_step("y", [
                    (pa_t[:, s * C:(s + 1) * C],
                     pa_s[row0 + s * 128:row0 + (s + 1) * 128, :])
                    for s in range(NSUB)], [pA_done["y"]] + dB_out_prev,
                    "xin", f"B_pain{gi}")

                casts = lnB_toks.pop(gi)
                t_h2T = None
                tr2 = None
                for s in range(NSUB):
                    deps_t = [casts[s]] + ([t_prev_pbfB] if t_prev_pbfB else [])
                    def trB(t, s=s):
                        for ic in range(8):
                            t.transpose(pbf[:, ic * 128:(ic + 1) * 128],
                                        h2bf[:, s * C + ic * 128:s * C + (ic + 1) * 128],
                                        ident[:, :])
                    t6 = p.step("t", deps_t, trB, f"B_tr{gi}_{s}")
                    tr2 = t6
                    def evB(v, s=s):
                        v.tensor_copy(
                            h2T_bf[:, :].rearrange("p (i w) -> p i w", w=TTOK)[:, :, s * 128:(s + 1) * 128],
                            pbf[:, :].rearrange("p (i w) -> p i w", w=128))
                    t_h2T = p.step("v", [t6], evB, f"B_trev{gi}_{s}")
                    t_prev_pbfB = t_h2T

                t_w1 = {}
                t_gel = {}
                t_w2 = {}
                gels = [gel0, gel1]
                def emit_w2(d4c):
                    def w2mm(t, d4c=d4c):
                        for s in range(NSUB):
                            for hh in range(2):
                                t.matmul(pf_a[:, (s * 2 + hh) * 512:(s * 2 + hh + 1) * 512],
                                         gels[d4c % 2][:, s * 128:(s + 1) * 128],
                                         w2_sb[:, d4c * 1024 + hh * 512:d4c * 1024 + (hh + 1) * 512],
                                         start=(d4c == 0), stop=(d4c == 31))
                    deps = [t_gel[d4c]]
                    if d4c == 0:
                        deps.append(t_prev_tail or tB_ge)
                    t_w2[d4c] = p.step("t", deps, w2mm, f"B_w2_{gi}_{d4c}")
                for d4c in range(32):
                    def w1mm(t, d4c=d4c):
                        for ic in range(8):
                            t.matmul(pf_b[:, (d4c % 2) * 512:(d4c % 2) * 512 + TTOK],
                                     w1_sb[:, ic * 4096 + d4c * 128:ic * 4096 + (d4c + 1) * 128],
                                     h2T_bf[:, ic * TTOK:(ic + 1) * TTOK],
                                     start=(ic == 0), stop=(ic == 7))
                    deps = [t_h2T, dB_w1]
                    if d4c >= 2:
                        deps.append(t_gel[d4c - 2])
                    t_w1[d4c] = p.step("t", deps, w1mm, f"B_w1_{gi}_{d4c}")
                    def gelB(sc, d4c=d4c):
                        sc.activation(gels[d4c % 2][:, :], pf_b[:, (d4c % 2) * 512:(d4c % 2) * 512 + TTOK],
                                      AF.Gelu, bias=b1cols[:, d4c:d4c + 1], scale=1.0)
                    t_gel[d4c] = p.step("s", [t_w1[d4c], tB_ge] +
                                        ([t_w2[d4c - 2]] if d4c >= 2 else []),
                                        gelB, f"B_gel_{gi}_{d4c}")
                    if d4c >= 1:
                        emit_w2(d4c - 1)
                emit_w2(31)
                tb2 = t_w2[31]

                if gi + 1 < NGI:
                    emit_frontB(gi + 1, tr2)

                def outB(v):
                    for s in range(NSUB):
                        for hh in range(2):
                            sl = pa_t[:, s * C + hh * 512:s * C + (hh + 1) * 512]
                            v.tensor_add(sl, sl, pf_a[:, (s * 2 + hh) * 512:(s * 2 + hh + 1) * 512])
                            v.drain()
                            v.tensor_add(sl, sl, b2bc[:, hh * 512:(hh + 1) * 512])
                t_o = p.step("v", [tb2, d_pain, tB_ge], outB, f"B_out{gi}")
                t_prev_tail = t_o
                od_deps = [t_o] + ([t_outdma] if t_outdma else [])
                trs_o = [(x1_s[row0 + s * 128:row0 + (s + 1) * 128, :],
                          pa_t[:, s * C:(s + 1) * C]) for s in range(NSUB)]
                t_outdma, d_o = p.dma_step("g", trs_o, od_deps, "out", f"B_odma{gi}")
                dB_out_prev = [d_o]
                if dbg_on("x1"):
                    _, d_xd = p.dma_step("g", [(x1_dbg[row0 + s * 128:row0 + (s + 1) * 128, :],
                                                pa_t[:, s * C:(s + 1) * C]) for s in range(NSUB)],
                                         [t_o, t_outdma], "dbg", f"B_xdbg{gi}")
                    dB_out_prev.append(d_xd)

        pB_done = p.barrier("pB_done")
        arenaB.close()

    wsc_g = A("wsc_g", [128, 22], F32)
    wsc_u = A("wsc_u", [128, 22], F32)
    wsc_d = A("wsc_d", [128, 8], F32)

    if "E" in phases:
        arenaW, AW = mkarena()
        wstg = AW("wstg", [128, 2 * 2816], F32)
        wq1 = AW("wq1", [128, 2816], F32)
        wqb = AW("wqb", [128, 2816], BF16)
        wqev = AW("wqev", [128, 1024], BF16)
        wmc = AW("wmc", [128, 1], F32)
        wrc = AW("wrc", [128, 1], F32)
        prevB = p.barrier("preW")

        jobs = []
        for rc in range(22):
            jobs.append((wg_in, wgq_s, wsc_g, rc, 1024, 8))
        for rc in range(22):
            jobs.append((wu_in, wuq_s, wsc_u, rc, 1024, 8))
        for rc in range(8):
            jobs.append((wd_in, wdq_s, wsc_d, rc, 2816, 22))
        jobs_t = {}
        t_q1s = {}
        t_evW_prev = None
        d_outs = []
        for jidx, (w_src, w_dst, wsc, rc, CW, NTR) in enumerate(jobs):
            buf = jidx % 2
            deps = [prevB["y"]] if jidx == 0 else [jobs_t[jidx - 1]]
            if jidx >= 2:
                deps.append(t_q1s[jidx - 2])
            t_l, d_l = p.dma_step("y", [(wstg[:, buf * 2816:buf * 2816 + CW],
                                         w_src[rc * 128:(rc + 1) * 128, :])],
                                  deps, "wld", f"W_l{jidx}")
            jobs_t[jidx] = t_l
            def q0(v, buf=buf, CW=CW, wsc=wsc, rc=rc):
                v.tensor_reduce(wmc[:, :], wstg[:, buf * 2816:buf * 2816 + CW],
                                axis=AX.X, op=OP.max, apply_absolute_value=True)
                v.drain()
                v.reciprocal(wrc[:, :], wmc[:, :])
                v.drain()
                v.tensor_scalar_mul(wrc[:, :], wrc[:, :], QMAX)
                v.tensor_scalar_mul(wsc[:, rc:rc + 1], wmc[:, :], 1.0 / QMAX)
                v.drain()
                v.tensor_scalar(wq1[:, 0:CW], wstg[:, buf * 2816:buf * 2816 + CW],
                                wrc[:, 0:1], MR, OP.mult, OP.add)
                v.drain()
                v.tensor_scalar(wq1[:, 0:CW], wq1[:, 0:CW], -MR, -(QMAX + 1.0),
                                OP.add, OP.max)
                v.drain()
                v.tensor_scalar(wqb[:, 0:CW], wq1[:, 0:CW], QMAX, None, OP.min)
            dep_q = [d_l] + ([t_q1s[jidx - 1]] if jidx >= 1 else [])
            if t_evW_prev is not None:
                dep_q.append(t_trW_last)
            t_q = p.step("v", dep_q, q0, f"W_q{jidx}")
            t_q1s[jidx] = t_q
            for grp in range(0, NTR, 8):
                n = min(8, NTR - grp)
                def trW(t, grp=grp, n=n):
                    for i in range(n):
                        t.transpose(pbf[:, i * 128:(i + 1) * 128],
                                    wqb[:, (grp + i) * 128:(grp + i + 1) * 128],
                                    ident[:, :])
                depsT = [t_q, prevB["t"]] + ([t_evW_prev] if t_evW_prev else [])
                t_tr = p.step("t", depsT, trW, f"W_tr{jidx}_{grp}")
                t_trW_last = t_tr
                def evW(s, n=n):
                    s.copy(wqev[:, 0:n * 128], pbf[:, 0:n * 128])
                t_ev = p.step("s", [t_tr] + d_outs, evW, f"W_ev{jidx}_{grp}")
                t_evW_prev = t_ev
                trs = [(w_dst[(grp + i) * 128:(grp + i + 1) * 128,
                              rc * 128:(rc + 1) * 128],
                        wqev[:, i * 128:(i + 1) * 128]) for i in range(n)]
                t_od, d_od = p.dma_step("g", trs, [t_ev], "out", f"W_o{jidx}_{grp}")
                d_outs = [d_od]
        pW_done = p.barrier("pW_done")
        arenaW.close()

    mx_col = A("mx_col", [128, 1], F32)
    gmax_x = A("gmax_x", [1, 1], F32)
    sx_col = A("sx_col", [128, 1], F32)
    rsx_col = A("rsx_col", [128, 1], F32)
    gmax_h = A("gmax_h", [1, 1], F32)
    sh_col = A("sh_col", [128, 1], F32)
    rsh_col = A("rsh_col", [128, 1], F32)
    scr11 = A("scr11", [1, 4], F32)
    cc_sem = nc.alloc_semaphore("cc_sem")
    p.dsems["cc0"] = [cc_sem, 0]
    p.dlanes["cc"] = {"keys": ["cc0"], "rr": 0}

    pairswap = []
    for i in range(16):
        pairswap += [2 * i + 1, 2 * i]
    Lm05 = float(L) ** -0.5

    if "C" in phases:
        arenaCD, ACD = mkarena()
        klatT = ACD("klatT", [128, B * 4 * T], BF16)
        qfkT = ACD("qfkT", [128, B * 4 * T], BF16)
        vI = ACD("vI", [128, B * NPOS * 528], BF16)
        cos_fm = ACD("cos_fm", [128, T], BF16)
        sin_fm = ACD("sin_fm", [128, T], BF16)
        q2lBD = ACD("q2lBD", [128, 64], BF16)
        kv2lBD = ACD("kv2lBD", [128, 64], BF16)
        kv2l_rep = ACD("kv2l_rep", [128, 32], BF16)
        wqkBD = ACD("wqkBD", [128, 128], BF16)
        qb_col = ACD("qb_col", [128, 1], F32)
        kvb_col = ACD("kvb_col", [128, 1], F32)
        wqkb_col = ACD("wqkb_col", [128, 1], F32)
        kvb_rep = ACD("kvb_rep", [1, 512], BF16)
        kvb_bc = ACD("kvb_bc", [128, 512], F32)
        l2oI = ACD("l2oI", [33, 64], BF16)
        cpb_bf = ACD("cpb_bf", [1, C], BF16)

        arenaC, AC = mkarena()
        ca_sb = AC("ca_sb", [128, 8 * 3 * C], BF16)
        cab_cols = AC("cab_cols", [128, 24], F32)
        cab_bf = klatT[0:1, 0:3 * C]
        kvb_row = AC("kvb_row", [1, L], BF16)
        x1_t = AC("x1_t", [128, 2 * C], F32)
        xn2_f = AC("xn2_f", [128, C], F32)
        xnT_st = AC("xnT_st", [128, 8 * 256], F32)
        xnT_bfc = AC("xnT_bfc", [128, 2048], BF16)
        qkv_bf = AC("qkv_bf", [128, 24 * 256], BF16)
        qr_f = AC("qr_f", [128, 1024], F32)
        qr_bf = AC("qr_bf", [128, 1024], BF16)
        qsw = xn2_f
        bnsC = AC("bnsC", [128, 12], F32)
        agC = AC("agC", [128, 2], F32)
        mxsc = AC("mxsc", [128, 1], F32)
        rstdC = AC("rstdC", [128, 1], F32)
        nmrC = AC("nmrC", [128, 1], F32)
        iotaw = xnT_st[0:1, 0:T]
        invf_row = AC("invf_row", [1, 32], F32)
        sgn_row = AC("sgn_row", [1, 32], F32)
        invf_col = AC("invf_col", [128, 1], F32)
        sgn_col = AC("sgn_col", [128, 1], F32)
        tbc = x1_t[:, 0:T]
        s2t = xnT_st[:, 0:T]
        c2t = qkv_bf[:, :].bitcast(F32)[:, 0:T]

        prevW = p.barrier("preC")
        tC_r, dC_r = p.dma_step("g", [
            (cab_bf[:, :], cab[:, :]), (kvb_row[:, :], kv2lb[:, :]),
            (q2lBD[0:64, 0:32], q2lT[:, :]), (q2lBD[64:128, 32:64], q2lT[:, :]),
            (kv2lBD[0:64, 0:32], kv2lT[:, :]), (kv2lBD[64:128, 32:64], kv2lT[:, :]),
            (kv2l_rep[0:64, :], kv2lT[:, :]), (kv2l_rep[64:128, :], kv2lT[:, :]),
            (wqkBD[0:32, 0:32], wqkT[:, :]), (wqkBD[32:64, 32:64], wqkT[:, :]),
            (wqkBD[64:96, 64:96], wqkT[:, :]), (wqkBD[96:128, 96:128], wqkT[:, :]),
            (l2oI[0:32, :], l2oT[:, :]), (l2oI[32:33, :], l2ob[:, :]),
            (cpb_bf[:, :], cpb[:, :])],
            [prevW["g"]], "cst", "C_rows")

        def c_zero(g):
            g.memset(q2lBD[0:64, 32:64], 0.0)
            g.memset(q2lBD[64:128, 0:32], 0.0)
            g.memset(kv2lBD[0:64, 32:64], 0.0)
            g.memset(kv2lBD[64:128, 0:32], 0.0)
            g.memset(wqkBD[0:32, 32:128], 0.0)
            g.memset(wqkBD[32:64, 0:32], 0.0)
            g.memset(wqkBD[32:64, 64:128], 0.0)
            g.memset(wqkBD[64:96, 0:64], 0.0)
            g.memset(wqkBD[64:96, 96:128], 0.0)
            g.memset(wqkBD[96:128, 0:96], 0.0)
            g.memset(mx_col[:], 0.0)
            g.iota(iotaw, pattern=[[1, T]], base=0, channel_multiplier=0,
                   allow_small_or_imprecise_dtypes=True)
            g.iota(invf_row[:, :], pattern=[[1, 16], [0, 2]], base=0,
                   channel_multiplier=0, allow_small_or_imprecise_dtypes=True)
            g.iota(sgn_row[:, :], pattern=[[0, 16], [1, 2]], base=0,
                   channel_multiplier=0, allow_small_or_imprecise_dtypes=True)
        tC_z = p.step("g", [prevW["g"], dC_r], c_zero, "C_zero")

        p.dsems["colC00"] = [nc.alloc_semaphore("dma_colC0"), 0]
        rec_cc = p.dsems["colC00"]
        def c_cols(g):
            sem = rec_cc[0]
            for gq in range(4):
                g.dma_start(out=qb_col[32 * gq:32 * gq + 32, 0:1], in_=q2lb[0:1, :],
                            allow_slow_non_contiguous=True).then_inc(sem, 16)
                g.dma_start(out=kvb_col[32 * gq:32 * gq + 32, 0:1], in_=kv2lb[0:1, :],
                            allow_slow_non_contiguous=True).then_inc(sem, 16)
                g.dma_start(out=wqkb_col[32 * gq:32 * gq + 32, 0:1], in_=wqkb[0:1, :],
                            allow_slow_non_contiguous=True).then_inc(sem, 16)
        rec_cc[1] += 16 * 12
        tC_cols = p.step("g", [tC_z], c_cols, "C_cols")
        dC_cols = DmaTok((("colC00", rec_cc[1]),))

        def c_invf(s):
            s.activation(invf_row[:, :], invf_row[:, :], AF.Exp,
                         bias=0.0, scale=-math.log(10000.0) / 16.0)
        tC_iv = p.step("s", [tC_z], c_invf, "C_invf")
        def c_sgn(v):
            v.tensor_scalar(sgn_row[:, :], sgn_row[:, :], 2.0, -1.0, OP.mult, OP.add)
        tC_sg = p.step("v", [tC_z], c_sgn, "C_sgn")
        def c_tmm(t):
            wch = min(512, T)
            for ch in range(T // wch):
                t.matmul(pf_a[:, ch * wch:(ch + 1) * wch], onesf_row[:, :],
                         iotaw[0:1, ch * wch:(ch + 1) * wch], start=True, stop=True)
            for gq in range(4):
                t.matmul(pf_b[32 * gq:32 * gq + 32, 1024:1025], invf_row[:, :],
                         one_f[:, :], start=True, stop=True,
                         tile_position=(0, 32 * gq))
                t.matmul(pf_b[32 * gq:32 * gq + 32, 1026:1027], sgn_row[:, :],
                         one_f[:, :], start=True, stop=True,
                         tile_position=(0, 32 * gq))
        tC_tm = p.step("t", [tC_z, tC_iv, tC_sg, prevW["t"]], c_tmm, "C_tmm")
        def c_tb(v):
            v.tensor_copy(tbc[:, :], pf_a[:, 0:T])
            v.tensor_copy(invf_col[:, :], pf_b[:, 1024:1025])
            v.tensor_copy(sgn_col[:, :], pf_b[:, 1026:1027])
        tC_tb = p.step("v", [tC_tm], c_tb, "C_tb")
        def c_rope1(v):
            v.tensor_scalar(tbc, tbc, invf_col[:, 0:1], None, OP.mult)
            v.drain()
            v.tensor_scalar_mul(s2t, tbc, 1.0 / (2.0 * math.pi))
        tC_r1 = p.step("v", [tC_tb], c_rope1, "C_rope1")
        def c_rope2(s):
            s.activation(c2t, s2t, AF.Copy, bias=MR, scale=1.0)
        tC_r2 = p.step("s", [tC_r1], c_rope2, "C_rope2")
        def c_rope3(v):
            v.tensor_scalar_add(c2t, c2t, -MR)
            v.drain()
            v.tensor_sub(s2t, s2t, c2t)
        tC_r3 = p.step("v", [tC_r2], c_rope3, "C_rope3")
        dbg_r_tok = []
        if dbg_on("r"):
            r_dbg = dbg_out("r_dbg", (128, T))
            _, d_rd = p.dma_step("g", [(r_dbg[:, :], s2t)], [tC_r3], "dbg", "C_rdbg")
            dbg_r_tok = [d_rd]
        def c_rope4(s):
            s.activation(c2t, s2t, AF.Sin, bias=math.pi / 2.0,
                         scale=-math.pi)
            s.drain()
            s.activation(s2t, s2t, AF.Sin, bias=0.0, scale=math.pi)
            s.drain()
            s.activation(cos_fm[:, :], s2t, AF.Square, bias=0.0, scale=1.0)
        tC_r4 = p.step("s", [tC_r3] + dbg_r_tok, c_rope4, "C_rope4")
        def c_rope5(v):
            v.tensor_scalar(cos_fm[:, :], cos_fm[:, :], -2.0, 1.0, OP.mult, OP.add)
            v.drain()
            v.tensor_mul(sin_fm[:, :], s2t, c2t)
            v.drain()
            v.tensor_scalar(sin_fm[:, :], sin_fm[:, :], sgn_col[:, 0:1], 2.0,
                            OP.mult, OP.mult)
            v.drain()
            v.tensor_copy(kvb_rep[:, :].rearrange("p (h l) -> p h l", l=32),
                          kvb_row[:, :].unsqueeze(1).to_broadcast([1, 16, 32]))
        tC_r5 = p.step("v", [tC_r4, dC_r], c_rope5, "C_rope5")

        def c_kvbc(t):
            t.matmul(pf_b[:, 512:1024], ones_row[:, :], kvb_rep[:, :],
                     start=True, stop=True)
        tC_kvm = p.step("t", [tC_r5, tC_tm], c_kvbc, "C_kvbm")
        def c_kvbe(v):
            v.tensor_copy(kvb_bc[:, :], pf_b[:, 512:1024])
        tC_kv = p.step("v", [tC_kvm], c_kvbe, "C_kvbe")

        def c_cabc(t):
            for d3c in range(24):
                t.matmul(pf_b[:, 1040 + d3c:1041 + d3c],
                         cab_bf[:, d3c * 128:(d3c + 1) * 128],
                         ones_row[0:1, 0:1], start=True, stop=True)
        tC_cabm = p.step("t", [dC_r, tC_tm], c_cabc, "C_cabm")
        def c_cabe(v):
            v.tensor_copy(cab_cols[:, :], pf_b[:, 1040:1064])
        tC_cc = p.step("v", [tC_cabm], c_cabe, "C_cabe")

        tC_w, dC_w = p.dma_step("g", [
            (ca_sb[:, ic * 3072:(ic + 1) * 3072], caT[ic * 128:(ic + 1) * 128, :])
            for ic in range(8)], [prevW["g"], tC_cols], "wld", "C_ca")

        NTTC = T // 256
        d_inC_prev = []
        t_xnT_dma = None
        t_prev_pfbC = prevW["t"]
        t_prev_tailC = None
        t_mx = tC_z
        t_ptdma = None
        for b in range(B):
            for ti in range(NTTC):
                gi = b * NTTC + ti
                row0 = b * T + ti * 256
                t0 = ti * 256
                t_in, d_in = p.dma_step("y", [
                    (x1_t[:, s * C:(s + 1) * C],
                     x1_s[row0 + s * 128:row0 + (s + 1) * 128, :])
                    for s in range(2)], [prevW["y"], tC_r5] + d_inC_prev,
                    "xin", f"C_in{gi}")

                t_tr_toks = []
                for s in range(2):
                    xsub = x1_t[:, s * C:(s + 1) * C]
                    def stC(v, xsub=xsub):
                        v.bn_stats(bnsC[:, 0:6], xsub[:, 0:512])
                        v.bn_stats(bnsC[:, 6:12], xsub[:, 512:1024])
                        v.drain()
                        v.bn_aggr(agC[:, :], bnsC[:, :])
                    t1 = p.step("v", [d_in, tC_r5], stC, f"C_ln{gi}_{s}")
                    def rsC(sc):
                        sc.activation(rstdC[:, :], agC[:, 1:2], AF.Ln, bias=LN_EPS,
                                      scale=1.0)
                        sc.drain()
                        sc.activation(rstdC[:, :], rstdC[:, :], AF.Exp, bias=0.0,
                                      scale=-0.5)
                    t2 = p.step("s", [t1], rsC, f"C_rstd{gi}_{s}")
                    def nmC(v):
                        v.tensor_scalar(nmrC[:, :], agC[:, 0:1], rstdC[:, 0:1], -1.0,
                                        OP.mult, OP.mult)
                    t3 = p.step("v", [t2], nmC, f"C_nmr{gi}_{s}")
                    def apC(v, xsub=xsub):
                        v.tensor_scalar(xn2_f[:, :], xsub, rstdC[:, 0:1], nmrC[:, 0:1],
                                        OP.mult, OP.add)
                        v.drain()
                        v.tensor_reduce(mxsc[:, :], xn2_f[:, :], axis=AX.X,
                                        op=OP.max, apply_absolute_value=True)
                        v.drain()
                        v.tensor_max(mx_col[:, :], mx_col[:, :], mxsc[:, :])
                    t4 = p.step("v", [t3, t_mx], apC, f"C_xn2{gi}_{s}")
                    t_mx = t4
                    def trC(t):
                        for ic in range(8):
                            t.transpose(pf_b[:, ic * 128:(ic + 1) * 128],
                                        xn2_f[:, ic * 128:(ic + 1) * 128], identf[:, :])
                    depsT = [t4, t_prev_pfbC] + ([t_tr_toks[-1]] if t_tr_toks else [])
                    t5 = p.step("t", depsT, trC, f"C_tr{gi}_{s}")
                    def evC(v, s=s):
                        v.tensor_copy(
                            xnT_st[:, :].rearrange("p (i w) -> p i w", w=256)[:, :, s * 128:(s + 1) * 128],
                            pf_b[:, 0:1024].rearrange("p (i w) -> p i w", w=128))
                    t6 = p.step("v", [t5], evC, f"C_trev{gi}_{s}")
                    t_tr_toks.append(t6)
                    t_prev_pfbC = t6
                def cstC2(sc):
                    sc.copy(xnT_bfc[:, :], xnT_st[:, :])
                t_xc = p.step("s", t_tr_toks, cstC2, f"C_xc{gi}")
                xnTdeps = t_tr_toks + ([t_xnT_dma] if t_xnT_dma else [])
                t_xnT_dma, d_xnT = p.dma_step("g", [
                    (xnT_s[ic * 128:(ic + 1) * 128, row0:row0 + 256],
                     xnT_st[:, ic * 256:(ic + 1) * 256]) for ic in range(8)],
                    xnTdeps, "out", f"C_xnT{gi}")
                d_inC_prev = [d_xnT]

                t_ca = {}
                t_cev = {}
                for d3c in range(24):
                    def camm1(t, d3c=d3c):
                        for ic in range(8):
                            t.matmul(pf_a[:, (d3c % 4) * 512:(d3c % 4) * 512 + 256],
                                     ca_sb[:, ic * 3072 + d3c * 128:ic * 3072 + (d3c + 1) * 128],
                                     xnT_bfc[:, ic * 256:(ic + 1) * 256],
                                     start=(ic == 0), stop=(ic == 7))
                    deps = [t_xc, dC_w, tC_tm]
                    if t_prev_tailC and d3c == 0:
                        deps.append(t_prev_tailC)
                    if d3c >= 4:
                        deps.append(t_cev[d3c - 4])
                    t_ca[d3c] = p.step("t", deps, camm1, f"C_ca{gi}_{d3c}")
                    def cev(v, d3c=d3c):
                        v.tensor_scalar(qkv_bf[:, d3c * 256:(d3c + 1) * 256],
                                        pf_a[:, (d3c % 4) * 512:(d3c % 4) * 512 + 256],
                                        cab_cols[:, d3c:d3c + 1], None, OP.add)
                    t_cev[d3c] = p.step("v", [t_ca[d3c], tC_cc], cev, f"C_cev{gi}_{d3c}")

                def lat_q(t):
                    for g_ in range(8):
                        for s in range(2):
                            t.matmul(pf_b[64 * (g_ % 2):64 * (g_ % 2) + 64,
                                          (g_ // 2) * 256 + s * 128:
                                          (g_ // 2) * 256 + (s + 1) * 128],
                                     q2lBD[:, :],
                                     qkv_bf[:, g_ * 256 + s * 128:g_ * 256 + (s + 1) * 128],
                                     start=True, stop=True,
                                     tile_position=(0, 64 * (g_ % 2)))
                t_lq = p.step("t", [t_cev[7], t_prev_pfbC, dC_r], lat_q, f"C_lq{gi}")
                def lat_k(t):
                    for g_ in range(8):
                        for s in range(2):
                            t.matmul(pf_a[64 * (g_ % 2):64 * (g_ % 2) + 64,
                                          1024 + (g_ // 2) * 256 + s * 128:
                                          1024 + (g_ // 2) * 256 + (s + 1) * 128],
                                     kv2lBD[:, :],
                                     qkv_bf[:, (8 + g_) * 256 + s * 128:
                                            (8 + g_) * 256 + (s + 1) * 128],
                                     start=True, stop=True,
                                     tile_position=(0, 64 * (g_ % 2)))
                t_lk = p.step("t", [t_cev[22], t_cev[23]], lat_k, f"C_lk{gi}")

                cosv = cos_fm[:, t0:t0 + 256].unsqueeze(1).to_broadcast([128, 4, 256])
                sinv = sin_fm[:, t0:t0 + 256].unsqueeze(1).to_broadcast([128, 4, 256])
                qrv = qr_f[:, :].rearrange("p (hg w) -> p hg w", w=256)
                qsv = qsw[:, :].rearrange("p (hg w) -> p hg w", w=256)

                def rope_q(v):
                    v.tensor_scalar(qr_f[:, :], pf_b[:, 0:1024], qb_col[:, 0:1],
                                    None, OP.add)
                    v.drain()
                    v.stream_shuffle(qsw[:, :], qr_f[:, :], pairswap)
                    v.drain()
                    v.tensor_tensor(qrv, qrv, cosv, OP.mult)
                    v.tensor_tensor(qsv, qsv, sinv, OP.mult)
                    v.drain()
                    v.tensor_add(qr_f[:, :], qr_f[:, :], qsw[:, :])
                t_rq = p.step("v", [t_lq, dC_cols, tC_r5], rope_q, f"C_ropeq{gi}")
                def rq_cast(sc):
                    sc.copy(qr_bf[:, :], qr_f[:, :])
                t_rqc = p.step("s", [t_rq], rq_cast, f"C_rqc{gi}")

                klv = klatT[:, :].rearrange("p (bb hg w) -> p (bb hg) w", hg=4, w=T)[
                    :, 4 * b:4 * b + 4, t0:t0 + 256]
                def rope_k(v, klv=klv):
                    v.tensor_scalar(qsw[:, :], pf_a[:, 1024:2048], kvb_col[:, 0:1],
                                    None, OP.add)
                    v.drain()
                    v.stream_shuffle(qr_f[:, :], qsw[:, :], pairswap)
                    v.drain()
                    v.tensor_tensor(qsv, qsv, cosv, OP.mult)
                    v.tensor_tensor(qrv, qrv, sinv, OP.mult)
                    v.drain()
                    v.tensor_add(qsw[:, :], qsw[:, :], qr_f[:, :])
                    v.drain()
                    v.tensor_copy(klv, qsv)
                t_rk = p.step("v", [t_lk, t_rqc, t_rq], rope_k, f"C_ropek{gi}")

                def qfk_mm(t):
                    for hg in range(4):
                        t.matmul(pf_a[:, hg * 256:(hg + 1) * 256], wqkBD[:, :],
                                 qr_bf[:, hg * 256:(hg + 1) * 256],
                                 start=True, stop=True)
                t_qf = p.step("t", [t_rqc, t_cev[23], t_lk], qfk_mm, f"C_qfk{gi}")
                qfv = qfkT[:, :].rearrange("p (bb hg w) -> p (bb hg) w", hg=4, w=T)[
                    :, 4 * b:4 * b + 4, t0:t0 + 256]
                def qfk_ev(v, qfv=qfv):
                    v.tensor_scalar(qfv,
                                    pf_a[:, 0:1024].rearrange("p (hg w) -> p hg w", w=256),
                                    wqkb_col[:, 0:1], cpool[Lm05][:, 0:1],
                                    OP.add, OP.mult)
                t_qfe = p.step("v", [t_qf, dC_cols], qfk_ev, f"C_qfe{gi}")

                t_lv_toks = []
                for s in range(2):
                    kt = ti * 2 + s
                    blk = (b * NPOS + kt) * 528
                    def lat_v(t, s=s):
                        for i, h in enumerate([0, 2, 4, 6, 8, 10, 12, 14, 1, 3, 5, 7, 9, 11, 13, 15]):
                            if i == 8:
                                t.drain()
                            t.matmul(pf_b[:, 1024 + h * 32:1024 + (h + 1) * 32],
                                     qkv_bf[:, (16 + h // 2) * 256 + s * 128:
                                            (16 + h // 2) * 256 + (s + 1) * 128]
                                     [64 * (h % 2):64 * (h % 2) + 64, :],
                                     kv2l_rep[64 * (h % 2):64 * (h % 2) + 64, :],
                                     start=True, stop=True)
                    deps = [t_cev[23], tC_r5] + ([t_lv_toks[-1]] if t_lv_toks else [])
                    t_lvm = p.step("t", deps + [t_qf], lat_v, f"C_lv{gi}_{s}")
                    def lat_vev(v, blk=blk):
                        vIblk = vI[:, blk:blk + 528].rearrange("p (h l) -> p h l", l=33)
                        v.tensor_tensor(vIblk[:, :, 0:32],
                                        pf_b[:, 1024:1536].rearrange("p (h l) -> p h l", l=32),
                                        kvb_bc[:, :].rearrange("p (h l) -> p h l", l=32),
                                        OP.add)
                        v.drain()
                        v.memset(vIblk[:, :, 32:33], 1.0)
                    t_lve = p.step("v", [t_lvm, tC_kv], lat_vev, f"C_lvev{gi}_{s}")
                    t_lv_toks.append(t_lve)

                t_prev_tailC = t_qfe
                t_prev_pfbC = t_lv_toks[-1]

        def mxfin_mm(t):
            t.matmul(pf_b[0:1, 1024:1152], mx_col[:, :], identf[:, :],
                     start=True, stop=True)
        t_mm1 = p.step("t", [t_mx, t_prev_tailC, p.tok("v")], mxfin_mm, "C_mxmm")
        def mxfin_v(v):
            v.tensor_reduce(gmax_x[:, :], pf_b[0:1, 1024:1152], axis=AX.X, op=OP.max)
        t_mv1 = p.step("v", [t_mm1], mxfin_v, "C_mxv")
        tcc1, dcc1_out = p.dma_step("g", [(ccxi.ap()[0:1, 0:1], gmax_x[:, :])],
                                    [t_mv1], "ccio", "C_ccout")
        cc_inc = 16 if cfg.get("skip_cc") else 1
        if cfg.get("skip_cc"):
            def cc1(g):
                g.dma_start(out=ccxo.ap()[:, :], in_=ccxi.ap()[:, :],
                            ).then_inc(cc_sem, 16)
        else:
            def cc1(g):
                g.collective_compute(
                    "AllReduce", OP.max, replica_groups=RG,
                    ins=[ccxi.ap().opt()], outs=[ccxo.ap().opt()],
                ).then_inc(cc_sem, 1)
        t_cc1 = p.step("g", [tcc1, dcc1_out], cc1, "C_cc1")
        p.dsems["cc0"][1] += cc_inc
        dcc1 = DmaTok((("cc0", cc_inc),))
        tcc1b, dcc1b = p.dma_step("g", [(gmax_x[:, :], ccxo.ap()[0:1, 0:1])],
                                  [t_cc1, dcc1], "ccio", "C_ccin")

        if dbg_on("cs"):
            cs_dbg = dbg_out("cs_dbg", (128, 2 * T))
            iv_dbg = dbg_out("iv_dbg", (128, 1))
            sg_dbg = dbg_out("sg_dbg", (128, 1))
            p.dma_step("g", [(cs_dbg[:, 0:T], cos_fm[:, :]),
                             (cs_dbg[:, T:2 * T], sin_fm[:, :]),
                             (iv_dbg[:, :], invf_col[:, :]),
                             (sg_dbg[:, :], sgn_col[:, :])],
                       [tC_r5], "dbg", "C_csdbg")
            kl_dbg = dbg_out("kl_dbg", (128, B * 4 * T))
            qf_dbg = dbg_out("qf_dbg", (128, B * 4 * T))
            vI_dbg = dbg_out("vI_dbg", (128, B * NPOS * 528))
            p.dma_step("g", [(kl_dbg[:, :], klatT[:, :]), (qf_dbg[:, :], qfkT[:, :]),
                             (vI_dbg[:, :], vI[:, :])],
                       [p.tok("v"), p.tok("s"), p.tok("t")], "dbg", "C_dbg")

        pC_done = p.barrier("pC_done")
        arenaC.close()

    pt_dbg = dbg_out("pt_dbg", (NT, C))
    if "D" in phases:
        arenaD, AD = mkarena()
        cp_sb = AD("cp_sb", [128, 8 * C], BF16)
        E_bf = AD("E_bf", [128, 2048], BF16)
        AVZ_bf = AD("AVZ_bf", [33, 2048], BF16)
        zrow_t = AD("zrow_t", [33, 2048], F32)
        onef33 = AD("onef33", [33, 1], F32)
        nu_st = AD("nu_st", [128, 32], F32)
        nu_bf = AD("nu_bf", [128, 32], BF16)
        negc_col = AD("negc_col", [16, 2], F32)
        negc_bf = AD("negc_bf", [16, 2], BF16)
        negrow_bf = AD("negrow_bf", [1, 32], BF16)
        negC_cols = AD("negC_cols", [128, 32], F32)
        invZ_c = AD("invZ_c", [128, 16], F32)
        y_sb = AD("y_sb", [128, 1024], BF16)
        yT_bf = AD("yT_bf", [128, 1024], BF16)
        pt_f = AD("pt_f", [128, 1024], F32)
        cpb_bc = AD("cpb_bc", [128, C], F32)
        x1c = AD("x1c", [128, 2 * C], F32)
        mxD = AD("mxD", [128, 1], F32)

        tD_w, dD_w = p.dma_step("g", [
            (cp_sb[:, ic * C:(ic + 1) * C], cpT[ic * 128:(ic + 1) * 128, :])
            for ic in range(8)], [pC_done["g"]], "wld", "D_cp")
        def d_prep(g):
            g.memset(onef33[:], 1.0)
        tD_pr = p.step("g", [pC_done["g"]], d_prep, "D_prep")
        def d_pbm(t):
            for hh in range(2):
                t.matmul(pf_a[:, hh * 512:(hh + 1) * 512], ones_row[:, :],
                         cpb_bf[:, hh * 512:(hh + 1) * 512], start=True, stop=True)
        tD_pb = p.step("t", [pC_done["t"]], d_pbm, "D_pbm")
        def d_pbe(v):
            v.tensor_copy(cpb_bc[:, :], pf_a[:, 0:1024])
        tD_pe = p.step("v", [tD_pb, pC_done["v"]], d_pbe, "D_pbe")

        def kl_sl(b_, h, kt):
            colb = (b_ * 4 + h // 4) * T
            return klatT[32 * (h % 4):32 * (h % 4) + 32,
                         colb + kt * 128:colb + (kt + 1) * 128]
        def qf_sl(b_, h, qt):
            colb = (b_ * 4 + h // 4) * T
            return qfkT[32 * (h % 4):32 * (h % 4) + 32,
                        colb + qt * 128:colb + (qt + 1) * 128]
        def vI_sl(b_, kt, h):
            return vI[:, (b_ * NPOS + kt) * 528 + 33 * h:
                      (b_ * NPOS + kt) * 528 + 33 * h + 33]

        t_ptdma = None
        d_pt_prev = []
        t_prev_pbfD = pC_done["t"]
        t_prev_cproj = None
        t_prev_zro = [pC_done["s"], pC_done["v"]]
        dx1_toks = {}
        t_x1last = None
        last_slot_tok = [None, None]
        last_Eslot_tok = [[None, None], [None, None]]
        prev_ct = [{}, {}]
        NGQ = B * NPOS

        class DCtx:
            pass

        def emit_round1(cx, i):
            gq, par, b_, qt = cx.gq, cx.par, cx.b, cx.qt
            h, g = divmod(i, cx.ng)
            k0 = g * 4
            k1 = min(qt + 1, k0 + 4)
            def r1(t, h=h, k0=k0, k1=k1, par=par, b_=b_, qt=qt):
                tp = (32 * (h % 4), 0) if h % 4 == 3 else None
                for kt in range(k0, k1):
                    t.matmul(pf_b[:, par * 512 + (kt - k0) * 128:
                                  par * 512 + (kt - k0 + 1) * 128],
                             kl_sl(b_, h, kt), qf_sl(b_, h, qt),
                             start=True, stop=True, tile_position=tp)
            deps = [pC_done["t"]]
            if i > 0:
                deps.append(cx.t_vmax[i - 1])
            elif last_slot_tok[par] is not None:
                deps.append(last_slot_tok[par])
            cx.t_r1[i] = p.step("t", deps, r1, f"D_r1_{gq}_{h}_{g}")
            def vmax(v, h=h, g=g, k0=k0, k1=k1, par=par):
                if g == 0:
                    v.tensor_reduce(nu_st[:, par * 16 + h:par * 16 + h + 1],
                                    pf_b[:, par * 512:par * 512 + (k1 - k0) * 128],
                                    axis=AX.X, op=OP.max)
                else:
                    v.tensor_reduce(mxD[:, :],
                                    pf_b[:, par * 512:par * 512 + (k1 - k0) * 128],
                                    axis=AX.X, op=OP.max)
                    v.drain()
                    v.tensor_max(nu_st[:, par * 16 + h:par * 16 + h + 1],
                                 nu_st[:, par * 16 + h:par * 16 + h + 1],
                                 mxD[:, :])
            cx.t_vmax[i] = p.step("v", [cx.t_r1[i]], vmax, f"D_vm_{gq}_{h}_{g}")
            if i == cx.nidx - 1:
                last_slot_tok[par] = cx.t_vmax[i]

        def emit_ctail(cx):
            gq, par = cx.gq, cx.par
            pc = prev_ct[par]
            ct2 = pf_b[0:16, 1056 + par * 128:1184 + par * 128]
            ct4 = pf_b[0:1, 1312 + par * 16:1328 + par * 16]
            ct6 = pf_b[:, 1344 + par * 16:1360 + par * 16]
            def c_t1(v, par=par):
                v.tensor_copy(nu_bf[:, par * 16:(par + 1) * 16],
                              nu_st[:, par * 16:(par + 1) * 16])
            t_ct1 = p.step("v", [cx.t_vmax[cx.nidx - 1]], c_t1, f"D_ct1_{gq}")
            def c_t2(t, par=par, ct2=ct2):
                t.matmul(ct2, nu_bf[:, par * 16:(par + 1) * 16], ident[:, :],
                         start=True, stop=True)
            d2 = [t_ct1, cx.t_vmax[cx.nidx - 1]]
            if "t3" in pc:
                d2.append(pc["t3"])
            t_ct2 = p.step("t", d2, c_t2, f"D_ct2_{gq}")
            def c_t3(v, par=par, ct2=ct2):
                v.tensor_reduce(negc_col[:, par:par + 1], ct2, axis=AX.X,
                                op=OP.max, negate=True)
                v.drain()
                v.tensor_copy(negc_bf[:, par:par + 1], negc_col[:, par:par + 1])
            t_ct3 = p.step("v", [t_ct2], c_t3, f"D_ct3_{gq}")
            pc["t3"] = t_ct3
            def c_t4(t, par=par, ct4=ct4):
                t.matmul(ct4, negc_bf[:, par:par + 1], ident[0:16, 0:16],
                         start=True, stop=True)
            d4 = [t_ct3] + ([pc["t5"]] if "t5" in pc else [])
            t_ct4 = p.step("t", d4, c_t4, f"D_ct4_{gq}")
            def c_t5(v, par=par, ct4=ct4):
                v.tensor_copy(negrow_bf[:, par * 16:(par + 1) * 16], ct4)
            t_ct5 = p.step("v", [t_ct4], c_t5, f"D_ct5_{gq}")
            pc["t5"] = t_ct5
            def c_t6(t, par=par, ct6=ct6):
                t.matmul(ct6, ones_row[:, :],
                         negrow_bf[:, par * 16:(par + 1) * 16],
                         start=True, stop=True)
            d6 = [t_ct5] + ([pc["t7"]] if "t7" in pc else [])
            t_ct6 = p.step("t", d6, c_t6, f"D_ct6_{gq}")
            def c_t7(v, par=par, ct6=ct6):
                v.tensor_copy(negC_cols[:, par * 16:(par + 1) * 16], ct6)
            t_ct7 = p.step("v", [t_ct6], c_t7, f"D_ct7_{gq}")
            pc["t7"] = t_ct7
            cx.t_ct7 = t_ct7

        def emit_r2exp(cx, i):
            gq, par, b_, qt = cx.gq, cx.par, cx.b, cx.qt
            h, g = divmod(i, cx.ng)
            k0 = g * 4
            k1 = min(qt + 1, k0 + 4)
            es = (par * 2 + (i % 2)) * 512
            def r2(t, h=h, k0=k0, k1=k1, par=par, b_=b_, qt=qt):
                tp = (32 * (h % 4), 0) if h % 4 == 3 else None
                for kt in range(k0, k1):
                    t.matmul(pf_b[:, par * 512 + (kt - k0) * 128:
                                  par * 512 + (kt - k0 + 1) * 128],
                             kl_sl(b_, h, kt), qf_sl(b_, h, qt),
                             start=True, stop=True, tile_position=tp)
            deps = [cx.t_ct7]
            if i > 0:
                deps.append(cx.t_exp[i - 1])
            cx.t_r2[i] = p.step("t", deps, r2, f"D_r2_{gq}_{h}_{g}")
            def expf(sc, h=h, k0=k0, k1=k1, par=par, es=es):
                sc.activation(E_bf[:, es:es + (k1 - k0) * 128],
                              pf_b[:, par * 512:par * 512 + (k1 - k0) * 128],
                              AF.Exp, bias=negC_cols[:, par * 16 + h:par * 16 + h + 1],
                              scale=1.0)
            deps_e = [cx.t_r2[i], cx.t_ct7]
            if i > 1:
                deps_e.append(cx.t_av[i - 2])
            elif last_Eslot_tok[par][i % 2] is not None:
                deps_e.append(last_Eslot_tok[par][i % 2])
            cx.t_exp[i] = p.step("s", deps_e, expf, f"D_ex_{gq}_{h}_{g}")
            cx.tok_e[i] = cx.t_exp[i]
            if k1 - 1 == qt:
                def mskf(v, k0=k0, k1=k1, es=es):
                    sl = E_bf[:, es + (k1 - 1 - k0) * 128:es + (k1 - k0) * 128]
                    v.tensor_mul(sl, sl, maskCa[:, :])
                cx.tok_e[i] = p.step("v", [cx.t_exp[i]], mskf, f"D_msk_{gq}_{h}")

        def emit_av(cx, i):
            gq, par, b_, qt = cx.gq, cx.par, cx.b, cx.qt
            h, g = divmod(i, cx.ng)
            k0 = g * 4
            k1 = min(qt + 1, k0 + 4)
            es = (par * 2 + (i % 2)) * 512
            def avf(t, h=h, k0=k0, k1=k1, es=es, b_=b_, qt=qt):
                for kt in range(k0, k1):
                    t.matmul(pf_a[0:33, h * 128:(h + 1) * 128],
                             vI_sl(b_, kt, h),
                             E_bf[:, es + (kt - k0) * 128:es + (kt - k0 + 1) * 128],
                             start=(kt == 0), stop=(kt == qt))
            deps_a = [cx.tok_e[i]] + (cx.zro + [tD_pe] if i == 0 else [])
            cx.t_av[i] = p.step("t", deps_a, avf, f"D_av_{gq}_{h}_{g}")
            if i == cx.nidx - 1:
                last_slot_tok[par] = cx.t_exp[i]
                last_Eslot_tok[par] = [cx.t_av[i - 1] if i > 0 else None,
                                       cx.t_av[i]] if i % 2 == 1 else \
                                      [cx.t_av[i],
                                       cx.t_av[i - 1] if i > 0 else None]

        def emit_tail(cx):
            nonlocal t_prev_pbfD, t_prev_cproj, t_prev_zro, t_ptdma, d_pt_prev
            gq, par, row0 = cx.gq, cx.par, cx.row0
            zt = pf_b[:, 1376 + par * 16:1392 + par * 16]
            t_last_av = cx.t_av[cx.nidx - 1]
            def d_ave(sc):
                sc.copy(AVZ_bf[:, :], pf_a[0:33, :])
                sc.drain()
                sc.copy(zrow_t[32:33, :], pf_a[32:33, :])
            t_ave = p.step("s", [t_last_av], d_ave, f"D_ave_{gq}")
            def d_zt(t, zt=zt):
                for h in range(16):
                    t.matmul(zt[:, h:h + 1],
                             zrow_t[32:33, h * 128:(h + 1) * 128],
                             onef33[32:33, :], start=True, stop=True)
            dz = [t_ave, t_last_av, tD_pr]
            if "zi" in prev_ct[par]:
                dz.append(prev_ct[par]["zi"])
            t_zt = p.step("t", dz, d_zt, f"D_zt_{gq}")
            def d_iz(v, zt=zt):
                v.reciprocal(invZ_c[:, :], zt)
            t_iz = p.step("v", [t_zt], d_iz, f"D_iz_{gq}")
            prev_ct[par]["zi"] = t_iz
            def d_l2o(t):
                for h in range(16):
                    t.matmul(pf_a[:, 1024 + h * 64:1024 + (h + 1) * 64],
                             AVZ_bf[:, h * 128:(h + 1) * 128], l2oI[:, :],
                             start=True, stop=True)
            t_l2o = p.step("t", [t_ave, t_iz], d_l2o, f"D_l2o_{gq}")
            def d_ye(v):
                for h in range(16):
                    v.tensor_scalar(y_sb[:, h * 64:(h + 1) * 64],
                                    pf_a[:, 1024 + h * 64:1024 + (h + 1) * 64],
                                    invZ_c[:, h:h + 1], None, OP.mult)
            t_ye = p.step("v", [t_l2o, t_iz], d_ye, f"D_ye_{gq}")
            def d_ytr(t):
                for ic in range(8):
                    t.transpose(pbf[:, ic * 128:(ic + 1) * 128],
                                y_sb[:, ic * 128:(ic + 1) * 128], ident[:, :])
            t_ytr = p.step("t", [t_ye, t_prev_pbfD], d_ytr, f"D_ytr_{gq}")
            def d_ytre(sc):
                sc.copy(yT_bf[:, :], pbf[:, :])
            t_ytre = p.step("s", [t_ytr], d_ytre, f"D_ytre_{gq}")
            t_prev_pbfD = t_ytre
            def d_cproj(t):
                for hh in range(2):
                    for ic in range(8):
                        t.matmul(pf_a[:, hh * 512:(hh + 1) * 512],
                                 yT_bf[:, ic * 128:(ic + 1) * 128],
                                 cp_sb[:, ic * C + hh * 512:ic * C + (hh + 1) * 512],
                                 start=(ic == 0), stop=(ic == 7))
            t_cproj = p.step("t", [t_ytre, dD_w, t_ave], d_cproj, f"D_cp_{gq}")
            t_prev_cproj = t_cproj
            def d_pt(v, par=par):
                v.tensor_add(pt_f[:, :], x1c[:, par * C:(par + 1) * C],
                             pf_a[:, 0:1024])
                v.drain()
                v.tensor_add(pt_f[:, :], pt_f[:, :], cpb_bc[:, :])
            t_pt = p.step("v", [t_cproj, dx1_toks[gq], tD_pe] + d_pt_prev, d_pt,
                          f"D_pt_{gq}")
            t_prev_zro = [t_pt]
            ptdeps = [t_pt] + ([t_ptdma] if t_ptdma else [])
            t_ptdma, d_ptd = p.dma_step("g", [(pt_s[row0:row0 + 128, :],
                                               pt_f[:, :])], ptdeps, "out",
                                        f"D_ptd_{gq}")
            d_pt_prev = [d_ptd]
            if dbg_on("pt"):
                _, d_ptdbg = p.dma_step("g", [(pt_dbg[row0:row0 + 128, :],
                                               pt_f[:, :])],
                                        [t_pt, t_ptdma], "dbg", f"D_ptdbg_{gq}")
                d_pt_prev.append(d_ptdbg)

        prev_cx = None
        for gq in range(NGQ + 1):
            cx = None
            if gq < NGQ:
                cx = DCtx()
                cx.gq = gq
                cx.b = gq // NPOS
                cx.qt = gq % NPOS
                cx.par = gq % 2
                cx.row0 = cx.b * T + cx.qt * 128
                cx.ng = (cx.qt + 1 + 3) // 4
                cx.nidx = 16 * cx.ng
                cx.t_r1, cx.t_vmax = {}, {}
                cx.t_r2, cx.t_exp, cx.t_av, cx.tok_e = {}, {}, {}, {}
                if gq == 0:
                    t_x1last, dx1_toks[0] = p.dma_step(
                        "y", [(x1c[:, 0:C], x1_s[cx.row0:cx.row0 + 128, :])],
                        [pC_done["y"]], "xin", "D_x0")
            n1 = cx.nidx if cx else 0
            n2 = prev_cx.nidx if prev_cx else 0
            if prev_cx:
                prev_cx.zro = t_prev_zro
            for i in range(max(n1, n2)):
                if prev_cx and i < n2:
                    emit_r2exp(prev_cx, i)
                if cx and i < n1:
                    emit_round1(cx, i)
                if prev_cx and i < n2:
                    emit_av(prev_cx, i)
            if cx:
                emit_ctail(cx)
            if prev_cx:
                emit_tail(prev_cx)
            if cx and gq + 1 < NGQ:
                nb, nqt = (gq + 1) // NPOS, (gq + 1) % NPOS
                nrow = nb * T + nqt * 128
                deps = [t_x1last] + d_pt_prev
                t_x1last, dx1_toks[gq + 1] = p.dma_step(
                    "y", [(x1c[:, ((gq + 1) % 2) * C:((gq + 1) % 2 + 1) * C],
                           x1_s[nrow:nrow + 128, :])], deps, "xin",
                    f"D_x{gq+1}")
            prev_cx = cx

        pD_done = p.barrier("pD_done")
        arenaD.close()
        arenaCD.close()

    if "E" in phases:
        arenaE, AE = mkarena()
        wgq_sb = AE("wgq_sb", [128, 8 * ISW], BF16)
        wuq_sb = AE("wuq_sb", [128, 8 * ISW], BF16)
        xstgE = AE("xstgE", [128, 2 * 4096], F32)
        Xq_bf = AE("Xq_bf", [128, 2 * 4096], BF16)
        silu_g = AE("silu_g", [128, 512], F32)
        up_f = AE("up_f", [128, 512], F32)
        hbuf = AE("hbuf", [128, 2 * 512], F32)
        sgc = AE("sgc", [128, 22], F32)
        suc = AE("suc", [128, 22], F32)
        hmx_col = AE("hmx_col", [128, 1], F32)
        mxE = AE("mxE", [128, 1], F32)
        sgE = AE("sgE", [1, 1], F32)

        tE_w, dE_w = p.dma_step("g", [
            (wgq_sb[:, hc * ISW:(hc + 1) * ISW], wgq_s[hc * 128:(hc + 1) * 128, :])
            for hc in range(8)], [pD_done["g"]], "wld", "E_w")
        tE_w2, dE_w2 = p.dma_step("s", [
            (wuq_sb[:, hc * ISW:(hc + 1) * ISW], wuq_s[hc * 128:(hc + 1) * 128, :])
            for hc in range(8)], [pD_done["s"]], "wld", "E_w2")

        def e_sg(sc):
            sc.activation(sgE[:, :], clipx_r[:, :], AF.Sigmoid, bias=0.0, scale=1.0)
        tE_sg = p.step("s", [pD_done["s"], dcc1b], e_sg, "E_sg")
        def e_sx(v):
            v.tensor_mul(sgE[:, :], sgE[:, :], gmax_x[:, :])
            v.drain()
            v.tensor_scalar_mul(sgE[:, :], sgE[:, :], 1.0 / QMAX)
        tE_sx = p.step("v", [tE_sg, dcc1b, pD_done["v"]], e_sx, "E_sx")
        def e_sxb(t):
            t.matmul(pf_b[:, 0:1], onesf_row[:, :], sgE[:, :], start=True, stop=True)
        tE_sxb = p.step("t", [tE_sx, pD_done["t"]], e_sxb, "E_sxb")
        def e_sxc(v):
            v.tensor_copy(sx_col[:, :], pf_b[:, 0:1])
            v.drain()
            v.reciprocal(rsx_col[:, :], sx_col[:, :])
            v.drain()
            v.tensor_scalar(sgc[:, :], wsc_g[:, :], sx_col[:, 0:1], None, OP.mult)
            v.tensor_scalar(suc[:, :], wsc_u[:, :], sx_col[:, 0:1], None, OP.mult)
            v.memset(hmx_col[:, :], 0.0)
        tE_sxc = p.step("v", [tE_sxb], e_sxc, "E_sxc")

        WE = min(512, NT)
        NTT5 = NT // WE
        t_hdma = None
        d_h_prev_by = {}
        t_mxE = tE_sxc
        t_prev_g = None
        t_prev_u = None
        t_inE = [None, None]
        prev_qE = [None, None]
        prev_lastmm = [None, None]
        for tts in range(NTT5):
            c0 = tts * WE
            eb = tts % 2
            xst = xstgE[:, eb * 4096:(eb + 1) * 4096]
            xq = Xq_bf[:, eb * 4096:(eb + 1) * 4096]
            in_deps = [pD_done["y"]]
            for tok in (t_inE[eb], prev_qE[eb]):
                if tok is not None:
                    in_deps.append(tok)
            t_in, d_in = p.dma_step("y", [
                (xst[:, hc * WE:(hc + 1) * WE],
                 xnT_s[hc * 128:(hc + 1) * 128, c0:c0 + WE]) for hc in range(8)],
                in_deps, "xin", f"E_in{tts}")
            t_inE[eb] = t_in
            def e_q(v, xst=xst, xq=xq):
                v.tensor_scalar(xst[:, 0:8 * WE], xst[:, 0:8 * WE], rsx_col[:, 0:1], MR,
                                OP.mult, OP.add)
                v.drain()
                v.tensor_scalar(xst[:, 0:8 * WE], xst[:, 0:8 * WE], -MR, -(QMAX + 1.0),
                                OP.add, OP.max)
                v.drain()
                v.tensor_scalar(xq[:, 0:8 * WE], xst[:, 0:8 * WE], QMAX, None, OP.min)
            qdeps = [d_in, tE_sxc]
            if prev_lastmm[eb] is not None:
                qdeps.append(prev_lastmm[eb])
            t_q = p.step("v", qdeps, e_q, f"E_q{tts}")
            prev_qE[eb] = t_q

            t_gmm = {}
            t_umm = {}
            t_sil = {}
            t_upe = {}
            t_hop = {}
            for ic in range(22):
                def gmm(t, ic=ic, xq=xq):
                    for hc in range(8):
                        t.matmul(pf_a[:, (ic % 2) * 512:(ic % 2) * 512 + WE],
                                 wgq_sb[:, hc * ISW + ic * 128:hc * ISW + (ic + 1) * 128],
                                 xq[:, hc * WE:(hc + 1) * WE],
                                 start=(hc == 0), stop=(hc == 7))
                deps = [t_q, dE_w, dE_w2]
                if ic >= 2:
                    deps.append(t_sil[ic - 2])
                if ic == 0 and t_prev_g:
                    deps.append(t_prev_g)
                    deps.append(t_prev_u)
                if ic == 1 and t_prev_g:
                    deps.append(t_prev_g)
                    deps.append(t_prev_u)
                t_gmm[ic] = p.step("t", deps, gmm, f"E_g{tts}_{ic}")
                def sil(sc, ic=ic):
                    sc.activation(silu_g[:, 0:WE], pf_a[:, (ic % 2) * 512:(ic % 2) * 512 + WE],
                                  AF.Silu, bias=0.0, scale=sgc[:, ic:ic + 1])
                t_sil[ic] = p.step("s", [t_gmm[ic], tE_sxc] +
                                   ([t_hop[ic - 1]] if ic >= 1 else []),
                                   sil, f"E_s{tts}_{ic}")
                def umm(t, ic=ic, xq=xq):
                    for hc in range(8):
                        t.matmul(pf_a[:, 1024 + (ic % 2) * 512:1024 + (ic % 2) * 512 + WE],
                                 wuq_sb[:, hc * ISW + ic * 128:hc * ISW + (ic + 1) * 128],
                                 xq[:, hc * WE:(hc + 1) * WE],
                                 start=(hc == 0), stop=(hc == 7))
                deps_u = [t_gmm[ic]]
                if ic >= 2:
                    deps_u.append(t_hop[ic - 2])
                t_umm[ic] = p.step("t", deps_u, umm, f"E_u{tts}_{ic}")
                def upe(v, ic=ic):
                    v.tensor_scalar(up_f[:, 0:WE], pf_a[:, 1024 + (ic % 2) * 512:
                                                        1024 + (ic % 2) * 512 + WE],
                                    suc[:, ic:ic + 1], None, OP.mult)
                deps_ue = [t_umm[ic], tE_sxc] + ([t_hop[ic - 1]] if ic >= 1 else [])
                t_upe[ic] = p.step("v", deps_ue, upe, f"E_ue{tts}_{ic}")
                def hop(v, ic=ic):
                    v.tensor_mul(hbuf[:, (ic % 2) * 512:(ic % 2) * 512 + WE],
                                 silu_g[:, 0:WE], up_f[:, 0:WE])
                    v.drain()
                    v.tensor_reduce(mxE[:, :], hbuf[:, (ic % 2) * 512:(ic % 2) * 512 + WE],
                                    axis=AX.X, op=OP.max, apply_absolute_value=True)
                    v.drain()
                    v.tensor_max(hmx_col[:, :], hmx_col[:, :], mxE[:, :])
                deps_h = [t_upe[ic], t_sil[ic], t_mxE]
                dkey = (tts, ic - 2) if ic >= 2 else (tts - 1, 20 + ic)
                if dkey in d_h_prev_by:
                    deps_h.append(d_h_prev_by[dkey])
                t_hop[ic] = p.step("v", deps_h, hop, f"E_h{tts}_{ic}")
                t_mxE = t_hop[ic]
                hdeps = [t_hop[ic]] + ([t_hdma] if t_hdma else [])
                t_hdma, d_hd = p.dma_step("g", [
                    (h_s[ic * 128:(ic + 1) * 128, c0:c0 + WE],
                     hbuf[:, (ic % 2) * 512:(ic % 2) * 512 + WE])], hdeps,
                    "out", f"E_hd{tts}_{ic}")
                d_h_prev_by[(tts, ic)] = d_hd
            t_prev_g = t_sil[21]
            t_prev_u = t_upe[21]
            prev_lastmm[eb] = t_umm[21]

        def e_mxmm(t):
            t.matmul(pf_b[0:1, 2:130], hmx_col[:, :], identf[:, :],
                     start=True, stop=True)
        t_em = p.step("t", [t_mxE, t_prev_g, t_prev_u], e_mxmm, "E_mxmm")
        def e_mxv(v):
            v.tensor_reduce(gmax_h[:, :], pf_b[0:1, 2:130], axis=AX.X, op=OP.max)
        t_ev2 = p.step("v", [t_em], e_mxv, "E_mxv")
        tcc2, dcc2_out = p.dma_step("g", [(cchi.ap()[0:1, 0:1], gmax_h[:, :])],
                                    [t_ev2], "ccio", "E_ccout")
        if cfg.get("skip_cc"):
            def cc2(g):
                g.dma_start(out=ccho.ap()[:, :], in_=cchi.ap()[:, :],
                            ).then_inc(cc_sem, 16)
        else:
            def cc2(g):
                g.collective_compute(
                    "AllReduce", OP.max, replica_groups=RG,
                    ins=[cchi.ap().opt()], outs=[ccho.ap().opt()],
                ).then_inc(cc_sem, 1)
        t_cc2 = p.step("g", [tcc2, dcc2_out], cc2, "E_cc2")
        p.dsems["cc0"][1] += cc_inc
        dcc2 = DmaTok((("cc0", 2 * cc_inc),))
        tcc2b, dcc2b = p.dma_step("g", [(gmax_h[:, :], ccho.ap()[0:1, 0:1])],
                                  [t_cc2, dcc2], "ccio", "E_ccin")

        pE_done = p.barrier("pE_done")
        arenaE.close()

    if "F" in phases:
        arenaF, AF_ = mkarena()
        wdq_sb = AF_("wdq_sb", [128, 22 * C], BF16)
        hstgF = AF_("hstgF", [128, 2 * 2816], F32)
        Hq_bf = AF_("Hq_bf", [128, 2 * 2816], BF16)
        ptF = AF_("ptF", [128, 2 * C], F32)
        otF = AF_("otF", [128, 2 * C], F32)
        swd_bc = AF_("swd_bc", [128, C], F32)
        swrow = AF_("swrow", [1, C], F32)
        shE = AF_("shE", [1, 1], F32)

        tF_w, dF_w = p.dma_step("g", [
            (wdq_sb[:, ic * C:(ic + 1) * C], wdq_s[ic * 128:(ic + 1) * 128, :])
            for ic in range(11)], [pE_done["g"]], "wld", "F_w")
        tF_w2, dF_w2 = p.dma_step("s", [
            (wdq_sb[:, ic * C:(ic + 1) * C], wdq_s[ic * 128:(ic + 1) * 128, :])
            for ic in range(11, 22)], [pE_done["s"]], "wld", "F_w2")

        def f_sg(sc):
            sc.activation(shE[:, :], clipd_r[:, :], AF.Sigmoid, bias=0.0, scale=1.0)
        tF_sg = p.step("s", [pE_done["s"], dcc2b], f_sg, "F_sg")
        def f_sx(v):
            v.tensor_mul(shE[:, :], shE[:, :], gmax_h[:, :])
            v.drain()
            v.tensor_scalar_mul(shE[:, :], shE[:, :], 1.0 / QMAX)
        tF_sx = p.step("v", [tF_sg, dcc2b, pE_done["v"]], f_sx, "F_sx")
        def f_sxb(t):
            t.matmul(pf_b[:, 1024:1025], onesf_row[:, :], shE[:, :],
                     start=True, stop=True)
            for hc in range(8):
                t.matmul(pf_b[0:1, hc * 128:(hc + 1) * 128],
                         wsc_d[:, hc:hc + 1], identf[:, :], start=True, stop=True)
        tF_sxb = p.step("t", [tF_sx, pE_done["t"]], f_sxb, "F_sxb")
        def f_sxc(v):
            v.tensor_copy(sh_col[:, :], pf_b[:, 1024:1025])
            v.drain()
            v.reciprocal(rsh_col[:, :], sh_col[:, :])
            v.tensor_scalar(swrow[:, :], pf_b[0:1, 0:1024], sh_col[0:1, 0:1],
                            None, OP.mult)
        tF_sxc = p.step("v", [tF_sxb], f_sxc, "F_sxc")
        def f_swb(t):
            for hh in range(2):
                t.matmul(pf_a[:, hh * 512:(hh + 1) * 512], onesf_row[:, :],
                         swrow[:, hh * 512:(hh + 1) * 512], start=True, stop=True)
        tF_swb = p.step("t", [tF_sxc], f_swb, "F_swb")
        def f_swe(v):
            v.tensor_copy(swd_bc[:, :], pf_a[:, 0:1024])
        tF_swe = p.step("v", [tF_swb], f_swe, "F_swe")

        t_ydma = None
        t_inF = [None, None]
        prev_q = [None, None]
        prev_o = [None, None]
        prev_dy = [[], []]
        for tc in range(NT // 128):
            bb = tc % 2
            hst = hstgF[:, bb * 2816:(bb + 1) * 2816]
            hq = Hq_bf[:, bb * 2816:(bb + 1) * 2816]
            ptv = ptF[:, bb * C:(bb + 1) * C]
            otv = otF[:, bb * C:(bb + 1) * C]
            psl = pf_a[:, bb * 1024:(bb + 1) * 1024]
            in_deps = [pE_done["y"]]
            in_deps_s = [pE_done["s"], tF_w2]
            for tok in (t_inF[bb], prev_q[bb], prev_o[bb]):
                if tok is not None:
                    in_deps.append(tok)
                    in_deps_s.append(tok)
            t_in, d_in = p.dma_step("y", [
                (hst[:, ic * 128:(ic + 1) * 128],
                 h_s[ic * 128:(ic + 1) * 128, tc * 128:(tc + 1) * 128])
                for ic in range(11)] + [
                (ptv, pt_s[tc * 128:(tc + 1) * 128, :])],
                in_deps, "xin", f"F_in{tc}")
            t_in2, d_in2 = p.dma_step("s", [
                (hst[:, ic * 128:(ic + 1) * 128],
                 h_s[ic * 128:(ic + 1) * 128, tc * 128:(tc + 1) * 128])
                for ic in range(11, 22)],
                in_deps_s, "xin", f"F_in2_{tc}")
            t_inF[bb] = t_in
            def f_q(v, hst=hst, hq=hq):
                v.tensor_scalar(hst, hst, rsh_col[:, 0:1], MR, OP.mult, OP.add)
                v.drain()
                v.tensor_scalar(hst, hst, -MR, -(QMAX + 1.0), OP.add, OP.max)
                v.drain()
                v.tensor_scalar(hq, hst, QMAX, None, OP.min)
            t_q = p.step("v", [d_in, d_in2, tF_sxc, tF_swe], f_q, f"F_q{tc}")
            prev_q[bb] = t_q
            def f_mm(t, hq=hq, psl=psl):
                for hh in range(2):
                    for ic in range(22):
                        t.matmul(psl[:, hh * 512:(hh + 1) * 512],
                                 hq[:, ic * 128:(ic + 1) * 128],
                                 wdq_sb[:, ic * C + hh * 512:ic * C + (hh + 1) * 512],
                                 start=(ic == 0), stop=(ic == 21))
            deps_m = [t_q, dF_w, dF_w2, tF_swe]
            if prev_o[bb] is not None:
                deps_m.append(prev_o[bb])
            t_mm = p.step("t", deps_m, f_mm, f"F_mm{tc}")
            def f_out(v, otv=otv, ptv=ptv, psl=psl):
                v.tensor_mul(otv, psl, swd_bc[:, :])
                v.drain()
                v.tensor_add(otv, otv, ptv)
            t_o = p.step("v", [t_mm] + prev_dy[bb], f_out, f"F_o{tc}")
            prev_o[bb] = t_o
            ydeps = [t_o] + ([t_ydma] if t_ydma else [])
            t_ydma, d_yd = p.dma_step("g", [(y_out[tc * 128:(tc + 1) * 128, :],
                                             otv)], ydeps, "out", f"F_y{tc}")
            prev_dy[bb] = [d_yd]
    final = []
    for nm_ in ("out_g", "out_y", "dbg_g"):
        if nm_ in p.dlanes:
            waits = tuple((k, p.dsems[k][1]) for k in p.dlanes[nm_]["keys"]
                          if p.dsems[k][1] > 0)
            final.append(("g", DmaTok(waits)))
    p.finalize(final)
    return nc, dbg_outs



def shard_inputs(inputs, n_cores=8):
    x = np.ascontiguousarray(inputs["L0_attn_x"])
    Bfull, T, C = x.shape
    Bper = Bfull // n_cores
    asc = np.ascontiguousarray

    def r1(a):
        return asc(np.asarray(a, dtype=np.float32).reshape(1, -1))

    common = dict(
        wqT=asc(np.asarray(inputs["L0_attn_wq"]).T),
        wkT=asc(np.asarray(inputs["L0_attn_wk"]).T),
        wvT=asc(np.asarray(inputs["L0_attn_wv"]).T),
        woT=asc(np.asarray(inputs["L0_attn_wo"]).T),
        ln1w=r1(inputs["L0_attn_ln1_w"]), ln1b=r1(inputs["L0_attn_ln1_b"]),
        ln2w=r1(inputs["L0_attn_ln2_w"]), ln2b=r1(inputs["L0_attn_ln2_b"]),
        lr=r1(inputs["L0_attn_lr_scale"]),
        w1T=asc(np.asarray(inputs["L0_attn_w1"]).T), b1=r1(inputs["L0_attn_b1"]),
        w2T=asc(np.asarray(inputs["L0_attn_w2"]).T), b2=r1(inputs["L0_attn_b2"]),
        fWin=asc(np.asarray(inputs["L0_ffn_Win"])), fbin=r1(inputs["L0_ffn_bin_"]),
        fW1=r1(inputs["L0_ffn_W1"]), fb1=r1(inputs["L0_ffn_b1"]),
        fW2=r1(inputs["L0_ffn_W2"]), fb2=r1(inputs["L0_ffn_b2"]),
        fWout=asc(np.asarray(inputs["L0_ffn_Wout"])), fbout=r1(inputs["L0_ffn_bout"]),
        caT=asc(np.asarray(inputs["L1_attn_c_attn_w"]).T), cab=r1(inputs["L1_attn_c_attn_b"]),
        q2lT=asc(np.asarray(inputs["L1_attn_q2l_w"]).T), q2lb=r1(inputs["L1_attn_q2l_b"]),
        kv2lT=asc(np.asarray(inputs["L1_attn_kv2l_w"]).T), kv2lb=r1(inputs["L1_attn_kv2l_b"]),
        l2oT=asc(np.asarray(inputs["L1_attn_l2o_w"]).T), l2ob=r1(inputs["L1_attn_l2o_b"]),
        wqkT=asc(np.asarray(inputs["L1_attn_wqk_w"]).T), wqkb=r1(inputs["L1_attn_wqk_b"]),
        cpT=asc(np.asarray(inputs["L1_attn_cproj_w"]).T), cpb=r1(inputs["L1_attn_cproj_b"]),
        clipx=r1(inputs["L1_ffn_clip_x"]), clipd=r1(inputs["L1_ffn_clip_down"]),
        wg=asc(np.asarray(inputs["L1_ffn_w_gate"])),
        wu=asc(np.asarray(inputs["L1_ffn_w_up"])),
        wd=asc(np.asarray(inputs["L1_ffn_w_down"])),
    )
    in_maps = []
    for c in range(n_cores):
        m = dict(common)
        m["x"] = asc(x[c * Bper:(c + 1) * Bper].reshape(Bper * T, C))
        in_maps.append(m)
    return in_maps, (Bper, T, C)


def kernel(**inputs):
    import os
    n_cores = 8
    in_maps, (Bper, T, C) = shard_inputs(inputs, n_cores)
    cfg = dict(T=T, B=Bper, n_cores=n_cores, debug=False,
               phases=os.environ.get("KPHASES", "ABCDEF"),
               skip_cc=bool(os.environ.get("KSKIPCC")))
    nc, _ = build_program(cfg)
    res = run_bass_kernel_spmd(nc, in_maps, core_ids=list(range(n_cores)))
    outs = [r["y"].reshape(Bper, T, C) for r in res.results]
    return np.concatenate(outs, axis=0)
